# version 06
# baseline (speedup 1.0000x reference)
import contextlib
import math
import numpy as np
import concourse.bass as bass
import concourse.mybir as mybir
from concourse.bass_core_rust import IndirectOffsetOnAxis

F32 = mybir.dt.float32
BF16 = mybir.dt.bfloat16
U32 = mybir.dt.uint32
AF = mybir.ActivationFunctionType
ALU = mybir.AluOpType
AX = mybir.AxisListType
RG8 = [[0, 1, 2, 3, 4, 5, 6, 7]]

D = 2048
TOK = 2048
NT = TOK // 128
NSEQ = 2048 * 8
BIG = 1.0e30


class Prog:
    def __init__(self):
        self.nc = bass.Bass("TRN2", target_bir_lowering=False)
        self.stack = contextlib.ExitStack()
        self.stack.__enter__()
        self.host_consts = {}
        self.cache = {}
        self._cvals = {}
        self._cpending = []
        self._uid = 0
        self.dn = 0
        self.on = 0
        self.cn = 0
        self.dsem = self.sem("dsem_g")
        self.osem = self.sem("osem_g")
        self.csem = self.sem("csem_g")

    def uid(self):
        self._uid += 1
        return self._uid

    def _cur(self):
        return self._scopes[-1] if getattr(self, "_scopes", None) else self.stack

    def sbuf(self, name, shape, dtype):
        return self._cur().enter_context(self.nc.sbuf_tensor(f"{name}_{self.uid()}", shape, dtype))

    def psum(self, name, shape, dtype=F32):
        return self._cur().enter_context(self.nc.psum_tensor(f"{name}_{self.uid()}", shape, dtype))

    def sem(self, name):
        return self.stack.enter_context(self.nc.semaphore(f"{name}_{self.uid()}"))

    @contextlib.contextmanager
    def scope(self):
        if not hasattr(self, "_scopes"):
            self._scopes = []
        st = contextlib.ExitStack()
        self._scopes.append(st)
        try:
            with st:
                yield
        finally:
            self._scopes.pop()

    def dram(self, name, shape, dtype, kind="Internal"):
        return self.nc.dram_tensor(name, shape, dtype, kind=kind)

    def hconst(self, name, arr):
        if name in self.host_consts:
            return self._hconst_handles[name]
        a0 = arr[0] if isinstance(arr, list) else arr
        h = self.nc.dram_tensor(name, list(a0.shape), np_to_dt(a0.dtype), kind="ExternalInput")
        self.host_consts[name] = arr
        if not hasattr(self, "_hconst_handles"):
            self._hconst_handles = {}
        self._hconst_handles[name] = h
        return h

    def need_consts(self, vals):
        for v in vals:
            v = float(v)
            if v in (0.0, 1.0) or v in self._cvals:
                continue
            t = self.sbuf("constap", [128, 1], F32)
            self._cvals[v] = t
            self._cpending.append((t, v))
            self.nc.const_aps.register(F32, v, t[:])

    def emit_consts(self, g):
        for t, v in self._cpending:
            g.memset(t[:], v)
        self._cpending = []

    def dump(self, name, ap_):
        if not getattr(self, "dbg", False):
            return
        t = ap_.tensor if hasattr(ap_, "tensor") else ap_
        shape = list(ap_.shape) if hasattr(ap_, "shape") else None
        nc = self.nc
        pat = ap_.ap if hasattr(ap_, "ap") else None
        dims = [p[1] for p in pat] if pat is not None else shape
        o = nc.dram_tensor(f"dbg_{name}", dims, ap_.dtype, kind="ExternalOutput")
        with nc.Block() as b:
            @b.sync
            def _(sync):
                self.st(sync, o.ap(), ap_)
                self.stwait(sync)

    def ld(self, eng, dst, src):
        eng.dma_start(dst, src).then_inc(self.dsem, 16)
        self.dn += 1

    def ldwait(self, eng):
        eng.wait_ge(self.dsem, 16 * self.dn)

    def st(self, eng, dst, src):
        eng.dma_start(dst, src).then_inc(self.osem, 16)
        self.on += 1

    def stwait(self, eng):
        eng.wait_ge(self.osem, 16 * self.on)

    def wrap(self, eng, key):
        if not hasattr(self, "_ew"):
            self._ew = {}
        if key not in self._ew:
            self._ew[key] = EW(self, key)
        w = self._ew[key]
        w.bind(eng)
        return w


GUARD_DIST = 2


def _ap_name(a):
    if a is None or isinstance(a, (int, float)):
        return None
    t = getattr(a, "tensor", None)
    if t is None:
        t = a
    return getattr(t, "name", None)


class EW:
    OPS = {
        "tensor_tensor":        (lambda a: ([a[1], a[2]], [a[0]])),
        "tensor_scalar":        (lambda a: ([a[1], a[2], a[3]], [a[0]])),
        "tensor_single_scalar": (lambda a: ([a[1], a[2]], [a[0]])),
        "scalar_tensor_tensor": (lambda a: ([a[1], a[2], a[3]], [a[0]])),
        "tensor_tensor_scan":   (lambda a: ([a[1], a[2], a[3]], [a[0]])),
        "tensor_copy":          (lambda a: ([a[1]], [a[0]])),
        "tensor_reduce":        (lambda a: ([a[1]], [a[0]])),
        "reciprocal":           (lambda a: ([a[1]], [a[0]])),
        "max":                  (lambda a: ([a[1]], [a[0]])),
        "memset":               (lambda a: ([], [a[0]])),
        "iota":                 (lambda a: ([], [a[0]])),
        "transpose":            (lambda a: ([a[1]], [a[0]])),
    }

    def __init__(self, P, key):
        self.P = P
        self.key = key
        self.sem = P.sem(f"selfsem_{key}")
        self.cnt = 0
        self.idx = 0
        self.lastw = {}
        self.lastr = {}
        self.pend = {}
        self.inced = {}
        self.eng = None

    def bind(self, eng):
        self.eng = eng
        self.idx += GUARD_DIST + 1

    def _guard(self, reads, writes):
        hazard_idx = -10**9
        for n in reads:
            if n in self.lastw:
                hazard_idx = max(hazard_idx, self.lastw[n])
        for n in writes:
            if n in self.lastw:
                hazard_idx = max(hazard_idx, self.lastw[n])
            if n in self.lastr:
                hazard_idx = max(hazard_idx, self.lastr[n])
        if hazard_idx >= self.idx - GUARD_DIST and hazard_idx in self.pend:
            if hazard_idx in self.inced:
                self.eng.wait_ge(self.sem, self.inced[hazard_idx])
            else:
                self.cnt += 1
                self.pend[hazard_idx].then_inc(self.sem, 1)
                self.inced[hazard_idx] = self.cnt
                self.eng.wait_ge(self.sem, self.cnt)

    def _emit(self, name, args, kwargs):
        get = self.OPS[name]
        rina, wouta = get(list(args) + [None] * 8)
        reads = [x for x in (_ap_name(a) for a in rina) if x]
        writes = [x for x in (_ap_name(a) for a in wouta) if x]
        for k in ("bias", "scale", "accum_out", "identity"):
            if k in kwargs:
                n = _ap_name(kwargs[k])
                if n:
                    (writes if k == "accum_out" else reads).append(n)
        self._guard(reads, writes)
        ins = getattr(self.eng, name)(*args, **kwargs)
        self.pend[self.idx] = ins
        for n in writes:
            self.lastw[n] = self.idx
        for n in reads:
            self.lastr[n] = self.idx
        self.idx += 1
        return ins

    def activation(self, *args, **kwargs):
        a = list(args)
        reads = [_ap_name(a[1])]
        writes = [_ap_name(a[0])]
        for k, v in list(kwargs.items()) + list(zip(["bias", "scale"], a[3:5])):
            n = _ap_name(v)
            if n:
                (writes if k == "accum_out" else reads).append(n)
        reads = [x for x in reads if x]; writes = [x for x in writes if x]
        self._guard(reads, writes)
        ins = self.eng.activation(*args, **kwargs)
        self.pend[self.idx] = ins
        for n in writes: self.lastw[n] = self.idx
        for n in reads: self.lastr[n] = self.idx
        self.idx += 1
        return ins

    def wait_ge(self, *a):
        return self.eng.wait_ge(*a)

    def dma_start(self, *a, **k):
        return self.eng.dma_start(*a, **k)

    def __getattr__(self, name):
        if name in EW.OPS:
            return lambda *args, **kwargs: self._emit(name, args, kwargs)
        raise AttributeError(name)


def np_to_dt(dtype):
    import ml_dtypes
    dtype = np.dtype(dtype)
    if dtype == np.float32:
        return F32
    if dtype == ml_dtypes.bfloat16:
        return BF16
    if dtype == np.int32:
        return mybir.dt.int32
    if dtype == np.uint32:
        return U32
    if dtype == np.float16:
        return mybir.dt.float16
    raise ValueError(dtype)


def ap3(t, offset, dims):
    return bass.AP(t, offset, dims)



def v_argmax_onehot(v, oh_out, x_ap, n, scratch):
    m, dd, pfx, ones = scratch["m"], scratch["d"], scratch["pfx"], scratch["ones"]
    v.tensor_reduce(m[:], x_ap, AX.X, ALU.max)
    v.tensor_scalar(dd[:], x_ap, m[:], None, ALU.subtract)
    v.tensor_scalar(dd[:], dd[:], 0.0, None, ALU.is_equal)
    v.tensor_tensor_scan(pfx[:], ones[:], dd[:], 0.0, ALU.mult, ALU.add)
    v.tensor_scalar(pfx[:], pfx[:], 1.0, None, ALU.is_equal)
    v.tensor_tensor(oh_out, dd[:], pfx[:], ALU.mult)


def ln_stats(P, v_or_none, block, x_ap, scr_ap, stats, dinv, eps):
    nc = P.nc
    Dp = x_ap.shape[-1] if hasattr(x_ap, "shape") else None
    negmu, rstd, s1, ssq = stats["negmu"], stats["rstd"], stats["s1"], stats["ssq"]
    with nc.Block() as b:
        @b.vector
        def _(v):
            v = P.wrap(v, "v")
            v.tensor_reduce(s1[:], x_ap, AX.X, ALU.add)
            v.tensor_scalar(negmu[:], s1[:], -dinv, None, ALU.mult)
    with nc.Block() as b:
        @b.scalar
        def _(sc):
            sc = P.wrap(sc, "s")
            sc.activation(scr_ap, x_ap, AF.Square, bias=negmu[:], scale=1.0,
                          accum_out=ssq[:])
            sc.activation(ssq[:], ssq[:], AF.Ln, bias=(eps / dinv), scale=1.0)
            sc.activation(rstd[:], ssq[:], AF.Exp, bias=(0.5 * math.log(1.0 / dinv)),
                          scale=-0.5)



def host_identity(n, dtype):
    return np.eye(n, dtype=dtype)


def host_blockmask_l0():
    import ml_dtypes
    bm = np.zeros((128, 16 * 72), np.float32)
    for k in range(16):
        for r in range(128):
            d = 128 * k + r
            bm[r, 72 * k + d // 32] = 1.0
            bm[r, 72 * k + 64 + d // 256] = 1.0
    return bm.astype(ml_dtypes.bfloat16)



def build_l0ffn(P, x_in, y_out, wnp, tag):
    nc = P.nc
    import ml_dtypes
    id128b = P.hconst("c_id128b", host_identity(128, np.float32).astype(ml_dtypes.bfloat16))
    id128f = P.hconst("c_id128f", host_identity(128, np.float32))
    bmask = P.hconst("c_bmask_l0", host_blockmask_l0())

    P.need_consts([-1.0 / D, D * 1e-5, 0.5 * math.log(D), 0.5, -1.0, 100.0,
                   1e-8, 1.0 - 1e-6, 16.0, -BIG, BIG] + [float(g) for g in range(2, 16)])

    gb = P.sbuf("gb", [128, D], F32)
    bb = P.sbuf("bb", [128, D], F32)
    wsig = P.sbuf("wsig", [128, 16 * 72], BF16)
    tables = P.sbuf("tables", [64, 1024], F32)
    stat3 = P.sbuf("stat3", [64, 4], F32)
    vacc = P.sbuf("vacc", [128, 1], F32)
    big64 = P.sbuf("big64", [128, 64], F32)
    ones64 = P.sbuf("ones64", [128, 64], F32)
    slo = P.sbuf("slo", [64, 512], F32)
    ssum = P.sbuf("ssum", [64, 32], F32)
    ssq16 = P.sbuf("ssq16", [16, 128], F32)
    ssflat = P.sbuf("ssflat", [128, 16], F32)
    sb_id128b = P.sbuf("id128b", [128, 128], BF16)
    sb_id128f = P.sbuf("id128f", [128, 128], F32)
    sb_bmask = P.sbuf("bmask", [128, 16 * 72], BF16)
    oh16 = P.sbuf("oh16", [128, 64 * NT], F32)
    ohT16 = P.sbuf("ohT16", [64, 128 * NT], F32)
    xs16 = P.sbuf("xs16", [128, 32 * NT], F32)
    xt = P.sbuf("xt", [128, D], F32)
    zt = P.sbuf("zt", [128, D], F32)
    zn = P.sbuf("zn", [128, D], F32)
    scr = P.sbuf("scr", [128, D], F32)
    sg = P.sbuf("sg", [128, D], BF16)
    sgT = P.sbuf("sgT", [128, D], BF16)
    sc72 = P.sbuf("sc72", [128, 72], F32)
    ohc = P.sbuf("ohc", [128, 8], F32)
    inc64 = P.sbuf("inc64", [128, 64], F32)
    msc = P.sbuf("msc", [128, 64], F32)
    d64 = P.sbuf("d64", [128, 64], F32)
    pfx64 = P.sbuf("pfx64", [128, 64], F32)
    m1 = P.sbuf("m1", [128, 1], F32)
    Mcat = P.sbuf("Mcat", [128, 128], F32)
    stats = dict(negmu=P.sbuf("negmu", [128, 1], F32), rstd=P.sbuf("rstd", [128, 1], F32),
                 s1=P.sbuf("s1", [128, 1], F32), ssq=P.sbuf("ssq", [128, 1], F32))
    stats2 = dict(negmu=P.sbuf("negmu2", [128, 1], F32), rstd=P.sbuf("rstd2", [128, 1], F32),
                  s1=P.sbuf("s12", [128, 1], F32), ssq=P.sbuf("ssq2", [128, 1], F32))
    tmin = P.sbuf("tmin", [128, 1], F32)
    tmax = P.sbuf("tmax", [128, 1], F32)
    tmn = P.sbuf("tmn", [128, 1], F32)
    a64 = P.sbuf("a64", [128, 64], F32)
    b64 = P.sbuf("b64", [128, 64], F32)
    st3 = P.sbuf("st3", [128, 4], F32)
    mn_neg = P.sbuf("mn_neg", [128, 1], F32)
    dd1 = P.sbuf("dd1", [128, 1], F32)
    di1 = P.sbuf("di1", [128, 1], F32)
    uu = P.sbuf("uu", [128, 32], F32)
    geb = P.sbuf("geb", [128, 512], F32)
    ohgb = P.sbuf("ohgb", [128, 512], F32)
    bt512 = P.sbuf("bt512", [128, 512], F32)
    b32 = P.sbuf("b32", [128, 32], F32)
    s32 = P.sbuf("s32", [128, 32], F32)
    ger = P.sbuf("ger", [128, 32], F32)
    xl = P.sbuf("xl", [128, 32], F32)
    delta = P.sbuf("delta", [128, 32], F32)
    outacc = P.sbuf("outacc", [128, D], F32)

    psA = P.psum("psA", [128, 1024], BF16)
    psB1 = P.psum("psB1", [128, 512], F32)
    psB2 = P.psum("psB2", [128, 512], F32)
    psC = P.psum("psC", [128, 128], F32)
    psD = P.psum("psD", [64, 128], F32)
    psE = P.psum("psE", [128, 72], F32)
    psF = P.psum("psF", [128, 4], F32)

    z_dram = P.dram(f"z_dram_{tag}", [TOK, D], F32)
    cc_in = P.dram(f"ccin_{tag}", [128, 1], F32)
    cc_out = P.dram(f"ccout_{tag}", [128, 1], F32)

    x_t = x_in.ap().rearrange("(n p) d -> n p d", p=128)
    y_t = y_out.ap().rearrange("(n p) d -> n p d", p=128)
    z_t = z_dram.ap().rearrange("(n p) d -> n p d", p=128)

    slopes_flat = P.hconst(f"w_slopes_{tag}", wnp["slopes"].reshape(64, 512).astype(np.float32))
    bases_flat = P.hconst(f"w_bases_{tag}", wnp["bases"].reshape(64, 512).astype(np.float32))
    gb_h = P.hconst(f"w_gb_{tag}", np.ascontiguousarray(
        np.broadcast_to(wnp["ln_gamma"][None, :], (128, D))).astype(np.float32))
    bb_h = P.hconst(f"w_bb_{tag}", np.ascontiguousarray(
        np.broadcast_to(wnp["ln_beta"][None, :], (128, D))).astype(np.float32))
    osc_h = P.hconst(f"w_osc_{tag}", wnp["oscale"].reshape(64, 1).astype(np.float32))

    with nc.Block() as b:
        @b.sync
        def _(sync):
            P.ld(sync, slo[:], slopes_flat.ap())
            P.ld(sync, tables[:, 0:512], bases_flat.ap())
            P.ld(sync, tables[:, 512:1024], slopes_flat.ap())
            P.ld(sync, gb[:], gb_h.ap())
            P.ld(sync, bb[:], bb_h.ap())
            P.ld(sync, sb_id128b[:], id128b.ap())
            P.ld(sync, sb_id128f[:], id128f.ap())
            P.ld(sync, sb_bmask[:], bmask.ap())
            P.ldwait(sync)
        @b.gpsimd
        def _(g):
            P.emit_consts(g)
            g.memset(ones64[:], 1.0)
            g.memset(big64[:], BIG)
            g.memset(vacc[:], BIG)
            g.memset(stat3[:], 0.0)
    with nc.Block() as b:
        @b.vector
        def _(v):
            v = P.wrap(v, "v")
            v.tensor_reduce(ssum[:], ap3(slo, 0, [[512, 64], [16, 32], [1, 16]]), AX.X, ALU.add)
    with nc.Block() as b:
        @b.scalar
        def _(sc):
            sc = P.wrap(sc, "s")
            sc.activation(ssum[:], ssum[:], AF.Sign)
    ss_dram = P.dram(f"ssd_{tag}", [64, 32], F32)
    with nc.Block() as b:
        @b.sync
        def _(sync):
            P.ld(sync, ss_dram.ap(), ssum[:])
            P.ldwait(sync)
    with nc.Block() as b:
        @b.sync
        def _(sync):
            P.ld(sync, ssq16[:], ss_dram.ap().rearrange("a b -> (a b)").rearrange("(k r) -> k r", k=16))
            P.ldwait(sync)
    with nc.Block() as b:
        @b.tensor
        def _(t):
            t.transpose(psC[:, 0:16], ssq16[:], sb_id128f[0:16, 0:16])
    with nc.Block() as b:
        @b.vector
        def _(v):
            v = P.wrap(v, "v")
            v.tensor_copy(ssflat[:], psC[:, 0:16])
            for k in range(16):
                v.tensor_scalar(wsig[:, 72 * k:72 * k + 72], sb_bmask[:, 72 * k:72 * k + 72],
                                ssflat[:, k:k + 1], None, ALU.mult)

    for ti in range(NT):
        ohsl = oh16[:, 64 * ti:64 * ti + 64]
        with nc.Block() as b:
            @b.sync
            def _(sync):
                P.ld(sync, xt[:], x_t[ti])
                P.ldwait(sync)
        ln_stats(P, None, None, xt[:], scr[:], stats, 1.0 / D, 1e-5)
        with nc.Block() as b:
            @b.vector
            def _(v):
                v = P.wrap(v, "v")
                v.tensor_scalar(zt[:], xt[:], stats["negmu"][:], stats["rstd"][:], ALU.add, ALU.mult)
                v.tensor_reduce(stats2["s1"][:], zt[:], AX.X, ALU.add)
                v.tensor_scalar(stats2["negmu"][:], stats2["s1"][:], -1.0 / D, None, ALU.mult)
        with nc.Block() as b:
            @b.scalar
            def _(sc):
                sc = P.wrap(sc, "s")
                sc.activation(scr[:], zt[:], AF.Square, bias=stats2["negmu"][:], scale=1.0,
                              accum_out=stats2["ssq"][:])
                sc.activation(stats2["ssq"][:], stats2["ssq"][:], AF.Ln, bias=(D * 1e-5), scale=1.0)
                sc.activation(stats2["rstd"][:], stats2["ssq"][:], AF.Exp,
                              bias=(0.5 * math.log(D)), scale=-0.5)
            @b.sync
            def _(sync):
                P.ld(sync, z_t[ti], zt[:])
                P.ldwait(sync)
        with nc.Block() as b:
            @b.vector
            def _(v):
                v = P.wrap(v, "v")
                v.tensor_scalar(zn[:], zt[:], stats2["negmu"][:], stats2["rstd"][:], ALU.add, ALU.mult)
                v.tensor_tensor(zn[:], zn[:], gb[:], ALU.mult)
                v.tensor_tensor(zn[:], zn[:], bb[:], ALU.add)
        with nc.Block() as b:
            @b.scalar
            def _(sc):
                sc = P.wrap(sc, "s")
                sc.activation(sg[:], zn[:], AF.Sign)
        for half in range(2):
            with nc.Block() as b:
                @b.tensor
                def _(t):
                    for k in range(8):
                        kk = 8 * half + k
                        t.transpose(psA[:, 128 * k:128 * k + 128],
                                    sg[:, 128 * kk:128 * kk + 128], sb_id128b[:])
            with nc.Block() as b:
                @b.vector
                def _(v):
                    v = P.wrap(v, "v")
                    v.tensor_copy(sgT[:, 1024 * half:1024 * half + 1024], psA[:])
        with nc.Block() as b:
            @b.tensor
            def _(t):
                for k in range(16):
                    t.matmul(psE[:], sgT[:, 128 * k:128 * k + 128],
                             wsig[:, 72 * k:72 * k + 72], start=(k == 0), stop=(k == 15))
        with nc.Block() as b:
            @b.vector
            def _(v):
                v = P.wrap(v, "v")
                v.tensor_copy(sc72[:], psE[:])
                v.tensor_reduce(m1[:], sc72[:, 64:72], AX.X, ALU.max)
                v.tensor_scalar(d64[:, 0:8], sc72[:, 64:72], m1[:], None, ALU.subtract)
                v.tensor_scalar(d64[:, 0:8], d64[:, 0:8], 0.0, None, ALU.is_equal)
                v.tensor_tensor_scan(pfx64[:, 0:8], ones64[:, 0:8], d64[:, 0:8], 0.0,
                                     ALU.mult, ALU.add)
                v.tensor_scalar(pfx64[:, 0:8], pfx64[:, 0:8], 1.0, None, ALU.is_equal)
                v.tensor_tensor(ohc[:], d64[:, 0:8], pfx64[:, 0:8], ALU.mult)
                for c in range(8):
                    v.tensor_scalar(inc64[:, 8 * c:8 * c + 8], ones64[:, 0:8],
                                    ohc[:, c:c + 1], None, ALU.mult)
                v.scalar_tensor_tensor(msc[:], inc64[:], 100.0, sc72[:, 0:64],
                                       ALU.mult, ALU.add)
                v.tensor_reduce(m1[:], msc[:], AX.X, ALU.max)
                v.tensor_scalar(d64[:], msc[:], m1[:], None, ALU.subtract)
                v.tensor_scalar(d64[:], d64[:], 0.0, None, ALU.is_equal)
                v.tensor_tensor_scan(pfx64[:], ones64[:], d64[:], 0.0, ALU.mult, ALU.add)
                v.tensor_scalar(pfx64[:], pfx64[:], 1.0, None, ALU.is_equal)
                v.tensor_tensor(ohsl, d64[:], pfx64[:], ALU.mult)
        with nc.Block() as b:
            @b.vector
            def _(v):
                v = P.wrap(v, "v")
                xssl = xs16[:, 32 * ti:32 * ti + 32]
                v.tensor_scalar(xssl, zn[:, 0:32], ohsl[:, 0:1], None, ALU.mult)
                for t_ in range(1, 64):
                    v.scalar_tensor_tensor(xssl, zn[:, 32 * t_:32 * t_ + 32],
                                           ohsl[:, t_:t_ + 1], xssl, ALU.mult, ALU.add)
                v.tensor_reduce(tmin[:], xssl, AX.X, ALU.min)
                v.tensor_reduce(tmax[:], xssl, AX.X, ALU.max)
                v.tensor_scalar(tmn[:], tmax[:], -1.0, None, ALU.mult)
                v.tensor_scalar(a64[:], ohsl, tmin[:], None, ALU.mult)
                v.scalar_tensor_tensor(b64[:], ohsl, -BIG, big64[:], ALU.mult, ALU.add)
                v.tensor_tensor(Mcat[:, 0:64], a64[:], b64[:], ALU.add)
                v.tensor_scalar(a64[:], ohsl, tmn[:], None, ALU.mult)
                v.tensor_tensor(Mcat[:, 64:128], a64[:], b64[:], ALU.add)
        with nc.Block() as b:
            @b.tensor
            def _(t):
                t.transpose(psC[:], Mcat[:], sb_id128f[:])
                t.transpose(psD[:], ohsl, sb_id128f[:])
        with nc.Block() as b:
            @b.vector
            def _(v):
                v = P.wrap(v, "v")
                v.tensor_reduce(tmin[:], psC[:], AX.X, ALU.min)
                v.tensor_tensor(vacc[:], vacc[:], tmin[:], ALU.min)
                v.tensor_copy(ohT16[:, 128 * ti:128 * ti + 128], psD[:])
        if ti == 0:
            P.dump("zn0", zn[:]); P.dump("sg0", sg[:]); P.dump("sc72", sc72[:])
            P.dump("ohc", ohc[:]); P.dump("oh0", oh16[:, 0:64]); P.dump("xs0", xs16[:, 0:32])
            P.dump("vacc0", vacc[:]); P.dump("wsig", wsig[:]); P.dump("ssflat", ssflat[:])
            P.dump("zt0", zt[:])

    with nc.Block() as b:
        @b.sync
        def _(sync):
            P.ld(sync, cc_in.ap(), vacc[:])
            P.ldwait(sync)
    cs = P.sem("ccs")
    with nc.Block() as b:
        @b.gpsimd
        def _(g):
            g.collective_compute("AllReduce", ALU.min, replica_groups=RG8,
                                 ins=[cc_in.ap().opt()], outs=[cc_out.ap().opt()]).then_inc(cs)
            g.wait_ge(cs, 1)
    with nc.Block() as b:
        @b.sync
        def _(sync):
            P.ld(sync, stat3[:, 0:1], cc_out.ap()[0:64, 0:1])
            P.ld(sync, stat3[:, 1:2], cc_out.ap()[64:128, 0:1])
            P.ld(sync, stat3[:, 2:3], osc_h.ap())
            P.ldwait(sync)
    with nc.Block() as b:
        @b.vector
        def _(v):
            v = P.wrap(v, "v")
            v.tensor_scalar(stat3[:, 1:2], stat3[:, 1:2], -1.0, None, ALU.mult)
            v.tensor_scalar(stat3[:, 2:3], stat3[:, 2:3], 0.5, None, ALU.mult)

    for ti in range(NT):
        ohsl = oh16[:, 64 * ti:64 * ti + 64]
        ohTsl = ohT16[:, 128 * ti:128 * ti + 128]
        xssl = xs16[:, 32 * ti:32 * ti + 32]
        with nc.Block() as b:
            @b.sync
            def _(sync):
                P.ld(sync, xt[:], x_t[ti])
                P.ld(sync, zt[:], z_t[ti])
                P.ldwait(sync)
        with nc.Block() as b:
            @b.tensor
            def _(t):
                t.matmul(psF[:], ohTsl, stat3[:], start=True, stop=True)
                t.matmul(psB1[:], ohTsl, tables[:, 0:512], start=True, stop=True)
                t.matmul(psB2[:], ohTsl, tables[:, 512:1024], start=True, stop=True)
        with nc.Block() as b:
            @b.vector
            def _(v):
                v = P.wrap(v, "v")
                v.tensor_copy(st3[:], psF[:])
                v.tensor_scalar(mn_neg[:], st3[:, 0:1], -1.0, None, ALU.mult)
                v.tensor_tensor(dd1[:], st3[:, 1:2], st3[:, 0:1], ALU.subtract)
                v.tensor_scalar(dd1[:], dd1[:], 1e-8, None, ALU.add)
                v.reciprocal(di1[:], dd1[:])
                v.tensor_scalar(uu[:], xssl, mn_neg[:], di1[:], ALU.add, ALU.mult)
                v.tensor_scalar(uu[:], uu[:], 0.0, 1.0 - 1e-6, ALU.max, ALU.min)
                v.tensor_scalar(uu[:], uu[:], 16.0, None, ALU.mult)
                for g_ in range(16):
                    v.tensor_scalar(ap3(geb, g_, [[512, 128], [16, 32]]), uu[:],
                                    float(g_), None, ALU.is_ge)
                v.tensor_tensor(ap3(ohgb, 0, [[512, 128], [16, 32], [1, 15]]),
                                ap3(geb, 0, [[512, 128], [16, 32], [1, 15]]),
                                ap3(geb, 1, [[512, 128], [16, 32], [1, 15]]), ALU.subtract)
                v.tensor_copy(ap3(ohgb, 15, [[512, 128], [16, 32]]),
                              ap3(geb, 15, [[512, 128], [16, 32]]))
                v.tensor_reduce(ger[:], ap3(geb, 0, [[512, 128], [16, 32], [1, 16]]),
                                AX.X, ALU.add)
                v.tensor_tensor(xl[:], uu[:], ger[:], ALU.subtract)
                v.tensor_scalar(xl[:], xl[:], 1.0, None, ALU.add)
                v.tensor_tensor(bt512[:], psB1[:], ohgb[:], ALU.mult)
                v.tensor_reduce(b32[:], ap3(bt512, 0, [[512, 128], [16, 32], [1, 16]]),
                                AX.X, ALU.add)
                v.tensor_tensor(bt512[:], psB2[:], ohgb[:], ALU.mult)
                v.tensor_reduce(s32[:], ap3(bt512, 0, [[512, 128], [16, 32], [1, 16]]),
                                AX.X, ALU.add)
                v.tensor_tensor(delta[:], s32[:], xl[:], ALU.mult)
                v.tensor_tensor(delta[:], delta[:], b32[:], ALU.add)
                v.tensor_scalar(delta[:], delta[:], st3[:, 2:3], None, ALU.mult)
                for t_ in range(64):
                    v.tensor_scalar(outacc[:, 32 * t_:32 * t_ + 32], delta[:],
                                    ohsl[:, t_:t_ + 1], None, ALU.mult)
                v.scalar_tensor_tensor(scr[:], zt[:], 0.5, xt[:], ALU.mult, ALU.add)
                v.tensor_tensor(scr[:], scr[:], outacc[:], ALU.add)
        if ti == 0:
            P.dump("st3", st3[:]); P.dump("uu0", uu[:]); P.dump("geb0", geb[:])
            P.dump("ohgb0", ohgb[:]); P.dump("b32", b32[:]); P.dump("s32", s32[:])
            P.dump("xl", xl[:]); P.dump("delta", delta[:]); P.dump("outacc0", outacc[:, 0:256])
        with nc.Block() as b:
            @b.sync
            def _(sync):
                P.st(sync, y_t[ti], scr[:])
                P.stwait(sync)



def finish(P):
    with P.nc.Block() as b:
        @b.sync
        def _(sync):
            P.stwait(sync)


def in_maps_for(P, per_core_inputs):
    maps = []
    for c in range(8):
        m = dict(per_core_inputs[c])
        for name, arr in P.host_consts.items():
            m[name] = arr[c] if isinstance(arr, list) else arr
        maps.append(m)
    return maps


def run_sublayer(build_fn, x_full, wnp, tag):
    from concourse.bass_utils import run_bass_kernel_spmd
    P = Prog()
    x_in = P.dram("x", [TOK, D], F32, kind="ExternalInput")
    y_out = P.dram("y", [TOK, D], F32, kind="ExternalOutput")
    build_fn(P, x_in, y_out, wnp, tag)
    finish(P)
    shards = np.split(np.ascontiguousarray(x_full.reshape(8 * TOK, D)), 8, axis=0)
    maps = in_maps_for(P, [{"x": s} for s in shards])
    res = run_bass_kernel_spmd(P.nc, maps, core_ids=list(range(8)))
    return np.concatenate([res.results[c]["y"] for c in range(8)], axis=0)



def build_l0attn(P, x_in, y_out, wnp, tag):
    nc = P.nc
    import ml_dtypes
    H = 4096
    HS = 512
    TC = 512
    NCH = 8192 // TC
    id128b = P.hconst("c_id128b", host_identity(128, np.float32).astype(ml_dtypes.bfloat16))
    P.need_consts([-1.0 / D, D * 1e-5, 0.5 * math.log(D)])

    WinT = wnp["W_in"].T
    win_h = P.hconst(f"w_winT_{tag}",
                     [np.ascontiguousarray(WinT[:, 512 * c:512 * c + 512]).astype(np.float32)
                      for c in range(8)])
    bin_h = P.hconst(f"w_bin_{tag}",
                     [np.ascontiguousarray(wnp["b_in"][512 * c:512 * c + 512].reshape(4, 128).T)
                      for c in range(8)])
    beta_h = P.hconst(f"w_betasl_{tag}",
                      [np.ascontiguousarray(wnp["beta"][512 * c:512 * c + 512].reshape(4, 128).T)
                       for c in range(8)])
    gam_h = P.hconst(f"w_gamsl_{tag}",
                     [np.ascontiguousarray(wnp["gamma"][512 * c:512 * c + 512].reshape(4, 128).T)
                      for c in range(8)])
    gam8_h = P.hconst(f"w_gam8_{tag}",
                      [np.ascontiguousarray(wnp["gamma"][512 * c:512 * c + 512]
                                            .reshape(4, 128).T[:, [0, 0, 1, 1, 2, 2, 3, 3]])
                       for c in range(8)])
    woutT_h = P.hconst(f"w_woutT_{tag}", np.ascontiguousarray(wnp["W_out"].T).astype(np.float32))
    bout_h = P.hconst(f"w_bout_{tag}", np.ascontiguousarray(wnp["b_out"].reshape(16, 128).T))

    zT_shard = P.dram(f"zTs_{tag}", [D, TOK], BF16)
    zT_all = P.dram(f"zTa_{tag}", [8 * D, TOK], BF16)
    st_shard = P.dram(f"sts_{tag}", [HS, 2 * 8192], BF16)
    st_all = P.dram(f"sta_{tag}", [8 * HS, 2 * 8192], BF16)
    wout_bf = P.dram(f"woutbf_{tag}", [H, D], BF16)
    def _mkidx(c):
        h = np.arange(4096).reshape(32, 128).T
        out = np.zeros((128, 128), np.int32)
        for tg in range(4):
            out[:, 32 * tg:32 * tg + 32] = h * 32 + 4 * c + tg
        return np.ascontiguousarray(out)
    idx_h = P.hconst("c_hcidx", [_mkidx(c) for c in range(8)])

    sb_idb = P.sbuf("idb", [128, 128], BF16)
    xt = P.sbuf("xt", [128, D], F32)
    scr = P.sbuf("scr", [128, D], F32)
    zb = P.sbuf("zb", [128, D], BF16)
    zTt = P.sbuf("zTt", [128, D], BF16)
    stats = dict(negmu=P.sbuf("negmu", [128, 1], F32), rstd=P.sbuf("rstd", [128, 1], F32),
                 s1=P.sbuf("s1", [128, 1], F32), ssq=P.sbuf("ssq", [128, 1], F32))
    winT_sb = P.sbuf("winT", [128, 16 * 512], BF16)
    ubias = P.sbuf("ubias", [128, 4], F32)
    beta_sb = P.sbuf("betasb", [128, 4], F32)
    gam_sb = P.sbuf("gam", [128, 4], F32)
    gam8_sb = P.sbuf("gam8", [128, 8], F32)
    rhsz = P.sbuf("rhsz", [128, 32 * 512], BF16)
    u_sb = [P.sbuf("u0", [128, 4096], F32), P.sbuf("u1", [128, 4096], F32)]
    stt_sb = [P.sbuf("st0", [128, 4096], F32), P.sbuf("st1", [128, 4096], F32)]
    z4 = P.sbuf("z4", [128, 8], F32)
    z4m = P.sbuf("z4m", [128, 16], F32)
    s8 = P.sbuf("s8", [128, 16], F32)
    wcol = P.sbuf("wcol", [128, 32 * 128], BF16)
    states_tg = P.sbuf("stg", [128, 32 * 512], BF16)
    yT_sb = P.sbuf("yTsb", [128, 512], BF16)
    ytok = [P.sbuf(f"ytok{j}", [128, 2048], BF16) for j in range(4)]
    bout_sb = P.sbuf("bout", [128, 16], F32)
    wtmp = P.sbuf("wtmp", [128, 2048], BF16)
    idx32 = P.sbuf("idx32", [128, 128], mybir.dt.int32)


    x_t = x_in.ap().rearrange("(n p) d -> n p d", p=128)
    y_t = y_out.ap().rearrange("(n p) d -> n p d", p=128)

    esem = P.sem("evac")
    vsem = P.sem("vscan")
    ssem = P.sem("sscan")
    vsA = P.sem("vscanA")
    vsB = P.sem("vscanB")
    ssA = P.sem("sscanA")
    ssB = P.sem("sscanB")
    wsem = P.sem("stwb")

    with nc.Block() as b:
        @b.sync
        def _(sync):
            P.ld(sync, sb_idb[:], id128b.ap())
            P.ld(sync, ubias[:], bin_h.ap())
            P.ld(sync, beta_sb[:], beta_h.ap())
            P.ld(sync, gam_sb[:], gam_h.ap())
            P.ld(sync, gam8_sb[:], gam8_h.ap())
            P.ld(sync, bout_sb[:], bout_h.ap())
            P.ld(sync, idx32[:], idx_h.ap())
            P.ldwait(sync)
        @b.gpsimd
        def _(g):
            P.emit_consts(g)
            for k in range(16):
                g.dma_start(winT_sb[:, 512 * k:512 * k + 512],
                            win_h.ap()[128 * k:128 * k + 128, :]).then_inc(P.dsem, 16)
                P.dn += 1
            g.wait_ge(P.dsem, 16 * P.dn)
    with nc.Block() as b:
        @b.vector
        def _(v):
            v = P.wrap(v, "v")
            v.tensor_tensor(ubias[:], ubias[:], beta_sb[:], ALU.add)
        @b.gpsimd
        def _(g):
            for hc in range(32):
                g.dma_start(wtmp[:], woutT_h.ap()[128 * hc:128 * hc + 128, :]).then_inc(P.dsem, 16)
                P.dn += 1
                g.wait_ge(P.dsem, 16 * P.dn)
                g.dma_start(wout_bf.ap()[128 * hc:128 * hc + 128, :], wtmp[:]).then_inc(P.dsem, 16)
                P.dn += 1
                g.wait_ge(P.dsem, 16 * P.dn)

    ph1 = contextlib.ExitStack()
    psT = ph1.enter_context(nc.psum_tensor(f"psT_{P.uid()}", [128, 1024], BF16))
    for ti in range(NT):
        with nc.Block() as b:
            @b.sync
            def _(sync):
                P.ld(sync, xt[:], x_t[ti])
                P.ldwait(sync)
        ln_stats(P, None, None, xt[:], scr[:], stats, 1.0 / D, 1e-5)
        with nc.Block() as b:
            @b.vector
            def _(v):
                v = P.wrap(v, "v")
                v.tensor_scalar(zb[:], xt[:], stats["negmu"][:], stats["rstd"][:],
                                ALU.add, ALU.mult)
        for half in range(2):
            with nc.Block() as b:
                @b.tensor
                def _(t):
                    for k in range(8):
                        kk = 8 * half + k
                        t.transpose(psT[:, 128 * k:128 * k + 128],
                                    zb[:, 128 * kk:128 * kk + 128], sb_idb[:])
            with nc.Block() as b:
                @b.vector
                def _(v):
                    v = P.wrap(v, "v")
                    v.tensor_copy(zTt[:, 1024 * half:1024 * half + 1024], psT[:])
        with nc.Block() as b:
            @b.sync
            def _(sync):
                P.ld(sync, ap3(zT_shard, 128 * ti, [[TOK, 128], [128 * TOK, 16], [1, 128]]),
                     ap3(zTt, 0, [[2048, 128], [128, 16], [1, 128]]))
                P.ldwait(sync)
    cs1 = P.sem("agz")
    with nc.Block() as b:
        @b.gpsimd
        def _(g):
            g.collective_compute("AllGather", ALU.bypass, replica_groups=RG8,
                                 ins=[zT_shard.ap().opt()], outs=[zT_all.ap().opt()]).then_inc(cs1)
            g.wait_ge(cs1, 1)

    ph1.close()
    ph2 = contextlib.ExitStack()
    psU = [ph2.enter_context(nc.psum_tensor(f"psU{i}_{P.uid()}", [128, 512], F32))
           for i in range(8)]
    def rhs_slice(seq, c, k):
        g0 = seq * 8192 + 512 * c
        r = g0 // 2048
        tl = g0 - 2048 * r
        return zT_all.ap()[2048 * r + 128 * k: 2048 * r + 128 * k + 128, tl:tl + 512]

    for c in range(NCH + 2):
        with nc.Block() as b:
            @b.sync
            def _(sync):
                if c <= NCH - 1:
                    for seq in range(2):
                        for k in range(16):
                            P.ld(sync, rhsz[:, 512 * (16 * seq + k):512 * (16 * seq + k) + 512],
                                 rhs_slice(seq, c, k))
                    P.ldwait(sync)
            @b.tensor
            def _(t):
                if c <= NCH - 1:
                    t.wait_ge(P.dsem, 16 * P.dn)
                    if c > 0:
                        t.wait_ge(esem, 8 * c)
                    for k in range(16):
                        for hb in range(4):
                            for seq in range(2):
                                t.matmul(psU[2 * hb + seq][:],
                                         winT_sb[:, 512 * k + 128 * hb:512 * k + 128 * hb + 128],
                                         rhsz[:, 512 * (16 * seq + k):512 * (16 * seq + k) + 512],
                                         start=(k == 0), stop=(k == 15))
            @b.vector
            def _(v):
                if 1 <= c <= NCH:
                    cc = c - 1
                    ub = u_sb[cc % 2]
                    for hb in range(4):
                        for seq in range(2):
                            v.tensor_scalar(
                                ub[:, 1024 * hb + 512 * seq:1024 * hb + 512 * seq + 512],
                                psU[2 * hb + seq][:], ubias[:, hb:hb + 1], None,
                                ALU.add).then_inc(esem, 1)
                    def uch(tl, s):
                        return ap3(ub, tl + 512 * s, [[4096, 128], [1024, 4]])
                    def stch(j, s):
                        ccj, tlj = (j - 1) // TC, (j - 1) % TC
                        return ap3(stt_sb[ccj % 2], tlj + 512 * s,
                                   [[4096, 128], [1024, 4]])
                    gch = [ap3(gam8_sb, s, [[8, 128], [2, 4]]) for s in range(2)]
                    for tl in range(TC):
                        i = 512 * cc + tl + 1
                        if i == 1:
                            continue
                        v.wait_ge(ssA, i - 1)
                        v.tensor_tensor(z4m[:, 0:4], stch(i - 1, 0), gch[0], ALU.mult)
                        v.wait_ge(ssB, i - 1)
                        v.tensor_tensor(z4m[:, 4:8], stch(i - 1, 1), gch[1], ALU.mult)
                        v.tensor_tensor(z4[:, 0:4], z4m[:, 0:4], uch(tl, 0),
                                        ALU.add).then_inc(vsA, 1)
                        v.tensor_tensor(z4[:, 4:8], z4m[:, 4:8], uch(tl, 1),
                                        ALU.add).then_inc(vsB, 1)
            @b.scalar
            def _(sc):
                if 1 <= c <= NCH:
                    cc = c - 1
                    ub = u_sb[cc % 2]
                    def stch(j, s):
                        ccj, tlj = (j - 1) // TC, (j - 1) % TC
                        return ap3(stt_sb[ccj % 2], tlj + 512 * s,
                                   [[4096, 128], [1024, 4]])
                    if cc >= 2:
                        sc.wait_ge(wsem, 64 * (cc - 1))
                    for tl in range(TC):
                        i = 512 * cc + tl + 1
                        if i == 1:
                            sc.wait_ge(esem, 8)
                            sc.activation(stch(1, 0),
                                          ap3(ub, 0, [[4096, 128], [1024, 4]]),
                                          AF.Tanh).then_inc(ssA, 1)
                            sc.activation(stch(1, 1),
                                          ap3(ub, 512, [[4096, 128], [1024, 4]]),
                                          AF.Tanh).then_inc(ssB, 1)
                            continue
                        sc.wait_ge(vsA, i - 1)
                        sc.activation(stch(i, 0), z4[:, 0:4], AF.Tanh).then_inc(ssA, 1)
                        sc.wait_ge(vsB, i - 1)
                        sc.activation(stch(i, 1), z4[:, 4:8], AF.Tanh).then_inc(ssB, 1)
            @b.gpsimd
            def _(g):
                if c >= 2:
                    cc = c - 2
                    stb = stt_sb[cc % 2]
                    g.wait_ge(ssA, 512 * (cc + 1))
                    g.wait_ge(ssB, 512 * (cc + 1))
                    for hb in range(4):
                        g.dma_start(
                            ap3(st_shard, 128 * hb * 16384 + 512 * cc,
                                [[16384, 128], [8192, 2], [1, 512]]),
                            ap3(stb, 1024 * hb, [[4096, 128], [512, 2], [1, 512]])
                        ).then_inc(wsem, 16)
                    g.wait_ge(wsem, 64 * (cc + 1))
        if c == 1:
            P.dump("u0", u_sb[0][:])
            P.dump("stt0", stt_sb[0][:])

    cs2 = P.sem("agst")
    with nc.Block() as b:
        @b.gpsimd
        def _(g):
            g.collective_compute("AllGather", ALU.bypass, replica_groups=RG8,
                                 ins=[st_shard.ap().opt()], outs=[st_all.ap().opt()]).then_inc(cs2)
            g.wait_ge(cs2, 1)

    if getattr(P, "dbg", False):
        dbg_sb = P.sbuf("dbgsb", [128, 512], BF16)
        with nc.Block() as b:
            @b.sync
            def _(sync):
                P.ld(sync, dbg_sb[:], st_all.ap()[0:128, 0:512])
                P.ldwait(sync)
        P.dump("stall0", dbg_sb[:])
    ph2.close()
    psY = P.psum("psY", [128, 512], F32)
    psY2 = P.psum("psY2", [128, 512], BF16)
    for tg in range(4):
        with nc.Block() as b:
            @b.gpsimd
            def _(g):
                for hc in range(32):
                    g.indirect_dma_start(
                        states_tg[:, 512 * hc:512 * hc + 512], None,
                        ap3(st_all, 0, [[512, 131072], [1, 512]]),
                        IndirectOffsetOnAxis(idx32[:, 32 * tg + hc:32 * tg + hc + 1], 0)
                    ).then_inc(P.dsem, 16)
                    P.dn += 1
                g.wait_ge(P.dsem, 16 * P.dn)
        if tg == 0:
            P.dump("stg0", states_tg[:, 0:1024])
        for dc in range(16):
            with nc.Block() as b:
                @b.sync
                def _(sync):
                    P.ld(sync, ap3(wcol, 0, [[4096, 128], [128, 32], [1, 128]]),
                         ap3(wout_bf, 128 * dc, [[2048, 128], [128 * 2048, 32], [1, 128]]))
                    P.ldwait(sync)
                @b.tensor
                def _(t):
                    t.wait_ge(P.dsem, 16 * P.dn)
                    for hc in range(32):
                        t.matmul(psY[:], wcol[:, 128 * hc:128 * hc + 128],
                                 states_tg[:, 512 * hc:512 * hc + 512],
                                 start=(hc == 0), stop=(hc == 31))
            with nc.Block() as b:
                @b.vector
                def _(v):
                    v = P.wrap(v, "v")
                    v.tensor_scalar(yT_sb[:], psY[:], bout_sb[:, dc:dc + 1], None, ALU.add)
            with nc.Block() as b:
                @b.tensor
                def _(t):
                    for j in range(4):
                        t.transpose(psY2[:, 128 * j:128 * j + 128],
                                    yT_sb[:, 128 * j:128 * j + 128], sb_idb[:])
            with nc.Block() as b:
                @b.vector
                def _(v):
                    v = P.wrap(v, "v")
                    for j in range(4):
                        v.tensor_copy(ytok[j][:, 128 * dc:128 * dc + 128],
                                      psY2[:, 128 * j:128 * j + 128])
        if tg == 0:
            P.dump("ytok0", ytok[0][:])
        for j in range(4):
            ti = 4 * tg + j
            with nc.Block() as b:
                @b.sync
                def _(sync):
                    P.ld(sync, xt[:], x_t[ti])
                    P.ldwait(sync)
            with nc.Block() as b:
                @b.vector
                def _(v):
                    v = P.wrap(v, "v")
                    v.tensor_tensor(scr[:], xt[:], ytok[j][:], ALU.add)
            with nc.Block() as b:
                @b.sync
                def _(sync):
                    P.st(sync, y_t[ti], scr[:])
                    P.stwait(sync)



def _emit_dequant_tile(P, v, pk_i32, scf, scf_fs, wout_bf, wout_fs, scratch, ncols,
                       scf0=0, w0=0):
    W = ncols
    S = [tuple(scratch[k + sfx] for k in ("nib", "nf", "sbit", "m", "mag", "t1", "t2"))
         for sfx in ("", "2", "3", "4")]
    for gp in range(2):
        gg = (4 * gp, 4 * gp + 1, 4 * gp + 2, 4 * gp + 3)
        def both(fn):
            for j in range(4):
                nib, nf, sbit, m, mag, t1, t2 = S[j]
                fn(gg[j], nib[:, 0:W], nf[:, 0:W], sbit[:, 0:W], m[:, 0:W],
                   mag[:, 0:W], t1[:, 0:W], t2[:, 0:W])
        both(lambda g, nib, nf, sbit, m, mag, t1, t2:
             v.tensor_single_scalar(nib, pk_i32, 4 * g, ALU.logical_shift_right))
        both(lambda g, nib, nf, sbit, m, mag, t1, t2:
             v.tensor_single_scalar(nib, nib, 15, ALU.bitwise_and))
        both(lambda g, nib, nf, sbit, m, mag, t1, t2:
             v.tensor_copy(nf, nib))
        both(lambda g, nib, nf, sbit, m, mag, t1, t2:
             v.tensor_single_scalar(sbit, nf, 8.0, ALU.is_ge))
        both(lambda g, nib, nf, sbit, m, mag, t1, t2:
             v.scalar_tensor_tensor(m, sbit, -8.0, nf, ALU.mult, ALU.add))
        both(lambda g, nib, nf, sbit, m, mag, t1, t2:
             v.tensor_single_scalar(t1, m, 2.0, ALU.subtract))
        both(lambda g, nib, nf, sbit, m, mag, t1, t2:
             v.scalar_tensor_tensor(mag, m, 0.5, t1, ALU.mult, ALU.max))
        both(lambda g, nib, nf, sbit, m, mag, t1, t2:
             v.tensor_scalar(t2, m, 2.0, 8.0, ALU.mult, ALU.subtract))
        both(lambda g, nib, nf, sbit, m, mag, t1, t2:
             v.tensor_tensor(mag, mag, t2, ALU.max))
        both(lambda g, nib, nf, sbit, m, mag, t1, t2:
             v.tensor_scalar(t1, sbit, -2.0, 1.0, ALU.mult, ALU.add))
        both(lambda g, nib, nf, sbit, m, mag, t1, t2:
             v.tensor_tensor(t2, mag, t1, ALU.mult))
        both(lambda g, nib, nf, sbit, m, mag, t1, t2:
             v.tensor_tensor(ap3(wout_bf, w0 + g, [[wout_fs, 128], [8, W]]), t2,
                             ap3(scf, scf0 + g, [[scf_fs, 128], [8, W]]), ALU.mult))


def build_l1ffn(P, x_in, y_out, wnp, tag):
    nc = P.nc
    import ml_dtypes
    I = 5632
    id128b = P.hconst("c_id128b", host_identity(128, np.float32).astype(ml_dtypes.bfloat16))
    P.need_consts([-1.0 / D, D * 1e-5, 0.5 * math.log(D), 0.5, 2.0, 8.0, -8.0, -2.0,
                   D * 1e-6])

    dq = "l1w" not in P.cache
    if dq:
        gup = wnp["gate_up_packed"]
        dnp_ = wnp["down_packed"]
        gsc = wnp["gate_up_scales"].astype(np.float32)
        dsc = wnp["down_scales"].astype(np.float32)
        gup_h = P.hconst("w_gup_sl", [np.ascontiguousarray(gup[:, 32 * c:32 * c + 32])
                                      for c in range(8)])
        dnp_h = P.hconst("w_dnp_sl", [np.ascontiguousarray(dnp_[:, 88 * c:88 * c + 88])
                                      for c in range(8)])
        gsc_h = P.hconst("w_gsc_sl",
                         [np.ascontiguousarray(gsc.T[:, (256 * c + np.arange(256)) // 128])
                          for c in range(8)])
        dsc_h = P.hconst("w_dsc_sl",
                         [np.ascontiguousarray(dsc.T[:, (704 * c + np.arange(704)) // 128])
                          for c in range(8)])
        w_guT = P.dram("l1_wguT", [D, 11264], BF16)
        w_dnT = P.dram("l1_wdnT", [I, D], BF16)
        guT_sh = P.dram("l1_guTsh", [256, 11264], BF16)
        dnT_sh = P.dram("l1_dnTsh", [704, D], BF16)
        P.cache["l1w"] = (w_guT, w_dnT)
    else:
        w_guT, w_dnT = P.cache["l1w"]

    x_t = x_in.ap().rearrange("(n p) d -> n p d", p=128)
    y_t = y_out.ap().rearrange("(n p) d -> n p d", p=128)

    with P.scope():
        sb_idb = P.sbuf("idb", [128, 128], BF16)
        with nc.Block() as b:
            @b.sync
            def _(sync):
                P.ld(sync, sb_idb[:], id128b.ap())
                P.ldwait(sync)
            @b.gpsimd
            def _(g):
                P.emit_consts(g)

        if dq:
            phDQ = contextlib.ExitStack()
            P._scopes.append(phDQ)
            scratch = {}
            for sfx in ("", "2", "3", "4"):
                for k in ("nf", "sbit", "m", "mag", "t1", "t2"):
                    scratch[k + sfx] = P.sbuf(k + sfx, [128, 88], F32)
                scratch["nib" + sfx] = P.sbuf("nib" + sfx, [128, 88], mybir.dt.int32)
            pk = P.sbuf("pk", [128, 2 * 88], mybir.dt.int32)
            scf = P.sbuf("scf", [128, 2 * 704], F32)
            wde = P.sbuf("wde", [128, 2 * 704], BF16)
            wtr = P.sbuf("wtr", [128, 2 * 768], BF16)
            phW = contextlib.ExitStack()
            psW = [phW.enter_context(nc.psum_tensor(f"psW{i}_{P.uid()}", [128, 1024], BF16))
                   for i in range(2)]
            ngu = getattr(P, "ngu_tiles", 88)
            ndn = getattr(P, "ndn_tiles", 16)
            dqcnt = {}

            def tile_loop(ntiles, W, ld_tile, tr_tile, cp_tile, st_tile):
                with nc.Block() as b:
                    @b.sync
                    def _(sync):
                        ld_tile(sync, None, 0)
                    @b.gpsimd
                    def _(g):
                        ld_tile(None, g, 0)
                for i in range(ntiles + 3):
                    with nc.Block() as b:
                        @b.sync
                        def _(sync):
                            if i + 1 <= ntiles - 1:
                                ld_tile(sync, None, i + 1)
                            if 3 <= i:
                                st_tile(sync, i - 3)
                        @b.gpsimd
                        def _(g):
                            if i + 1 <= ntiles - 1:
                                ld_tile(None, g, i + 1)
                        @b.vector
                        def _(v):
                            v = P.wrap(v, "v")
                            if i <= ntiles - 1:
                                v.wait_ge(P.dsem, 16 * dqcnt[i])
                                pp = i % 2
                                _emit_dequant_tile(P, v, pk[:, 88 * pp:88 * pp + W],
                                                   scf, 1408, wde, 1408, scratch, W,
                                                   scf0=704 * pp, w0=704 * pp)
                            if 2 <= i:
                                cp_tile(v, (i - 2) % 2)
                        @b.tensor
                        def _(t):
                            if 1 <= i <= ntiles:
                                tr_tile(t, (i - 1) % 2)

            def ld_gu(sync, g, i):
                pp = i % 2
                if sync is not None:
                    P.ld(sync, pk[:, 88 * pp:88 * pp + 32], gup_h.ap()[128 * i:128 * i + 128, :])
                if g is not None:
                    g.dma_start(scf[:, 704 * pp:704 * pp + 256],
                                gsc_h.ap()[128 * i:128 * i + 128, :]).then_inc(P.dsem, 16)
                    P.dn += 1
                    dqcnt[i] = P.dn

            def tr_gu(t, pp):
                t.transpose(psW[pp][:, 0:128], wde[:, 704 * pp:704 * pp + 128], sb_idb[:])
                t.transpose(psW[pp][:, 128:256], wde[:, 704 * pp + 128:704 * pp + 256], sb_idb[:])

            def cp_gu(v, pp):
                v.tensor_copy(wtr[:, 768 * pp:768 * pp + 256], psW[pp][:, 0:256])

            def st_gu(sync, i):
                pp = i % 2
                P.ld(sync, ap3(guT_sh, 128 * i, [[11264, 128], [128 * 11264, 2], [1, 128]]),
                     ap3(wtr, 768 * pp, [[1536, 128], [128, 2], [1, 128]]))

            tile_loop(ngu, 32, ld_gu, tr_gu, cp_gu, st_gu)
            if getattr(P, "stop_after", None) == "gu_tiles":
                P.dump("wde", wde[:])
                P.dump("wtr", wtr[:])
                return

            def ld_dn2(sync, g, i):
                pp = i % 2
                if sync is not None:
                    P.ld(sync, pk[:, 88 * pp:88 * pp + 88], dnp_h.ap()[128 * i:128 * i + 128, :])
                if g is not None:
                    g.dma_start(scf[:, 704 * pp:704 * pp + 704],
                                dsc_h.ap()[128 * i:128 * i + 128, :]).then_inc(P.dsem, 16)
                    P.dn += 1
                    dqcnt[i] = P.dn

            def tr_dn(t, pp):
                for blk in range(5):
                    t.transpose(psW[pp][:, 128 * blk:128 * blk + 128],
                                wde[:, 704 * pp + 128 * blk:704 * pp + 128 * blk + 128],
                                sb_idb[:])
                t.transpose(psW[pp][0:64, 640:768], wde[:, 704 * pp + 640:704 * pp + 704],
                            sb_idb[:])

            def cp_dn(v, pp):
                v.tensor_copy(wtr[:, 768 * pp:768 * pp + 640], psW[pp][:, 0:640])
                v.tensor_copy(wtr[0:64, 768 * pp + 640:768 * pp + 768],
                              psW[pp][0:64, 640:768])

            def st_dn(sync, i):
                pp = i % 2
                P.ld(sync, ap3(dnT_sh, 128 * i, [[2048, 128], [128 * 2048, 5], [1, 128]]),
                     ap3(wtr, 768 * pp, [[1536, 128], [128, 5], [1, 128]]))
                P.ld(sync, ap3(dnT_sh, 5 * 128 * 2048 + 128 * i, [[2048, 64], [1, 128]]),
                     ap3(wtr, 768 * pp + 640, [[1536, 64], [1, 128]]))

            tile_loop(ndn, 88, ld_dn2, tr_dn, cp_dn, st_dn)
            if getattr(P, "stop_after", None) == "dn_tiles":
                P.dump("wde", wde[:])
                return
            cs1 = P.sem("agw1")
            cs1b = P.sem("agw1b")
            with nc.Block() as b:
                @b.gpsimd
                def _(g):
                    g.collective_compute("AllGather", ALU.bypass, replica_groups=RG8,
                                         ins=[guT_sh.ap().opt()],
                                         outs=[w_guT.ap().opt()]).then_inc(cs1)
                    g.wait_ge(cs1, 1)
                    g.collective_compute("AllGather", ALU.bypass, replica_groups=RG8,
                                         ins=[dnT_sh.ap().opt()],
                                         outs=[w_dnT.ap().opt()]).then_inc(cs1b)
                    g.wait_ge(cs1b, 1)
            phW.close()
            P._scopes.pop()
            phDQ.close()

        if getattr(P, "stop_after", None) == "dequant":
            og = nc.dram_tensor("dbg_wgu", [256, 2048], F32, kind="ExternalOutput")
            od = nc.dram_tensor("dbg_wdn", [256, 2048], F32, kind="ExternalOutput")
            tmpb = P.sbuf("tmpb", [128, 2048], BF16)
            tmpf = P.sbuf("tmpf", [128, 2048], F32)
            for rr in range(2):
                with nc.Block() as b:
                    @b.sync
                    def _(sync):
                        P.ld(sync, tmpb[:], w_guT.ap()[128 * rr:128 * rr + 128, 0:2048])
                        P.ldwait(sync)
                with nc.Block() as b:
                    @b.vector
                    def _(v):
                        v = P.wrap(v, "v")
                        v.tensor_copy(tmpf[:], tmpb[:])
                with nc.Block() as b:
                    @b.sync
                    def _(sync):
                        P.st(sync, og.ap()[128 * rr:128 * rr + 128, :], tmpf[:])
                        P.stwait(sync)
            for rr in range(2):
                with nc.Block() as b:
                    @b.sync
                    def _(sync):
                        P.ld(sync, tmpb[:], w_dnT.ap()[128 * rr:128 * rr + 128, 0:2048])
                        P.ldwait(sync)
                with nc.Block() as b:
                    @b.vector
                    def _(v):
                        v = P.wrap(v, "v")
                        v.tensor_copy(tmpf[:], tmpb[:])
                with nc.Block() as b:
                    @b.sync
                    def _(sync):
                        P.st(sync, od.ap()[128 * rr:128 * rr + 128, :], tmpf[:])
                        P.stwait(sync)
            return
        xt = P.sbuf("xt", [128, D], F32)
        scr = P.sbuf("scr", [128, D], F32)
        zb = P.sbuf("zb", [128, D], BF16)
        stats = dict(negmu=P.sbuf("negmu", [128, 1], F32), rstd=P.sbuf("rstd", [128, 1], F32),
                     s1=P.sbuf("s1", [128, 1], F32), ssq=P.sbuf("ssq", [128, 1], F32))
        zT_half = P.sbuf("zTh", [128, 16 * 1024], BF16)
        hT_half = P.sbuf("hTh", [128, 44 * 1024], BF16)
        y0T = zT_half
        wg_sb = P.sbuf("wg", [128, 2 * 16 * 128], BF16)
        wu_sb = P.sbuf("wu", [128, 2 * 16 * 128], BF16)
        wd_sb = P.sbuf("wd", [128, 2 * 44 * 128], BF16)
        sg0 = P.sbuf("sg0", [128, 512], F32)
        sg1 = P.sbuf("sg1", [128, 512], F32)
        y0tok = P.sbuf("y0tok", [128, D], BF16)
        rmsi = P.sbuf("rmsi", [128, 1], F32)
        sil = P.sem(f"sil_{tag}")
        nsil = [0]
        for half in range(2):
            phA = contextlib.ExitStack()
            psT2 = phA.enter_context(nc.psum_tensor(f"psT2a_{P.uid()}", [128, 1024], BF16))
            for j in range(8):
                ti = 8 * half + j
                with nc.Block() as b:
                    @b.sync
                    def _(sync):
                        P.ld(sync, xt[:], x_t[ti])
                        P.ldwait(sync)
                ln_stats(P, None, None, xt[:], scr[:], stats, 1.0 / D, 1e-5)
                with nc.Block() as b:
                    @b.vector
                    def _(v):
                        v = P.wrap(v, "v")
                        v.tensor_scalar(zb[:], xt[:], stats["negmu"][:], stats["rstd"][:],
                                        ALU.add, ALU.mult)
                for hf in range(2):
                    with nc.Block() as b:
                        @b.tensor
                        def _(t):
                            for k in range(8):
                                kk = 8 * hf + k
                                t.transpose(psT2[:, 128 * k:128 * k + 128],
                                            zb[:, 128 * kk:128 * kk + 128], sb_idb[:])
                    with nc.Block() as b:
                        @b.vector
                        def _(v):
                            v = P.wrap(v, "v")
                            for k in range(8):
                                kk = 8 * hf + k
                                v.tensor_copy(zT_half[:, 1024 * kk + 128 * j:1024 * kk + 128 * j + 128],
                                              psT2[:, 128 * k:128 * k + 128])
            phA.close()
            phB = contextlib.ExitStack()
            psG = [phB.enter_context(nc.psum_tensor(f"psG{i}_{P.uid()}", [128, 512], F32))
                   for i in range(4)]
            psU2 = [phB.enter_context(nc.psum_tensor(f"psU2{i}_{P.uid()}", [128, 512], F32))
                    for i in range(4)]
            ldcnt = {}

            def ld_gu(sync, hc):
                pp = hc % 2
                P.ld(sync, ap3(wg_sb, 2048 * pp, [[4096, 128], [128, 16], [1, 128]]),
                     ap3(w_guT, 128 * hc, [[11264, 128], [128 * 11264, 16], [1, 128]]))
                P.ld(sync, ap3(wu_sb, 2048 * pp, [[4096, 128], [128, 16], [1, 128]]),
                     ap3(w_guT, I + 128 * hc, [[11264, 128], [128 * 11264, 16], [1, 128]]))
                ldcnt[hc] = P.dn
            with nc.Block() as b:
                @b.sync
                def _(sync):
                    ld_gu(sync, 0)
            for hc in range(45):
                with nc.Block() as b:
                    @b.sync
                    def _(sync):
                        if hc + 1 <= 43:
                            ld_gu(sync, hc + 1)
                    @b.tensor
                    def _(t):
                        if hc <= 43:
                            pp = hc % 2
                            t.wait_ge(P.dsem, 16 * ldcnt[hc])
                            for k in range(16):
                                for tg in range(2):
                                    t.matmul(psG[2 * pp + tg][:],
                                             wg_sb[:, 2048 * pp + 128 * k:2048 * pp + 128 * k + 128],
                                             zT_half[:, 1024 * k + 512 * tg:1024 * k + 512 * tg + 512],
                                             start=(k == 0), stop=(k == 15))
                                for tg in range(2):
                                    t.matmul(psU2[2 * pp + tg][:],
                                             wu_sb[:, 2048 * pp + 128 * k:2048 * pp + 128 * k + 128],
                                             zT_half[:, 1024 * k + 512 * tg:1024 * k + 512 * tg + 512],
                                             start=(k == 0), stop=(k == 15))
                    @b.scalar
                    def _(sc):
                        if hc >= 1:
                            pq = (hc - 1) % 2
                            sc.activation(sg0[:], psG[2 * pq][:], AF.Silu)
                            sc.activation(sg1[:], psG[2 * pq + 1][:], AF.Silu).then_inc(sil, 1)
                            nsil[0] += 1
                    @b.vector
                    def _(v):
                        if hc >= 1:
                            pq = (hc - 1) % 2
                            v.wait_ge(sil, nsil[0])
                            v.tensor_tensor(hT_half[:, 1024 * (hc - 1):1024 * (hc - 1) + 512],
                                            sg0[:], psU2[2 * pq][:], ALU.mult)
                            v.tensor_tensor(hT_half[:, 1024 * (hc - 1) + 512:1024 * (hc - 1) + 1024],
                                            sg1[:], psU2[2 * pq + 1][:], ALU.mult)
            phB.close()
            phC = contextlib.ExitStack()
            psY = [phC.enter_context(nc.psum_tensor(f"psYl1{i}_{P.uid()}", [128, 512], F32))
                   for i in range(4)]
            ldcnt2 = {}

            def ld_dn(sync, dcs):
                pp = dcs % 2
                P.ld(sync, ap3(wd_sb, 5632 * pp, [[11264, 128], [128, 44], [1, 128]]),
                     ap3(w_dnT, 128 * dcs, [[2048, 128], [128 * 2048, 44], [1, 128]]))
                ldcnt2[dcs] = P.dn
            with nc.Block() as b:
                @b.sync
                def _(sync):
                    ld_dn(sync, 0)
            for dcs in range(17):
                with nc.Block() as b:
                    @b.sync
                    def _(sync):
                        if dcs + 1 <= 15:
                            ld_dn(sync, dcs + 1)
                    @b.tensor
                    def _(t):
                        if dcs <= 15:
                            pp = dcs % 2
                            t.wait_ge(P.dsem, 16 * ldcnt2[dcs])
                            for ic in range(44):
                                for tg in range(2):
                                    t.matmul(psY[2 * pp + tg][:],
                                             wd_sb[:, 5632 * pp + 128 * ic:5632 * pp + 128 * ic + 128],
                                             hT_half[:, 1024 * ic + 512 * tg:1024 * ic + 512 * tg + 512],
                                             start=(ic == 0), stop=(ic == 43))
                    @b.vector
                    def _(v):
                        if dcs >= 1:
                            pq = (dcs - 1) % 2
                            dc0 = dcs - 1
                            v.tensor_copy(y0T[:, 512 * 2 * dc0:512 * 2 * dc0 + 512],
                                          psY[2 * pq][:])
                            v.tensor_copy(y0T[:, 512 * 2 * dc0 + 512:512 * 2 * dc0 + 1024],
                                          psY[2 * pq + 1][:])
            phC.close()
            phD = contextlib.ExitStack()
            psT2 = phD.enter_context(nc.psum_tensor(f"psT2d_{P.uid()}", [128, 1024], BF16))
            for j in range(8):
                ti = 8 * half + j
                for hf in range(2):
                    with nc.Block() as b:
                        @b.tensor
                        def _(t):
                            for k in range(8):
                                dc = 8 * hf + k
                                t.transpose(psT2[:, 128 * k:128 * k + 128],
                                            y0T[:, 1024 * dc + 128 * j:1024 * dc + 128 * j + 128],
                                            sb_idb[:])
                    with nc.Block() as b:
                        @b.vector
                        def _(v):
                            v = P.wrap(v, "v")
                            for k in range(8):
                                dc = 8 * hf + k
                                v.tensor_copy(y0tok[:, 128 * dc:128 * dc + 128],
                                              psT2[:, 128 * k:128 * k + 128])
                with nc.Block() as b:
                    @b.scalar
                    def _(sc):
                        sc = P.wrap(sc, "s")
                        sc.activation(scr[:], y0tok[:], AF.Square, bias=0.0, scale=1.0,
                                      accum_out=rmsi[:])
                        sc.activation(rmsi[:], rmsi[:], AF.Ln, bias=(D * 1e-6), scale=1.0)
                        sc.activation(rmsi[:], rmsi[:], AF.Exp, bias=(0.5 * math.log(D)),
                                      scale=-0.5)
                    @b.sync
                    def _(sync):
                        P.ld(sync, xt[:], x_t[ti])
                        P.ldwait(sync)
                with nc.Block() as b:
                    @b.vector
                    def _(v):
                        v = P.wrap(v, "v")
                        v.tensor_single_scalar(rmsi[:], rmsi[:], 0.5, ALU.mult)
                        v.tensor_scalar(scr[:], y0tok[:], rmsi[:], None, ALU.mult)
                        v.tensor_tensor(scr[:], scr[:], xt[:], ALU.add)
                with nc.Block() as b:
                    @b.sync
                    def _(sync):
                        P.st(sync, y_t[ti], scr[:])
                        P.stwait(sync)
            phD.close()



RG2x4 = [[0, 1, 2, 3], [4, 5, 6, 7]]


def build_l1attn(P, x_in, y_out, wnp, tag):
    nc = P.nc
    import ml_dtypes
    id128b = P.hconst("c_id128b", host_identity(128, np.float32).astype(ml_dtypes.bfloat16))
    SCL = 1.0 / math.sqrt(128.0)
    P.need_consts([-1.0 / D, D * 1e-5, 0.5 * math.log(D), SCL, -1.0])

    wqkv_h = P.hconst(f"w_qkv_{tag}", wnp["w_qkv"].astype(np.float32))
    bq_h = P.hconst(f"w_bqkv_{tag}",
                    np.ascontiguousarray(wnp["b_qkv"].reshape(48, 128).T))
    vbias_h = P.hconst(f"w_vbias_{tag}", np.ascontiguousarray(
        np.broadcast_to(wnp["b_qkv"][4096:6144][None, :], (128, 2048))).astype(np.float32))
    wout_h = P.hconst(f"w_wo_{tag}", wnp["w_out"].astype(np.float32))
    bout_h = P.hconst(f"w_bo_{tag}", np.ascontiguousarray(wnp["b_out"].reshape(16, 128).T))

    K_sh = P.dram(f"Ksh_{tag}", [D, TOK], BF16)
    V_sh = P.dram(f"Vsh_{tag}", [TOK, D], BF16)
    K_seq = P.dram(f"Kseq_{tag}", [4 * D, TOK], BF16)
    V_seq = P.dram(f"Vseq_{tag}", [4 * TOK, D], BF16)
    km_in = P.dram(f"kmin_{tag}", [1, 16], F32)
    km_out = P.dram(f"kmout_{tag}", [1, 16], F32)
    wqkv_bf = P.dram(f"wqkvbf_{tag}", [D, 6144], BF16)
    wo_bf = P.dram(f"wobf_{tag}", [D, D], BF16)

    x_t = x_in.ap().rearrange("(n p) d -> n p d", p=128)
    y_t = y_out.ap().rearrange("(n p) d -> n p d", p=128)

    with P.scope():
        sb_idb = P.sbuf("idb", [128, 128], BF16)
        xt = P.sbuf("xt", [128, D], F32)
        scr = P.sbuf("scr", [128, D], F32)
        zb = P.sbuf("zb", [128, D], BF16)
        stats = dict(negmu=P.sbuf("negmu", [128, 1], F32), rstd=P.sbuf("rstd", [128, 1], F32),
                     s1=P.sbuf("s1", [128, 1], F32), ssq=P.sbuf("ssq", [128, 1], F32))
        qT = P.sbuf("qT", [128, 16 * 2048], BF16)
        wtile = P.sbuf("wtile", [128, 2 * 2048], BF16)
        bq_sb = P.sbuf("bq", [128, 48], F32)
        vb_sb = P.sbuf("vb", [128, 2048], F32)
        bo_sb = P.sbuf("bo", [128, 16], F32)
        ksh_t = P.sbuf("ksht", [128, 512], BF16)
        kn = P.sbuf("kn", [128, 1], F32)
        knacc = P.sbuf("knacc", [128, 16], F32)
        km16 = P.sbuf("km16", [1, 16], F32)
        qm16 = P.sbuf("qm16", [1, 16], F32)
        negM = P.sbuf("negM", [128, 16], F32)
        ones1f = P.sbuf("ones1f", [1, 128], F32)
        onescol = P.sbuf("onescol", [128, 1], BF16)
        sq2 = P.sbuf("sq2", [128, 2048], F32)
        m1 = P.sbuf("m1", [128, 1], F32)

        with nc.Block() as b:
            @b.sync
            def _(sync):
                P.ld(sync, sb_idb[:], id128b.ap())
                P.ld(sync, bq_sb[:], bq_h.ap())
                P.ld(sync, vb_sb[:], vbias_h.ap())
                P.ld(sync, bo_sb[:], bout_h.ap())
                P.ldwait(sync)
            @b.gpsimd
            def _(g):
                P.emit_consts(g)
                g.memset(ones1f[:], 1.0)
                g.memset(onescol[:], 1.0)
                g.memset(knacc[:], 0.0)
                for r in range(16):
                    g.dma_start(wtile[:, 0:2048],
                                wqkv_h.ap()[128 * r:128 * r + 128, 0:2048]).then_inc(P.dsem, 16)
                    g.dma_start(wtile[:, 2048:4096],
                                wqkv_h.ap()[128 * r:128 * r + 128, 2048:4096]).then_inc(P.dsem, 16)
                    P.dn += 2
                    g.wait_ge(P.dsem, 16 * P.dn)
                    g.dma_start(wqkv_bf.ap()[128 * r:128 * r + 128, 0:2048],
                                wtile[:, 0:2048]).then_inc(P.dsem, 16)
                    g.dma_start(wqkv_bf.ap()[128 * r:128 * r + 128, 2048:4096],
                                wtile[:, 2048:4096]).then_inc(P.dsem, 16)
                    P.dn += 2
                    g.wait_ge(P.dsem, 16 * P.dn)
                    g.dma_start(wtile[:, 0:2048],
                                wqkv_h.ap()[128 * r:128 * r + 128, 4096:6144]).then_inc(P.dsem, 16)
                    g.dma_start(wtile[:, 2048:4096],
                                wout_h.ap()[128 * r:128 * r + 128, :]).then_inc(P.dsem, 16)
                    P.dn += 2
                    g.wait_ge(P.dsem, 16 * P.dn)
                    g.dma_start(wqkv_bf.ap()[128 * r:128 * r + 128, 4096:6144],
                                wtile[:, 0:2048]).then_inc(P.dsem, 16)
                    g.dma_start(wo_bf.ap()[128 * r:128 * r + 128, :],
                                wtile[:, 2048:4096]).then_inc(P.dsem, 16)
                    P.dn += 2
                    g.wait_ge(P.dsem, 16 * P.dn)

        ph12 = contextlib.ExitStack()
        P._scopes.append(ph12)
        zT = P.sbuf("zT", [128, 16 * 2048], BF16)
        ph1 = contextlib.ExitStack()
        psT = ph1.enter_context(nc.psum_tensor(f"psTl1a_{P.uid()}", [128, 1024], BF16))
        for ti in range(NT):
            with nc.Block() as b:
                @b.sync
                def _(sync):
                    P.ld(sync, xt[:], x_t[ti])
                    P.ldwait(sync)
            ln_stats(P, None, None, xt[:], scr[:], stats, 1.0 / D, 1e-5)
            with nc.Block() as b:
                @b.vector
                def _(v):
                    v = P.wrap(v, "v")
                    v.tensor_scalar(zb[:], xt[:], stats["negmu"][:], stats["rstd"][:],
                                    ALU.add, ALU.mult)
            for hf in range(2):
                with nc.Block() as b:
                    @b.tensor
                    def _(t):
                        for k in range(8):
                            kk = 8 * hf + k
                            t.transpose(psT[:, 128 * k:128 * k + 128],
                                        zb[:, 128 * kk:128 * kk + 128], sb_idb[:])
                with nc.Block() as b:
                    @b.vector
                    def _(v):
                        v = P.wrap(v, "v")
                        for k in range(8):
                            kk = 8 * hf + k
                            v.tensor_copy(zT[:, 2048 * kk + 128 * ti:2048 * kk + 128 * ti + 128],
                                          psT[:, 128 * k:128 * k + 128])
        ph1.close()
        P.dump("zT0", zT[:, 0:2048])

        ph2 = contextlib.ExitStack()
        psP = [ph2.enter_context(nc.psum_tensor(f"psP{i}_{P.uid()}", [128, 512], F32))
               for i in range(4)]
        psN = ph2.enter_context(nc.psum_tensor(f"psN_{P.uid()}", [1, 2048], F32))
        kevac = P.sbuf("kevac", [128, 2048], BF16)
        onescolf = P.sbuf("onescolf", [128, 1], F32)
        qn16 = P.sbuf("qn16", [1, 16], F32)
        with nc.Block() as b:
            @b.gpsimd
            def _(g):
                g.memset(onescolf[:], 1.0)
        for oc in range(32):
            with nc.Block() as b:
                @b.sync
                def _(sync):
                    P.ld(sync, ap3(wtile, 0, [[4096, 128], [128, 16], [1, 128]]),
                         ap3(wqkv_bf, 128 * oc, [[6144, 128], [128 * 6144, 16], [1, 128]]))
                    P.ldwait(sync)
                @b.tensor
                def _(t):
                    t.wait_ge(P.dsem, 16 * P.dn)
                    for k in range(16):
                        for tg in range(4):
                            t.matmul(psP[tg][:], wtile[:, 128 * k:128 * k + 128],
                                     zT[:, 2048 * k + 512 * tg:2048 * k + 512 * tg + 512],
                                     start=(k == 0), stop=(k == 15))
            with nc.Block() as b:
                @b.vector
                def _(v):
                    v = P.wrap(v, "v")
                    for tg in range(4):
                        if oc < 16:
                            v.tensor_scalar(qT[:, 2048 * oc + 512 * tg:2048 * oc + 512 * tg + 512],
                                            psP[tg][:], bq_sb[:, oc:oc + 1], None, ALU.add)
                        else:
                            v.tensor_scalar(kevac[:, 512 * tg:512 * tg + 512],
                                            psP[tg][:], bq_sb[:, oc:oc + 1], None, ALU.add)
            src = (qT[:, 2048 * oc:2048 * oc + 2048] if oc < 16 else kevac[:])
            with nc.Block() as b:
                @b.scalar
                def _(sc):
                    sc = P.wrap(sc, "s")
                    sc.activation(sq2[:], src, AF.Square)
            with nc.Block() as b:
                @b.tensor
                def _(t):
                    for tg in range(4):
                        t.matmul(psN[0:1, 512 * tg:512 * tg + 512], onescolf[:],
                                 sq2[:, 512 * tg:512 * tg + 512], start=True, stop=True)
            with nc.Block() as b:
                @b.vector
                def _(v):
                    v = P.wrap(v, "v")
                    v.tensor_reduce(m1[0:1, 0:1], psN[0:1, 0:2048], AX.X, ALU.max)
                    dst = qn16 if oc < 16 else km16
                    v.tensor_copy(dst[0:1, (oc % 16):(oc % 16) + 1], m1[0:1, 0:1])
                if oc >= 16:
                    @b.sync
                    def _(sync):
                        P.ld(sync, K_sh.ap()[128 * (oc - 16):128 * (oc - 16) + 128, :], kevac[:])
                        P.ldwait(sync)
            if oc == 0:
                P.dump("qT0", qT[:, 0:2048])
            if oc == 16:
                P.dump("kevac0", kevac[:])
        P.dump("qn16", qn16[0:1, :])
        P.dump("km16pre", km16[0:1, :])
        P.dump("qTfull", qT[:])
        ph2.close()

        ph2b = contextlib.ExitStack()
        psV = [ph2b.enter_context(nc.psum_tensor(f"psV{i}_{P.uid()}", [128, 512], F32))
               for i in range(8)]
        wvrow = P.sbuf("wvrow", [128, 2 * 2048], BF16)
        vevac = P.sbuf("vevac", [128, 2048], BF16)
        vqack = P.sem("vqack")
        vqd = P.sem("vqd")
        vvs = P.sem("vvs")
        vq_n = [0]
        for ttp in range(8):
            with nc.Block() as b:
                @b.sync
                def _(sync):
                    for k in range(16):
                        kg = 16 * ttp + k
                        if kg >= 2:
                            sync.wait_ge(vqack, kg - 1)
                        sync.dma_start(wvrow[:, 2048 * (k % 2):2048 * (k % 2) + 2048],
                                       wqkv_bf.ap()[128 * k:128 * k + 128, 4096:6144]
                                       ).then_inc(vqd, 16)
                @b.tensor
                def _(t):
                    for k in range(16):
                        kg = 16 * ttp + k
                        t.wait_ge(vqd, 16 * (kg + 1))
                        for tt in range(2):
                            for vc in range(4):
                                ins = t.matmul(psV[4 * tt + vc][:],
                                               zT[:, 2048 * k + 128 * (2 * ttp + tt):2048 * k + 128 * (2 * ttp + tt) + 128],
                                               wvrow[:, 2048 * (k % 2) + 512 * vc:2048 * (k % 2) + 512 * vc + 512],
                                               start=(k == 0), stop=(k == 15))
                        ins.then_inc(vqack, 1)
            for tt in range(2):
                with nc.Block() as b:
                    @b.vector
                    def _(v):
                        v = P.wrap(v, "v")
                        for vc in range(4):
                            ins = v.tensor_tensor(vevac[:, 512 * vc:512 * vc + 512],
                                                  psV[4 * tt + vc][:],
                                                  vb_sb[:, 512 * vc:512 * vc + 512], ALU.add)
                        ins.then_inc(vvs, 1)
                        vq_n[0] += 1
                    @b.sync
                    def _(sync):
                        sync.wait_ge(vvs, vq_n[0])
                        P.ld(sync, V_sh.ap()[128 * (2 * ttp + tt):128 * (2 * ttp + tt) + 128, :],
                             vevac[:])
                        P.ldwait(sync)
        ph2b.close()
        P._scopes.pop()
        ph12.close()

        with nc.Block() as b:
            @b.sync
            def _(sync):
                P.ld(sync, km_in.ap(), km16[0:1, :])
                P.ldwait(sync)
        csK = P.sem("agK"); csV = P.sem("agV"); csM = P.sem("arM")
        with nc.Block() as b:
            @b.gpsimd
            def _(g):
                g.collective_compute("AllGather", ALU.bypass, replica_groups=RG2x4,
                                     ins=[K_sh.ap().opt()], outs=[K_seq.ap().opt()]).then_inc(csK)
                g.wait_ge(csK, 1)
                g.collective_compute("AllGather", ALU.bypass, replica_groups=RG2x4,
                                     ins=[V_sh.ap().opt()], outs=[V_seq.ap().opt()]).then_inc(csV)
                g.wait_ge(csV, 1)
                g.collective_compute("AllReduce", ALU.max, replica_groups=RG2x4,
                                     ins=[km_in.ap().opt()], outs=[km_out.ap().opt()]).then_inc(csM)
                g.wait_ge(csM, 1)
        with nc.Block() as b:
            @b.sync
            def _(sync):
                P.ld(sync, km16[0:1, :], km_out.ap())
                P.ldwait(sync)
        ph2c = contextlib.ExitStack()
        psB = ph2c.enter_context(nc.psum_tensor(f"psB_{P.uid()}", [128, 16], F32))
        with nc.Block() as b:
            @b.scalar
            def _(sc):
                sc = P.wrap(sc, "s")
                sc.activation(qn16[0:1, :], qn16[0:1, :], AF.Sqrt)
                sc.activation(km16[0:1, :], km16[0:1, :], AF.Sqrt)
        with nc.Block() as b:
            @b.vector
            def _(v):
                v = P.wrap(v, "v")
                v.tensor_tensor(qn16[0:1, :], qn16[0:1, :], km16[0:1, :], ALU.mult)
                v.tensor_single_scalar(qn16[0:1, :], qn16[0:1, :], -SCL, ALU.mult)
        with nc.Block() as b:
            @b.tensor
            def _(t):
                t.matmul(psB[:], ones1f[0:1, :], qn16[0:1, :], start=True, stop=True)
        with nc.Block() as b:
            @b.vector
            def _(v):
                v = P.wrap(v, "v")
                v.tensor_copy(negM[:], psB[:])
        ph2c.close()
        P.dump("negM", negM[:])

        ph3 = contextlib.ExitStack()
        psS = [ph3.enter_context(nc.psum_tensor(f"psS{i}_{P.uid()}", [128, 512], F32))
               for i in range(2)]
        psO = [ph3.enter_context(nc.psum_tensor(f"psO{i}_{P.uid()}", [128, 512], F32))
               for i in range(4)]
        psL = ph3.enter_context(nc.psum_tensor(f"psL_{P.uid()}", [128, 512], F32))
        psLB = ph3.enter_context(nc.psum_tensor(f"psLB_{P.uid()}", [128, 512], F32))
        kvt = P.sbuf("kvt", [128, 4 * 128], BF16)
        pT = [P.sbuf(f"pT{i}", [128, 512], BF16) for i in range(2)]
        oT = P.sbuf("oT", [128, 16 * 2048], BF16)
        linv = P.sbuf("linv", [1, 2048], F32)
        lbc = P.sbuf("lbc", [128, 512], F32)
        NB = 64
        tsem = P.sem("fl_t")
        s2sem = P.sem("fl_s")
        osem2 = P.sem("fl_o")
        dsem2 = P.sem("fl_d")
        for h in range(16):
            with nc.Block() as b:
                @b.sync
                def _(sync):
                    for bb in range(NB):
                        gb = NB * h + bb
                        if gb >= 2:
                            sync.wait_ge(osem2, 4 * gb if getattr(P, "flash_safe", False)
                                         else 4 * (gb - 1))
                        rr, tl0 = bb // 16, 128 * (bb % 16)
                        sync.dma_start(kvt[:, 256 * (gb % 2):256 * (gb % 2) + 128],
                                       K_seq.ap()[2048 * rr + 128 * h:2048 * rr + 128 * h + 128,
                                                  tl0:tl0 + 128]).then_inc(dsem2, 16)
                        sync.dma_start(kvt[:, 256 * (gb % 2) + 128:256 * (gb % 2) + 256],
                                       V_seq.ap()[2048 * rr + tl0:2048 * rr + tl0 + 128,
                                                  128 * h:128 * h + 128]).then_inc(dsem2, 16)
                @b.tensor
                def _(t):
                    for bb in range(NB):
                        gb = NB * h + bb
                        t.wait_ge(dsem2, 32 * (gb + 1))
                        for qg in range(4):
                            i = 4 * gb + qg
                            if i >= 2:
                                t.wait_ge(s2sem, i - 1)
                            t.matmul(psS[i % 2][:], kvt[:, 256 * (gb % 2):256 * (gb % 2) + 128],
                                     qT[:, 2048 * h + 512 * qg:2048 * h + 512 * qg + 512],
                                     start=True, stop=True).then_inc(tsem, 1)
                        for qg in range(4):
                            i = 4 * gb + qg
                            t.wait_ge(s2sem, i + 1)
                            t.matmul(psO[qg][:], kvt[:, 256 * (gb % 2) + 128:256 * (gb % 2) + 256],
                                     pT[i % 2][:], start=(bb == 0), stop=(bb == NB - 1))
                            t.matmul(psL[32 * qg:32 * qg + 1, :], onescol[:], pT[i % 2][:],
                                     start=(bb == 0), stop=(bb == NB - 1),
                                     tile_position=(0, 32 * qg)).then_inc(osem2, 1)
                @b.scalar
                def _(sc):
                    for bb in range(NB):
                        gb = NB * h + bb
                        for qg in range(4):
                            i = 4 * gb + qg
                            sc.wait_ge(tsem, i + 1)
                            if i >= 2:
                                sc.wait_ge(osem2, i - 1)
                            sc.activation(pT[i % 2][:], psS[i % 2][:], AF.Exp,
                                          bias=negM[:, h:h + 1], scale=SCL).then_inc(s2sem, 1)
            with nc.Block() as b:
                @b.vector
                def _(v):
                    v = P.wrap(v, "v")
                    for qg in range(4):
                        v.reciprocal(linv[0:1, 512 * qg:512 * qg + 512],
                                     psL[32 * qg:32 * qg + 1, :])
            for qg in range(4):
                with nc.Block() as b:
                    @b.tensor
                    def _(t):
                        t.matmul(psLB[:], ones1f[0:1, :], linv[0:1, 512 * qg:512 * qg + 512],
                                 start=True, stop=True)
                with nc.Block() as b:
                    @b.vector
                    def _(v):
                        v = P.wrap(v, "v")
                        v.tensor_copy(lbc[:], psLB[:])
                        v.tensor_tensor(oT[:, 2048 * h + 512 * qg:2048 * h + 512 * qg + 512],
                                        psO[qg][:], lbc[:], ALU.mult)
            if h == 0:
                P.dump("linv0", linv[0:1, :])
                P.dump("oT0", oT[:, 0:2048])
        P.dump("oTfull", oT[:])
        if getattr(P, "dbg", False):
            kdbg = zb
            for nm, rr in [("Kseq_h5_r0", 0), ("Kseq_h5_r2", 2)]:
                with nc.Block() as b:
                    @b.sync
                    def _(sync):
                        P.ld(sync, kdbg[:], K_seq.ap()[2048 * rr + 128 * 5:2048 * rr + 128 * 5 + 128, :])
                        P.ldwait(sync)
                P.dump(nm, kdbg[:])
            with nc.Block() as b:
                @b.sync
                def _(sync):
                    P.ld(sync, kdbg[:], V_seq.ap()[2048 * 2 + 0:2048 * 2 + 128, :])
                    P.ldwait(sync)
            P.dump("Vseq_r2", kdbg[:])
            with nc.Block() as b:
                @b.sync
                def _(sync):
                    P.ld(sync, kdbg[:], K_sh.ap()[128 * 5:128 * 5 + 128, :])
                    P.ldwait(sync)
            P.dump("Ksh_h5", kdbg[:])
            for rw in range(4):
                with nc.Block() as b:
                    @b.sync
                    def _(sync):
                        P.ld(sync, kdbg[:], V_sh.ap()[512 * rw:512 * rw + 128, :])
                        P.ldwait(sync)
                P.dump(f"Vsh_r{rw}", kdbg[:])
        ph3.close()

        ph4 = contextlib.ExitStack()
        psY4 = [ph4.enter_context(nc.psum_tensor(f"psY4{i}_{P.uid()}", [128, 512], F32))
                for i in range(4)]
        psT4 = ph4.enter_context(nc.psum_tensor(f"psT4_{P.uid()}", [128, 1024], BF16))
        yTl = P.sbuf("yTl", [128, 512], BF16)
        ytokl = [P.sbuf(f"ytokl{j}", [128, 2048], BF16) for j in range(4)]
        for tg in range(4):
            for dc in range(16):
                with nc.Block() as b:
                    @b.sync
                    def _(sync):
                        P.ld(sync, ap3(wtile, 0, [[4096, 128], [128, 16], [1, 128]]),
                             ap3(wo_bf, 128 * dc, [[2048, 128], [128 * 2048, 16], [1, 128]]))
                        P.ldwait(sync)
                    @b.tensor
                    def _(t):
                        t.wait_ge(P.dsem, 16 * P.dn)
                        for e in range(16):
                            t.matmul(psY4[0][:], wtile[:, 128 * e:128 * e + 128],
                                     oT[:, 2048 * e + 512 * tg:2048 * e + 512 * tg + 512],
                                     start=(e == 0), stop=(e == 15))
                with nc.Block() as b:
                    @b.vector
                    def _(v):
                        v = P.wrap(v, "v")
                        v.tensor_scalar(yTl[:], psY4[0][:], bo_sb[:, dc:dc + 1], None, ALU.add)
                with nc.Block() as b:
                    @b.tensor
                    def _(t):
                        for j in range(4):
                            t.transpose(psT4[:, 128 * j:128 * j + 128],
                                        yTl[:, 128 * j:128 * j + 128], sb_idb[:])
                with nc.Block() as b:
                    @b.vector
                    def _(v):
                        v = P.wrap(v, "v")
                        for j in range(4):
                            v.tensor_copy(ytokl[j][:, 128 * dc:128 * dc + 128],
                                          psT4[:, 128 * j:128 * j + 128])
            for j in range(4):
                ti = 4 * tg + j
                with nc.Block() as b:
                    @b.sync
                    def _(sync):
                        P.ld(sync, xt[:], x_t[ti])
                        P.ldwait(sync)
                with nc.Block() as b:
                    @b.vector
                    def _(v):
                        v = P.wrap(v, "v")
                        v.tensor_tensor(scr[:], xt[:], ytokl[j][:], ALU.add)
                with nc.Block() as b:
                    @b.sync
                    def _(sync):
                        P.st(sync, y_t[ti], scr[:])
                        P.stwait(sync)
        ph4.close()



def build_l2ffn(P, x_in, y_out, wnp, tag):
    nc = P.nc
    import ml_dtypes
    P.need_consts([-1.0 / D, D * 1e-5, 0.5 * math.log(D), 0.5, -2.0, -1.0, 100.0])
    id128f_h = P.hconst("c_id128f", host_identity(128, np.float32))

    sw = wnp["select_weight"].astype(np.float32)
    md = wnp["select_mean_diff"].astype(np.float32)
    upd = wnp["lora_update"].astype(np.float32)
    dgn = wnp["lora_degen"].astype(np.float32)
    Wcat = np.zeros((D, 136), np.float32)
    Wcat[:, 0:64] = sw.transpose(1, 0, 2).reshape(D, 64)
    Wcat[:, 64:72] = md.T
    Wcat[:, 72:136] = upd.transpose(1, 0, 2).reshape(D, 64)
    mde_cat = np.zeros((D, 136), np.float32)
    mde_cat[:, 0:64] = np.repeat(md, 8, axis=0).T.reshape(D, 64)
    mde_cat[:, 64:72] = md.T
    mde_cat[:, 72:136] = np.repeat(wnp["debias_w"].astype(np.float32), 8, axis=0).T.reshape(D, 64)
    B_cat = np.zeros((72, D), np.float32)
    B_cat[0:64, :] = dgn.transpose(0, 2, 1).reshape(64, D)
    B_cat[64:72, :] = wnp["bias_w"].astype(np.float32)
    Wcat_h = P.hconst(f"w_wcat_{tag}", Wcat)
    mde_h = P.hconst(f"w_mde_{tag}", mde_cat)
    bcat_h = P.hconst(f"w_bcat_{tag}", B_cat)
    slc_h = P.hconst(f"w_slc_{tag}", np.ascontiguousarray(
        np.broadcast_to(wnp["imp_slope"].astype(np.float32)[None, :], (128, 8))))
    ctr_h = P.hconst(f"w_ctr_{tag}", np.ascontiguousarray(
        np.broadcast_to(wnp["imp_center"].astype(np.float32)[None, :], (128, 8))))

    x_t = x_in.ap().rearrange("(n p) d -> n p d", p=128)
    y_t = y_out.ap().rearrange("(n p) d -> n p d", p=128)

    with P.scope():
        idf = P.sbuf("idf", [128, 128], F32)
        xt = P.sbuf("xt", [128, D], F32)
        zt = P.sbuf("zt", [128, D], F32)
        scr = P.sbuf("scr", [128, D], F32)
        zTt = P.sbuf("zTt", [128, D], F32)
        stats = dict(negmu=P.sbuf("negmu", [128, 1], F32), rstd=P.sbuf("rstd", [128, 1], F32),
                     s1=P.sbuf("s1", [128, 1], F32), ssq=P.sbuf("ssq", [128, 1], F32))
        wcat_sb = P.sbuf("wcat", [128, 16 * 136], F32)
        wtmp = P.sbuf("wtmp2", [128, 136], F32)
        bcat_sb = P.sbuf("bcat", [72, D], F32)
        bc136 = P.sbuf("bc136", [128, 136], F32)
        slc_sb = P.sbuf("slc", [128, 8], F32)
        ctr_sb = P.sbuf("ctr", [128, 8], F32)
        onescolf = P.sbuf("onescolf2", [128, 1], F32)
        ones1f = P.sbuf("ones1f2", [1, 128], F32)
        ones8 = P.sbuf("ones82", [128, 8], F32)
        sqz = P.sbuf("sqz", [128, 1], F32)
        a136 = P.sbuf("a136", [128, 136], F32)
        num8 = P.sbuf("num8", [128, 8], F32)
        den8 = P.sbuf("den8", [128, 8], F32)
        gate8 = P.sbuf("gate8", [128, 8], F32)
        oh8 = P.sbuf("oh8", [128, 8], F32)
        d8 = P.sbuf("d8", [128, 8], F32)
        pfx8 = P.sbuf("pfx8", [128, 8], F32)
        gmax = P.sbuf("gmax", [128, 1], F32)
        mod8 = P.sbuf("mod8", [128, 8], F32)
        Maug = P.sbuf("Maug", [128, 72], F32)
        MT = P.sbuf("MT", [72, 128], F32)
        sq64 = P.sbuf("sq64", [128, 64], F32)

        psA = P.psum("psA2", [128, 136], F32)
        psR = P.psum("psR2", [1, 136], F32)
        psB2 = P.psum("psB22", [128, 136], F32)
        psMT = P.psum("psMT2", [72, 128], F32)
        psMB = [P.psum(f"psMB{i}", [128, 512], F32) for i in range(2)]
        psTz = P.psum("psTz2", [128, 1024], F32)

        with nc.Block() as b:
            @b.sync
            def _(sync):
                P.ld(sync, idf[:], id128f_h.ap())
                P.ld(sync, ap3(wcat_sb, 0, [[16 * 136, 128], [136, 16], [1, 136]]),
                     ap3(Wcat_h, 0, [[136, 128], [128 * 136, 16], [1, 136]]))
                P.ld(sync, bcat_sb[:], bcat_h.ap())
                P.ld(sync, slc_sb[:], slc_h.ap())
                P.ld(sync, ctr_sb[:], ctr_h.ap())
                P.ldwait(sync)
            @b.gpsimd
            def _(g):
                P.emit_consts(g)
                g.memset(onescolf[:], 1.0)
                g.memset(ones1f[:], 1.0)
                g.memset(ones8[:], 1.0)
        for k in range(16):
            with nc.Block() as b:
                @b.sync
                def _(sync):
                    P.ld(sync, wtmp[:], mde_h.ap()[128 * k:128 * k + 128, :])
                    P.ldwait(sync)
            with nc.Block() as b:
                @b.vector
                def _(v):
                    v = P.wrap(v, "v")
                    v.tensor_tensor(wtmp[:], wtmp[:], wcat_sb[:, 136 * k:136 * k + 136],
                                    ALU.mult)
            with nc.Block() as b:
                @b.tensor
                def _(t):
                    t.matmul(psR[:], onescolf[:], wtmp[:], start=(k == 0), stop=(k == 15))
        with nc.Block() as b:
            @b.vector
            def _(v):
                v = P.wrap(v, "v")
                v.tensor_copy(wtmp[0:1, :], psR[0:1, :])
        with nc.Block() as b:
            @b.tensor
            def _(t):
                t.matmul(psB2[:], ones1f[0:1, :], wtmp[0:1, :], start=True, stop=True)
        with nc.Block() as b:
            @b.vector
            def _(v):
                v = P.wrap(v, "v")
                v.tensor_copy(bc136[:], psB2[:])

        for ti in range(NT):
            with nc.Block() as b:
                @b.sync
                def _(sync):
                    P.ld(sync, xt[:], x_t[ti])
                    P.ldwait(sync)
            ln_stats(P, None, None, xt[:], scr[:], stats, 1.0 / D, 1e-5)
            with nc.Block() as b:
                @b.vector
                def _(v):
                    v = P.wrap(v, "v")
                    v.tensor_scalar(zt[:], xt[:], stats["negmu"][:], stats["rstd"][:],
                                    ALU.add, ALU.mult)
            for hf in range(2):
                with nc.Block() as b:
                    @b.tensor
                    def _(t):
                        for k in range(8):
                            kk = 8 * hf + k
                            t.transpose(psTz[:, 128 * k:128 * k + 128],
                                        zt[:, 128 * kk:128 * kk + 128], idf[:])
                with nc.Block() as b:
                    @b.vector
                    def _(v):
                        v = P.wrap(v, "v")
                        v.tensor_copy(zTt[:, 1024 * hf:1024 * hf + 1024], psTz[:])
            with nc.Block() as b:
                @b.scalar
                def _(sc):
                    sc = P.wrap(sc, "s")
                    sc.activation(scr[:], zt[:], AF.Square, bias=0.0, scale=1.0,
                                  accum_out=sqz[:])
                @b.tensor
                def _(t):
                    for k in range(16):
                        t.matmul(psA[:], zTt[:, 128 * k:128 * k + 128],
                                 wcat_sb[:, 136 * k:136 * k + 136],
                                 start=(k == 0), stop=(k == 15))
            with nc.Block() as b:
                @b.vector
                def _(v):
                    v = P.wrap(v, "v")
                    v.tensor_copy(a136[:], psA[:])
                    v.tensor_tensor(sq64[:], a136[:, 0:64], bc136[:, 0:64], ALU.subtract)
            with nc.Block() as b:
                @b.scalar
                def _(sc):
                    sc = P.wrap(sc, "s")
                    sc.activation(sq64[:], sq64[:], AF.Square)
            with nc.Block() as b:
                @b.vector
                def _(v):
                    v = P.wrap(v, "v")
                    v.tensor_reduce(num8[:], ap3(sq64, 0, [[64, 128], [8, 8], [1, 8]]),
                                    AX.X, ALU.add)
                    v.tensor_single_scalar(den8[:], a136[:, 64:72], -2.0, ALU.mult)
                    v.tensor_scalar(den8[:], den8[:], sqz[:], None, ALU.add)
                    v.tensor_tensor(den8[:], den8[:], bc136[:, 64:72], ALU.add)
                    v.reciprocal(den8[:], den8[:])
                    v.tensor_tensor(num8[:], num8[:], den8[:], ALU.mult)
                    v.tensor_tensor(gate8[:], num8[:], ctr_sb[:], ALU.subtract)
                    v.tensor_tensor(gate8[:], gate8[:], slc_sb[:], ALU.mult)
            with nc.Block() as b:
                @b.scalar
                def _(sc):
                    sc = P.wrap(sc, "s")
                    sc.activation(gate8[:], gate8[:], AF.Sigmoid)
            with nc.Block() as b:
                @b.vector
                def _(v):
                    v = P.wrap(v, "v")
                    v.tensor_reduce(gmax[:], gate8[:], AX.X, ALU.max)
                    v.tensor_scalar(d8[:], gate8[:], gmax[:], None, ALU.subtract)
                    v.tensor_single_scalar(d8[:], d8[:], 0.0, ALU.is_equal)
                    v.tensor_tensor_scan(pfx8[:], ones8[:], d8[:], 0.0, ALU.mult, ALU.add)
                    v.tensor_single_scalar(pfx8[:], pfx8[:], 1.0, ALU.is_equal)
                    v.tensor_tensor(oh8[:], d8[:], pfx8[:], ALU.mult)
                    v.tensor_tensor(sq64[:], a136[:, 72:136], bc136[:, 72:136], ALU.subtract)
                    v.tensor_scalar(mod8[:], sq64[:, 0:8], oh8[:, 0:1], None, ALU.mult)
                    for n in range(1, 8):
                        v.scalar_tensor_tensor(mod8[:], sq64[:, 8 * n:8 * n + 8],
                                               oh8[:, n:n + 1], mod8[:], ALU.mult, ALU.add)
                    for n in range(8):
                        v.tensor_scalar(Maug[:, 8 * n:8 * n + 8], mod8[:],
                                        oh8[:, n:n + 1], None, ALU.mult)
                    v.tensor_copy(Maug[:, 64:72], oh8[:])
            with nc.Block() as b:
                @b.tensor
                def _(t):
                    t.transpose(psMT[:], Maug[:], idf[:])
            with nc.Block() as b:
                @b.vector
                def _(v):
                    v = P.wrap(v, "v")
                    v.tensor_copy(MT[:], psMT[:])
            for c4 in range(4):
                with nc.Block() as b:
                    @b.tensor
                    def _(t):
                        t.matmul(psMB[0][:], MT[:], bcat_sb[:, 512 * c4:512 * c4 + 512],
                                 start=True, stop=True)
                with nc.Block() as b:
                    @b.vector
                    def _(v):
                        v = P.wrap(v, "v")
                        sl = slice(512 * c4, 512 * c4 + 512)
                        v.tensor_tensor(scr[:, sl], psMB[0][:], zt[:, sl], ALU.subtract)
                        v.tensor_scalar(scr[:, sl], scr[:, sl], gmax[:], None, ALU.mult)
                        v.tensor_tensor(scr[:, sl], scr[:, sl], zt[:, sl], ALU.add)
                        v.scalar_tensor_tensor(scr[:, sl], scr[:, sl], 0.5, xt[:, sl],
                                               ALU.mult, ALU.add)
            with nc.Block() as b:
                @b.sync
                def _(sync):
                    P.st(sync, y_t[ti], scr[:])
                    P.stwait(sync)



def host_rope_tables(pos0, ntok):
    dh = 128
    inv_freq = 1.0 / 10000.0 ** (np.arange(0, dh, 2, dtype=np.float64) / dh)
    pos = np.arange(pos0, pos0 + ntok, dtype=np.float64)
    fr = pos[:, None] * inv_freq[None, :]
    cos = np.concatenate([np.cos(fr), np.cos(fr)], axis=-1)
    sin = np.concatenate([np.sin(fr), np.sin(fr)], axis=-1)
    return cos.astype(np.float32), sin.astype(np.float32)


def build_l2attn(P, x_in, y_out, wnp, tag):
    nc = P.nc
    import ml_dtypes
    CH = 128
    NCH2 = 8192 // CH
    lam = np.exp2(-8.0 * np.arange(1, 17) / 16.0)
    id128b = P.hconst("c_id128b", host_identity(128, np.float32).astype(ml_dtypes.bfloat16))
    P.need_consts([-1.0 / D, D * 1e-5, 0.5 * math.log(D)])

    wqT_h = P.hconst(f"w_wqT_{tag}", np.ascontiguousarray(wnp["wq"].T).astype(np.float32))
    wkT_h = P.hconst(f"w_wkT_{tag}", np.ascontiguousarray(wnp["wk"].T).astype(np.float32))
    wvT_h = P.hconst(f"w_wvT_{tag}", np.ascontiguousarray(wnp["wv"].T).astype(np.float32))
    woT_h = P.hconst(f"w_woT_{tag}", np.ascontiguousarray(wnp["wo"].T).astype(np.float32))
    cosF, sinF = [], []
    cosT, sinT = [], []
    for c in range(8):
        pos0 = 2048 * (c % 4)
        co, si = host_rope_tables(pos0, 2048)
        cosF.append(np.ascontiguousarray(co.T))
        ss_ = si.T.copy()
        ss_[0:64, :] *= -1.0
        sinF.append(np.ascontiguousarray(ss_))
        cosT.append(np.ascontiguousarray(co.reshape(16, 128, 128).transpose(1, 0, 2)
                                         .reshape(128, 16 * 128)))
        sinT.append(np.ascontiguousarray(si.reshape(16, 128, 128).transpose(1, 0, 2)
                                         .reshape(128, 16 * 128)))
    cosF_h = P.hconst(f"c_cosF_{tag}", cosF)
    sinF_h = P.hconst(f"c_sinF_{tag}", sinF)
    cosT_h = P.hconst(f"c_cosT_{tag}", cosT)
    sinT_h = P.hconst(f"c_sinT_{tag}", sinT)
    r = np.arange(CH, dtype=np.float64)
    diagcat = np.zeros((128, 16 * 128), np.float32)
    for h in range(16):
        dm = np.where(r[None, :] - r[:, None] >= 0,
                      np.exp(-lam[h] * (r[None, :] - r[:, None])), 0.0)
        diagcat[:, 128 * h:128 * h + 128] = dm
    diag_h = P.hconst(f"c_diag_{tag}", diagcat)
    qdec = np.zeros((128, 16 * 128), np.float32)
    for h in range(16):
        qdec[:, 128 * h:128 * h + 128] = np.exp(-lam[h] * (r[None, :] + 1.0))
    qdec_h = P.hconst(f"c_qdec_{tag}", qdec)
    kdb = np.zeros((128, 16), np.float32)
    for h in range(16):
        kdb[:, h] = np.exp(-lam[h] * (CH - 1.0 - r))
    kdb_h = P.hconst(f"c_kdb_{tag}", kdb)
    blkb = np.zeros((128, 16 * 128), np.float32)
    for h in range(16):
        blkb[:, 128 * h:128 * h + 128] = np.exp(-lam[h] * CH)
    blkb_h = P.hconst(f"c_blkb_{tag}", blkb)
    oidx = []
    for c in range(8):
        pos0 = 2048 * (c % 4)
        a = np.zeros((128, 64), np.int32)
        for ec in range(16):
            for tg in range(4):
                a[:, 4 * ec + tg] = (128 * ec + np.arange(128)) * 16 + pos0 // 512 + tg
        oidx.append(np.ascontiguousarray(a))
    oidx_h = P.hconst(f"c_oidx_{tag}", oidx)

    qsh = P.dram(f"qsh_{tag}", [D, TOK], BF16)
    ksh = P.dram(f"ksh_{tag}", [512, TOK], BF16)
    ktsh = P.dram(f"ktsh_{tag}", [TOK, 512], F32)
    vsh = P.dram(f"vsh_{tag}", [TOK, 512], F32)
    q_seq = P.dram(f"qseq_{tag}", [4 * D, TOK], BF16)
    k_seq = P.dram(f"kseq_{tag}", [4 * 512, TOK], BF16)
    kt_seq = P.dram(f"ktseq_{tag}", [4 * TOK, 512], F32)
    v_seq = P.dram(f"vseq_{tag}", [4 * TOK, 512], F32)
    oT_seq = P.dram(f"oTseq_{tag}", [D, 8192], BF16)
    wq_bf = P.dram(f"wqbf_{tag}", [D, D], BF16)
    wo_bf = P.dram(f"wobf_{tag}", [D, D], BF16)

    x_t = x_in.ap().rearrange("(n p) d -> n p d", p=128)
    y_t = y_out.ap().rearrange("(n p) d -> n p d", p=128)

    with P.scope():
        sb_idb = P.sbuf("idb", [128, 128], BF16)
        xt = P.sbuf("xt", [128, D], F32)
        scr = P.sbuf("scr", [128, D], F32)
        zb = P.sbuf("zb", [128, D], BF16)
        stats = dict(negmu=P.sbuf("negmu", [128, 1], F32), rstd=P.sbuf("rstd", [128, 1], F32),
                     s1=P.sbuf("s1", [128, 1], F32), ssq=P.sbuf("ssq", [128, 1], F32))
        wtile = P.sbuf("wtile2", [128, 2 * 2048], BF16)
        oidx_sb = P.sbuf("oidx", [128, 64], mybir.dt.int32)
        ph_mid = contextlib.ExitStack()
        P._scopes.append(ph_mid)
        wkv_sb = P.sbuf("wkv", [128, 16 * 1024], BF16)
        cosF_sb = P.sbuf("cosF", [128, 2048], F32)
        sinF_sb = P.sbuf("sinF", [128, 2048], F32)
        cosT_sb = P.sbuf("cosT", [128, 2048], F32)
        sinT_sb = P.sbuf("sinT", [128, 2048], F32)
        diag_sb = P.sbuf("diag", [128, 2048], F32)
        qdec_sb = P.sbuf("qdec", [128, 2048], F32)
        kdb_sb = P.sbuf("kdb", [128, 16], F32)
        blkb_sb = P.sbuf("blkb", [128, 2048], F32)
        rt1 = P.sbuf("rt1", [128, 2048], F32)
        rt2 = P.sbuf("rt2", [128, 2048], F32)

        with nc.Block() as b:
            @b.sync
            def _(sync):
                P.ld(sync, sb_idb[:], id128b.ap())
                P.ld(sync, cosF_sb[:], cosF_h.ap())
                P.ld(sync, sinF_sb[:], sinF_h.ap())
                P.ld(sync, cosT_sb[:], cosT_h.ap())
                P.ld(sync, sinT_sb[:], sinT_h.ap())
                P.ld(sync, diag_sb[:], diag_h.ap())
                P.ld(sync, qdec_sb[:], qdec_h.ap())
                P.ld(sync, kdb_sb[:], kdb_h.ap())
                P.ld(sync, blkb_sb[:], blkb_h.ap())
                P.ld(sync, oidx_sb[:], oidx_h.ap())
                P.ldwait(sync)
            @b.gpsimd
            def _(g):
                P.emit_consts(g)
                for r_ in range(16):
                    g.dma_start(wtile[:, 0:2048],
                                wqT_h.ap()[128 * r_:128 * r_ + 128, :]).then_inc(P.dsem, 16)
                    g.dma_start(wtile[:, 2048:4096],
                                woT_h.ap()[128 * r_:128 * r_ + 128, :]).then_inc(P.dsem, 16)
                    P.dn += 2
                    g.wait_ge(P.dsem, 16 * P.dn)
                    g.dma_start(wq_bf.ap()[128 * r_:128 * r_ + 128, :],
                                wtile[:, 0:2048]).then_inc(P.dsem, 16)
                    g.dma_start(wo_bf.ap()[128 * r_:128 * r_ + 128, :],
                                wtile[:, 2048:4096]).then_inc(P.dsem, 16)
                    P.dn += 2
                    g.wait_ge(P.dsem, 16 * P.dn)
                    g.dma_start(wkv_sb[:, 1024 * r_:1024 * r_ + 512],
                                wkT_h.ap()[128 * r_:128 * r_ + 128, :]).then_inc(P.dsem, 16)
                    g.dma_start(wkv_sb[:, 1024 * r_ + 512:1024 * r_ + 1024],
                                wvT_h.ap()[128 * r_:128 * r_ + 128, :]).then_inc(P.dsem, 16)
                    P.dn += 2
                    g.wait_ge(P.dsem, 16 * P.dn)

        ph1 = contextlib.ExitStack()
        P._scopes.append(ph1)
        zT = P.sbuf("zT2", [128, 16 * 2048], BF16)
        ktstage = P.sbuf("ktstage", [128, 2048], F32)
        idf2 = P.sbuf("idf2", [128, 128], F32)
        psT = ph1.enter_context(nc.psum_tensor(f"psTl2_{P.uid()}", [128, 1024], BF16))
        psTf = ph1.enter_context(nc.psum_tensor(f"psTfl2_{P.uid()}", [128, 1024], F32))
        id128f_h2 = P.hconst("c_id128f", host_identity(128, np.float32))
        with nc.Block() as b:
            @b.sync
            def _(sync):
                P.ld(sync, idf2[:], id128f_h2.ap())
                P.ldwait(sync)
        for ti in range(NT):
            with nc.Block() as b:
                @b.sync
                def _(sync):
                    P.ld(sync, xt[:], x_t[ti])
                    P.ldwait(sync)
            ln_stats(P, None, None, xt[:], scr[:], stats, 1.0 / D, 1e-5)
            with nc.Block() as b:
                @b.vector
                def _(v):
                    v = P.wrap(v, "v")
                    v.tensor_scalar(zb[:], xt[:], stats["negmu"][:], stats["rstd"][:],
                                    ALU.add, ALU.mult)
            for hf in range(2):
                with nc.Block() as b:
                    @b.tensor
                    def _(t):
                        for k in range(8):
                            kk = 8 * hf + k
                            t.transpose(psT[:, 128 * k:128 * k + 128],
                                        zb[:, 128 * kk:128 * kk + 128], sb_idb[:])
                with nc.Block() as b:
                    @b.vector
                    def _(v):
                        v = P.wrap(v, "v")
                        for k in range(8):
                            kk = 8 * hf + k
                            v.tensor_copy(zT[:, 2048 * kk + 128 * ti:2048 * kk + 128 * ti + 128],
                                          psT[:, 128 * k:128 * k + 128])

        psP = [ph1.enter_context(nc.psum_tensor(f"psPl2{i}_{P.uid()}", [128, 512], F32))
               for i in range(4)]
        for oc in range(20):
            with nc.Block() as b:
                @b.sync
                def _(sync):
                    if oc < 16:
                        P.ld(sync, ap3(wtile, 0, [[4096, 128], [128, 16], [1, 128]]),
                             ap3(wq_bf, 128 * oc, [[2048, 128], [128 * 2048, 16], [1, 128]]))
                        P.ldwait(sync)
                @b.tensor
                def _(t):
                    t.wait_ge(P.dsem, 16 * P.dn)
                    for k in range(16):
                        if oc < 16:
                            lh = wtile[:, 128 * k:128 * k + 128]
                        else:
                            lh = wkv_sb[:, 1024 * k + 128 * (oc - 16):1024 * k + 128 * (oc - 16) + 128]
                        for tg in range(4):
                            t.matmul(psP[tg][:], lh,
                                     zT[:, 2048 * k + 512 * tg:2048 * k + 512 * tg + 512],
                                     start=(k == 0), stop=(k == 15))
            with nc.Block() as b:
                @b.vector
                def _(v):
                    v = P.wrap(v, "v")
                    for tg in range(4):
                        v.tensor_copy(rt1[:, 512 * tg:512 * tg + 512], psP[tg][:])
            with nc.Block() as b:
                @b.sync
                def _(sync):
                    P.ld(sync, rt2[0:64, :], rt1[64:128, :])
                    P.ld(sync, rt2[64:128, :], rt1[0:64, :])
                    P.ldwait(sync)
            with nc.Block() as b:
                @b.vector
                def _(v):
                    v = P.wrap(v, "v")
                    v.tensor_tensor(rt1[:], rt1[:], cosF_sb[:], ALU.mult)
                    v.tensor_tensor(rt2[:], rt2[:], sinF_sb[:], ALU.mult)
                    v.tensor_tensor(rt1[:], rt1[:], rt2[:], ALU.add)
            if oc == 0:
                P.dump("q0fm", rt1[:])
            if oc == 16:
                P.dump("k0fm", rt1[:])
            with nc.Block() as b:
                @b.gpsimd
                def _(g):
                    if oc < 16:
                        g.dma_start(qsh.ap()[128 * oc:128 * oc + 128, :],
                                    rt1[:]).then_inc(P.dsem, 16)
                    else:
                        g.dma_start(ksh.ap()[128 * (oc - 16):128 * (oc - 16) + 128, :],
                                    rt1[:]).then_inc(P.dsem, 16)
                    P.dn += 1
                    g.wait_ge(P.dsem, 16 * P.dn)
            if oc >= 16:
                kh = oc - 16
                for hf in range(2):
                    with nc.Block() as b:
                        @b.tensor
                        def _(t):
                            for j in range(8):
                                jj = 8 * hf + j
                                t.transpose(psTf[:, 128 * j:128 * j + 128],
                                            rt1[:, 128 * jj:128 * jj + 128], idf2[:])
                    with nc.Block() as b:
                        @b.vector
                        def _(v):
                            v = P.wrap(v, "v")
                            v.tensor_copy(ktstage[:, 1024 * hf:1024 * hf + 1024], psTf[:])
                with nc.Block() as b:
                    @b.sync
                    def _(sync):
                        P.ld(sync, ap3(ktsh, 128 * kh, [[512, 128], [512 * 128, 16], [1, 128]]),
                             ap3(ktstage, 0, [[2048, 128], [128, 16], [1, 128]]))
                        P.ldwait(sync)
        for ttp in range(8):
            with nc.Block() as b:
                @b.tensor
                def _(t):
                    for k in range(16):
                        for tt in range(2):
                            t.matmul(psP[2 * tt][:] if False else psP[tt][:],
                                     zT[:, 2048 * k + 128 * (2 * ttp + tt):2048 * k + 128 * (2 * ttp + tt) + 128],
                                     wkv_sb[:, 1024 * k + 512:1024 * k + 1024],
                                     start=(k == 0), stop=(k == 15))
            with nc.Block() as b:
                @b.vector
                def _(v):
                    v = P.wrap(v, "v")
                    for tt in range(2):
                        v.tensor_copy(ktstage[:, 512 * tt:512 * tt + 512], psP[tt][:])
            with nc.Block() as b:
                @b.sync
                def _(sync):
                    P.ld(sync, vsh.ap()[256 * ttp:256 * ttp + 128, :], ktstage[:, 0:512])
                    P.ld(sync, vsh.ap()[256 * ttp + 128:256 * ttp + 256, :], ktstage[:, 512:1024])
                    P.ldwait(sync)
        ph1.close()
        P._scopes.pop()

        for nm, a, o in [("agq2", qsh, q_seq), ("agk2", ksh, k_seq),
                         ("agkt2", ktsh, kt_seq), ("agv2", vsh, v_seq)]:
            cs = P.sem(nm)
            with nc.Block() as b:
                @b.gpsimd
                def _(g):
                    g.collective_compute("AllGather", ALU.bypass, replica_groups=RG2x4,
                                         ins=[a.ap().opt()], outs=[o.ap().opt()]).then_inc(cs)
                    g.wait_ge(cs, 1)

        ph4 = contextlib.ExitStack()
        psS4 = ph4.enter_context(nc.psum_tensor(f"psS4_{P.uid()}", [128, 512], F32))
        psO4 = ph4.enter_context(nc.psum_tensor(f"psO4_{P.uid()}", [128, 512], F32))
        psKV = ph4.enter_context(nc.psum_tensor(f"psKV_{P.uid()}", [128, 512], F32))
        kvt2 = P.sbuf("kvt2", [128, 2 * 5 * 128], BF16)
        kvf2 = P.sbuf("kvf2", [128, 2 * 2 * 128], F32)
        kvcat = P.sbuf("kvcat", [128, 512], F32)
        kv2 = P.sbuf("kv2", [128, 512], F32)
        qp4 = P.sbuf("qp4", [128, 512], F32)
        ktd4 = P.sbuf("ktd4", [128, 512], F32)
        pT4 = P.sbuf("pT4", [128, 512], F32)
        oT4 = P.sbuf("oT4", [128, 4 * 1024], BF16)
        if getattr(P, "dbg", False):
            P.dbg_s = P.sbuf("dbgs", [128, 512], F32)
            P.dbg_p = P.sbuf("dbgp", [128, 512], F32)
            P.dbg_q = P.sbuf("dbgq", [128, 128], F32)
            P.dbg_v = P.sbuf("dbgv", [128, 128], F32)
        l2t = P.sem("l2t")
        l2v = P.sem("l2v")
        l2d = P.sem("l2d")
        l2f = P.sem("l2f")
        nflush = [0]
        for hg in range(4):
            with nc.Block() as b:
                @b.gpsimd
                def _(g):
                    g.memset(kvcat[:], 0.0)
            with nc.Block() as b:
                @b.sync
                def _(sync):
                    for c in range(NCH2):
                        gb = NCH2 * hg + c
                        par = 640 * (gb % 2)
                        if gb >= 2:
                            sync.wait_ge(l2v, 4 * (gb - 2) + 4)
                            sync.wait_ge(l2t, 3 * (gb - 2) + 3)
                        rr, tl0 = c // 16, 128 * (c % 16)
                        parf = 256 * (gb % 2)
                        sync.dma_start(kvt2[:, par:par + 128],
                                       k_seq.ap()[512 * rr + 128 * hg:512 * rr + 128 * hg + 128,
                                                  tl0:tl0 + 128]).then_inc(l2d, 16)
                        sync.dma_start(kvf2[:, parf:parf + 128],
                                       kt_seq.ap()[2048 * rr + tl0:2048 * rr + tl0 + 128,
                                                   128 * hg:128 * hg + 128]).then_inc(l2d, 16)
                        sync.dma_start(kvf2[:, parf + 128:parf + 256],
                                       v_seq.ap()[2048 * rr + tl0:2048 * rr + tl0 + 128,
                                                  128 * hg:128 * hg + 128]).then_inc(l2d, 16)
                        for hi in range(4):
                            h = 4 * hg + hi
                            sync.dma_start(kvt2[:, par + 128 + 128 * hi:par + 256 + 128 * hi],
                                           q_seq.ap()[2048 * rr + 128 * h:2048 * rr + 128 * h + 128,
                                                      tl0:tl0 + 128]).then_inc(l2d, 16)
                @b.vector
                def _(v):
                    for c in range(NCH2):
                        gb = NCH2 * hg + c
                        par = 640 * (gb % 2)
                        parf = 256 * (gb % 2)
                        if c % 8 == 0 and (8 * hg + c // 8) > 0:
                            v.wait_ge(l2f, 64 * (8 * hg + c // 8))
                        v.wait_ge(l2d, 16 * 7 * (gb + 1))
                        for hi in range(4):
                            h = 4 * hg + hi
                            v.tensor_tensor(qp4[:, 128 * hi:128 * hi + 128],
                                            kvt2[:, par + 128 + 128 * hi:par + 256 + 128 * hi],
                                            qdec_sb[:, 128 * h:128 * h + 128], ALU.mult)
                        for hi in range(4):
                            h = 4 * hg + hi
                            ins = v.tensor_scalar(ktd4[:, 128 * hi:128 * hi + 128],
                                                  kvf2[:, parf:parf + 128],
                                                  kdb_sb[:, h:h + 1], None, ALU.mult)
                        ins.then_inc(l2v, 1)
                        v.wait_ge(l2t, 3 * gb + 1)
                        if gb == 0 and getattr(P, "dbg", False):
                            v.tensor_copy(P.dbg_s[:], psS4[:])
                        v.tensor_tensor(pT4[:], psS4[:], diag_sb[:, 512 * hg:512 * hg + 512],
                                        ALU.mult).then_inc(l2v, 1)
                        if gb == 0 and getattr(P, "dbg", False):
                            v.tensor_copy(P.dbg_p[:], pT4[:])
                            v.tensor_copy(P.dbg_q[:], kvt2[:, par + 128:par + 256])
                            v.tensor_copy(P.dbg_v[:], kvf2[:, parf + 128:parf + 256])
                        v.wait_ge(l2t, 3 * gb + 2)
                        for hi in range(4):
                            ins = v.tensor_copy(
                                oT4[:, 1024 * hi + 128 * (c % 8):1024 * hi + 128 * (c % 8) + 128],
                                psO4[:, 128 * hi:128 * hi + 128])
                        ins.then_inc(l2v, 1)
                        v.wait_ge(l2t, 3 * gb + 3)
                        v.tensor_tensor(kv2[:], kvcat[:], blkb_sb[:, 512 * hg:512 * hg + 512],
                                        ALU.mult)
                        v.tensor_tensor(kvcat[:], kv2[:], psKV[:], ALU.add).then_inc(l2v, 1)
                @b.tensor
                def _(t):
                    for c in range(NCH2):
                        gb = NCH2 * hg + c
                        par = 640 * (gb % 2)
                        parf = 256 * (gb % 2)
                        t.wait_ge(l2v, 4 * gb + 1)
                        for hi in range(4):
                            ins = t.matmul(psS4[:, 128 * hi:128 * hi + 128],
                                           kvt2[:, par:par + 128],
                                           kvt2[:, par + 128 + 128 * hi:par + 256 + 128 * hi],
                                           start=True, stop=True)
                        ins.then_inc(l2t, 1)
                        t.wait_ge(l2v, 4 * gb + 2)
                        for hi in range(4):
                            t.matmul(psO4[:, 128 * hi:128 * hi + 128],
                                     kvf2[:, parf + 128:parf + 256],
                                     pT4[:, 128 * hi:128 * hi + 128],
                                     start=True, stop=False)
                            ins = t.matmul(psO4[:, 128 * hi:128 * hi + 128],
                                           kvcat[:, 128 * hi:128 * hi + 128],
                                           qp4[:, 128 * hi:128 * hi + 128],
                                           start=False, stop=True)
                        ins.then_inc(l2t, 1)
                        for hi in range(4):
                            ins = t.matmul(psKV[:, 128 * hi:128 * hi + 128],
                                           ktd4[:, 128 * hi:128 * hi + 128],
                                           kvf2[:, parf + 128:parf + 256],
                                           start=True, stop=True)
                        ins.then_inc(l2t, 1)
                @b.gpsimd
                def _(g):
                    for c in range(NCH2):
                        gb = NCH2 * hg + c
                        if c % 8 == 7:
                            g.wait_ge(l2v, 4 * gb + 3)
                            for hi in range(4):
                                h = 4 * hg + hi
                                g.dma_start(
                                    oT_seq.ap()[128 * h:128 * h + 128,
                                                128 * (c - 7):128 * (c + 1)],
                                    oT4[:, 1024 * hi:1024 * hi + 1024]).then_inc(l2f, 16)
                            nflush[0] += 1
        if getattr(P, "dbg", False):
            P.dump("ps0", P.dbg_s[:]); P.dump("pt0", P.dbg_p[:])
            P.dump("qt0", P.dbg_q[:]); P.dump("vt0", P.dbg_v[:])
            dbg2 = P.sbuf("dbg2", [128, 512], BF16)
            for nm, src_ in [("oseq0", oT_seq), ("kts0", kt_seq), ("vs0", v_seq)]:
                with nc.Block() as b:
                    @b.gpsimd
                    def _(g):
                        g.dma_start(dbg2[:], src_.ap()[0:128, 0:512]).then_inc(P.dsem, 16)
                        P.dn += 1
                        g.wait_ge(P.dsem, 16 * P.dn)
                P.dump(nm, dbg2[:])
        ph4.close()
        P._scopes.pop()
        ph_mid.close()

        ph5 = contextlib.ExitStack()
        psY5 = ph5.enter_context(nc.psum_tensor(f"psY5_{P.uid()}", [128, 512], F32))
        psT5 = ph5.enter_context(nc.psum_tensor(f"psT5_{P.uid()}", [128, 1024], BF16))
        oTmy = P.sbuf("oTmy", [128, 16 * 2048], BF16)
        yT5 = P.sbuf("yT5", [128, 512], BF16)
        ytok5 = [P.sbuf(f"ytok5{j}", [128, 2048], BF16) for j in range(4)]
        with nc.Block() as b:
            @b.gpsimd
            def _(g):
                for ec in range(16):
                    for tg in range(4):
                        g.indirect_dma_start(
                            oTmy[:, 2048 * ec + 512 * tg:2048 * ec + 512 * tg + 512], None,
                            ap3(oT_seq, 0, [[512, 32768], [1, 512]]),
                            IndirectOffsetOnAxis(oidx_sb[:, 4 * ec + tg:4 * ec + tg + 1], 0)
                        ).then_inc(P.dsem, 16)
                        P.dn += 1
                g.wait_ge(P.dsem, 16 * P.dn)
        for tg in range(4):
            for dc in range(16):
                with nc.Block() as b:
                    @b.sync
                    def _(sync):
                        P.ld(sync, ap3(wtile, 0, [[4096, 128], [128, 16], [1, 128]]),
                             ap3(wo_bf, 128 * dc, [[2048, 128], [128 * 2048, 16], [1, 128]]))
                        P.ldwait(sync)
                    @b.tensor
                    def _(t):
                        t.wait_ge(P.dsem, 16 * P.dn)
                        for ec in range(16):
                            t.matmul(psY5[:], wtile[:, 128 * ec:128 * ec + 128],
                                     oTmy[:, 2048 * ec + 512 * tg:2048 * ec + 512 * tg + 512],
                                     start=(ec == 0), stop=(ec == 15))
                with nc.Block() as b:
                    @b.vector
                    def _(v):
                        v = P.wrap(v, "v")
                        v.tensor_copy(yT5[:], psY5[:])
                with nc.Block() as b:
                    @b.tensor
                    def _(t):
                        for j in range(4):
                            t.transpose(psT5[:, 128 * j:128 * j + 128],
                                        yT5[:, 128 * j:128 * j + 128], sb_idb[:])
                with nc.Block() as b:
                    @b.vector
                    def _(v):
                        v = P.wrap(v, "v")
                        for j in range(4):
                            v.tensor_copy(ytok5[j][:, 128 * dc:128 * dc + 128],
                                          psT5[:, 128 * j:128 * j + 128])
            for j in range(4):
                ti = 4 * tg + j
                with nc.Block() as b:
                    @b.sync
                    def _(sync):
                        P.ld(sync, xt[:], x_t[ti])
                        P.ldwait(sync)
                with nc.Block() as b:
                    @b.vector
                    def _(v):
                        v = P.wrap(v, "v")
                        v.tensor_tensor(scr[:], xt[:], ytok5[j][:], ALU.add)
                with nc.Block() as b:
                    @b.sync
                    def _(sync):
                        P.st(sync, y_t[ti], scr[:])
                        P.stwait(sync)
        ph5.close()



def build_l3attn(P, x_in, y_out, wnp, tag):
    nc = P.nc
    import ml_dtypes
    id128b = P.hconst("c_id128b", host_identity(128, np.float32).astype(ml_dtypes.bfloat16))
    SCL = 1.0 / math.sqrt(128.0)
    P.need_consts([-1.0 / D, D * 1e-5, 0.5 * math.log(D), SCL, -1.0])

    wq_h = P.hconst(f"w_q3_{tag}", wnp["Wq"].astype(np.float32))
    wk_h = P.hconst(f"w_k3_{tag}", wnp["Wk"].astype(np.float32))
    wv_h = P.hconst(f"w_v3_{tag}", wnp["Wv"].astype(np.float32))
    wo_h = P.hconst(f"w_o3_{tag}", wnp["Wo"].astype(np.float32))
    rm = []
    bidx = []
    for c in range(8):
        pos0 = 2048 * (c % 4)
        a = np.zeros((128, 256), np.float32)
        for b_ in range(64):
            for qg in range(4):
                cut = pos0 + 512 * qg
                a[:, 4 * b_ + qg] = np.where(128 * b_ + np.arange(128) < cut, 0.0, -1.0e30)
        rm.append(np.ascontiguousarray(a))
        bi = np.zeros((128, 16), np.int32)
        for qg in range(4):
            for dl in range(4):
                bi[:, 4 * qg + dl] = pos0 + 512 * qg + 128 * dl + np.arange(128)
        bidx.append(np.ascontiguousarray(bi))
    rm_h = P.hconst(f"c_rm3_{tag}", rm)
    bidx_h = P.hconst(f"c_bidx3_{tag}", bidx)
    bm = np.zeros((128, 4 * 512), np.float32)
    for dl in range(4):
        bm[:, 512 * dl:512 * dl + 512] = (128 * dl + np.arange(128)[:, None]
                                          <= np.arange(512)[None, :])
    import ml_dtypes as _mld
    bm_h = P.hconst(f"c_bm3_{tag}", bm.astype(_mld.bfloat16))

    K_sh = P.dram(f"Ksh3_{tag}", [512, TOK], BF16)
    Kt_sh = P.dram(f"Ktsh3_{tag}", [TOK, 512], BF16)
    V_sh = P.dram(f"Vsh3_{tag}", [TOK, 512], BF16)
    K_seq = P.dram(f"Kseq3_{tag}", [4 * 512, TOK], BF16)
    Kt_seq = P.dram(f"Ktseq3_{tag}", [4 * TOK, 512], BF16)
    V_seq = P.dram(f"Vseq3_{tag}", [4 * TOK, 512], BF16)
    km_in = P.dram(f"kmin3_{tag}", [1, 16], F32)
    km_out = P.dram(f"kmout3_{tag}", [1, 16], F32)
    wq_bf = P.dram(f"wqbf3_{tag}", [D, D], BF16)
    wkv_bf = P.dram(f"wkvbf3_{tag}", [D, 1024], BF16)
    wo_bf = P.dram(f"wobf3_{tag}", [D, D], BF16)

    x_t = x_in.ap().rearrange("(n p) d -> n p d", p=128)
    y_t = y_out.ap().rearrange("(n p) d -> n p d", p=128)

    with P.scope():
        sb_idb = P.sbuf("idb", [128, 128], BF16)
        xt = P.sbuf("xt", [128, D], F32)
        scr = P.sbuf("scr", [128, D], F32)
        stats = dict(negmu=P.sbuf("negmu", [128, 1], F32), rstd=P.sbuf("rstd", [128, 1], F32),
                     s1=P.sbuf("s1", [128, 1], F32), ssq=P.sbuf("ssq", [128, 1], F32))
        qT = P.sbuf("qT3", [128, 16 * 2048], BF16)
        onescolf = P.sbuf("onescolf3", [128, 1], F32)
        ones1f = P.sbuf("ones1f3", [1, 128], F32)
        onescol = P.sbuf("onescol3", [128, 1], BF16)
        qn16 = P.sbuf("qn163", [1, 16], F32)
        km16 = P.sbuf("km163", [1, 16], F32)
        negM = P.sbuf("negM3", [128, 16], F32)
        negMr = P.sbuf("negMr3", [128, 16 * 256], F32)
        rm_sb = P.sbuf("rmsb3", [128, 256], F32)
        bidx_sb = P.sbuf("bidx3", [128, 16], mybir.dt.int32)
        bm_sb = P.sbuf("bmsb3", [128, 2048], BF16)
        m1 = P.sbuf("m13", [128, 2], F32)
        ph0 = contextlib.ExitStack()
        wtile = ph0.enter_context(nc.sbuf_tensor(f"wtile3a_{P.uid()}", [128, 2 * 2048], BF16))

        with nc.Block() as b:
            @b.sync
            def _(sync):
                P.ld(sync, sb_idb[:], id128b.ap())
                P.ld(sync, rm_sb[:], rm_h.ap())
                P.ld(sync, bidx_sb[:], bidx_h.ap())
                P.ld(sync, bm_sb[:], bm_h.ap())
                P.ldwait(sync)
            @b.gpsimd
            def _(g):
                P.emit_consts(g)
                g.memset(onescolf[:], 1.0)
                g.memset(ones1f[:], 1.0)
                g.memset(onescol[:], 1.0)
                for r_ in range(16):
                    g.dma_start(wtile[:, 0:2048],
                                wq_h.ap()[128 * r_:128 * r_ + 128, :]).then_inc(P.dsem, 16)
                    g.dma_start(wtile[:, 2048:4096],
                                wo_h.ap()[128 * r_:128 * r_ + 128, :]).then_inc(P.dsem, 16)
                    P.dn += 2
                    g.wait_ge(P.dsem, 16 * P.dn)
                    g.dma_start(wq_bf.ap()[128 * r_:128 * r_ + 128, :],
                                wtile[:, 0:2048]).then_inc(P.dsem, 16)
                    g.dma_start(wo_bf.ap()[128 * r_:128 * r_ + 128, :],
                                wtile[:, 2048:4096]).then_inc(P.dsem, 16)
                    P.dn += 2
                    g.wait_ge(P.dsem, 16 * P.dn)
                    g.dma_start(wtile[:, 0:512],
                                wk_h.ap()[128 * r_:128 * r_ + 128, :]).then_inc(P.dsem, 16)
                    g.dma_start(wtile[:, 512:1024],
                                wv_h.ap()[128 * r_:128 * r_ + 128, :]).then_inc(P.dsem, 16)
                    P.dn += 2
                    g.wait_ge(P.dsem, 16 * P.dn)
                    g.dma_start(wkv_bf.ap()[128 * r_:128 * r_ + 128, :],
                                wtile[:, 0:1024]).then_inc(P.dsem, 16)
                    P.dn += 1
                    g.wait_ge(P.dsem, 16 * P.dn)

        ph0.close()
        ph1 = contextlib.ExitStack()
        P._scopes.append(ph1)
        wtile = ph1.enter_context(nc.sbuf_tensor(f"wtile3b_{P.uid()}", [128, 2 * 2048], BF16))
        zT = P.sbuf("zT3", [128, 16 * 2048], BF16)
        zb = P.sbuf("zb", [128, D], BF16)
        kevac = P.sbuf("kevac3", [128, 2048], BF16)
        sq2 = P.sbuf("sq23", [128, 2048], F32)
        psT = ph1.enter_context(nc.psum_tensor(f"psTl3_{P.uid()}", [128, 1024], BF16))
        for ti in range(NT):
            with nc.Block() as b:
                @b.sync
                def _(sync):
                    P.ld(sync, xt[:], x_t[ti])
                    P.ldwait(sync)
            ln_stats(P, None, None, xt[:], scr[:], stats, 1.0 / D, 1e-5)
            with nc.Block() as b:
                @b.vector
                def _(v):
                    v = P.wrap(v, "v")
                    v.tensor_scalar(zb[:], xt[:], stats["negmu"][:], stats["rstd"][:],
                                    ALU.add, ALU.mult)
            for hf in range(2):
                with nc.Block() as b:
                    @b.tensor
                    def _(t):
                        for k in range(8):
                            kk = 8 * hf + k
                            t.transpose(psT[:, 128 * k:128 * k + 128],
                                        zb[:, 128 * kk:128 * kk + 128], sb_idb[:])
                with nc.Block() as b:
                    @b.vector
                    def _(v):
                        v = P.wrap(v, "v")
                        for k in range(8):
                            kk = 8 * hf + k
                            v.tensor_copy(zT[:, 2048 * kk + 128 * ti:2048 * kk + 128 * ti + 128],
                                          psT[:, 128 * k:128 * k + 128])

        psP = [ph1.enter_context(nc.psum_tensor(f"psP3{i}_{P.uid()}", [128, 512], F32))
               for i in range(4)]
        psN = ph1.enter_context(nc.psum_tensor(f"psN3_{P.uid()}", [1, 512], F32))
        for oc in range(24):
            with nc.Block() as b:
                @b.sync
                def _(sync):
                    if oc < 16:
                        P.ld(sync, ap3(wtile, 0, [[4096, 128], [128, 16], [1, 128]]),
                             ap3(wq_bf, 128 * oc, [[2048, 128], [128 * 2048, 16], [1, 128]]))
                    else:
                        P.ld(sync, ap3(wtile, 0, [[4096, 128], [128, 16], [1, 128]]),
                             ap3(wkv_bf, 128 * (oc - 16), [[1024, 128], [128 * 1024, 16], [1, 128]]))
                    P.ldwait(sync)
                @b.tensor
                def _(t):
                    t.wait_ge(P.dsem, 16 * P.dn)
                    for k in range(16):
                        for tg in range(4):
                            t.matmul(psP[tg][:], wtile[:, 128 * k:128 * k + 128],
                                     zT[:, 2048 * k + 512 * tg:2048 * k + 512 * tg + 512],
                                     start=(k == 0), stop=(k == 15))
            with nc.Block() as b:
                @b.vector
                def _(v):
                    v = P.wrap(v, "v")
                    for tg in range(4):
                        if oc < 16:
                            v.tensor_copy(qT[:, 2048 * oc + 512 * tg:2048 * oc + 512 * tg + 512],
                                          psP[tg][:])
                        else:
                            v.tensor_copy(kevac[:, 512 * tg:512 * tg + 512], psP[tg][:])
            src = (qT[:, 2048 * oc:2048 * oc + 2048] if oc < 16 else kevac[:])
            if oc < 16 or oc - 16 < 4:
                with nc.Block() as b:
                    @b.scalar
                    def _(sc):
                        sc = P.wrap(sc, "s")
                        sc.activation(sq2[:], src, AF.Square)
                for tg in range(4):
                    with nc.Block() as b:
                        @b.tensor
                        def _(t):
                            t.matmul(psN[0:1, :], onescolf[:],
                                     sq2[:, 512 * tg:512 * tg + 512], start=True, stop=True)
                    with nc.Block() as b:
                        @b.vector
                        def _(v):
                            v = P.wrap(v, "v")
                            v.tensor_reduce(m1[0:1, 1:2], psN[0:1, :], AX.X, ALU.max)
                            if tg == 0:
                                v.tensor_copy(m1[0:1, 0:1], m1[0:1, 1:2])
                            else:
                                v.tensor_tensor(m1[0:1, 0:1], m1[0:1, 0:1], m1[0:1, 1:2],
                                                ALU.max)
                with nc.Block() as b:
                    @b.vector
                    def _(v):
                        v = P.wrap(v, "v")
                        dst = qn16 if oc < 16 else km16
                        if oc < 16:
                            v.tensor_copy(dst[0:1, oc:oc + 1], m1[0:1, 0:1])
                        else:
                            for g_ in range(4):
                                v.tensor_copy(dst[0:1, 4 * (oc - 16) + g_:4 * (oc - 16) + g_ + 1],
                                              m1[0:1, 0:1])
            if oc >= 16:
                kh = oc - 16
                if kh < 4:
                    with nc.Block() as b:
                        @b.sync
                        def _(sync):
                            P.ld(sync, K_sh.ap()[128 * kh:128 * kh + 128, :], kevac[:])
                            P.ldwait(sync)
                for hf in range(2):
                    with nc.Block() as b:
                        @b.tensor
                        def _(t):
                            for j in range(8):
                                jj = 8 * hf + j
                                t.transpose(psT[:, 128 * j:128 * j + 128],
                                            kevac[:, 128 * jj:128 * jj + 128], sb_idb[:])
                    with nc.Block() as b:
                        @b.vector
                        def _(v):
                            v = P.wrap(v, "v")
                            v.tensor_copy(zb[:, 1024 * hf:1024 * hf + 1024], psT[:])
                with nc.Block() as b:
                    @b.sync
                    def _(sync):
                        dstt = Kt_sh if kh < 4 else V_sh
                        P.ld(sync, ap3(dstt, 128 * (kh % 4), [[512, 128], [512 * 128, 16], [1, 128]]),
                             ap3(zb, 0, [[2048, 128], [128, 16], [1, 128]]))
                        P.ldwait(sync)
        with nc.Block() as b:
            @b.sync
            def _(sync):
                P.ld(sync, km_in.ap(), km16[0:1, :])
                P.ldwait(sync)
        for nm, a, o in [("agK3", K_sh, K_seq), ("agKt3", Kt_sh, Kt_seq), ("agV3", V_sh, V_seq)]:
            cs = P.sem(nm)
            with nc.Block() as b:
                @b.gpsimd
                def _(g):
                    g.collective_compute("AllGather", ALU.bypass, replica_groups=RG2x4,
                                         ins=[a.ap().opt()], outs=[o.ap().opt()]).then_inc(cs)
                    g.wait_ge(cs, 1)
        csM = P.sem("arM3")
        with nc.Block() as b:
            @b.gpsimd
            def _(g):
                g.collective_compute("AllReduce", ALU.max, replica_groups=RG2x4,
                                     ins=[km_in.ap().opt()], outs=[km_out.ap().opt()]).then_inc(csM)
                g.wait_ge(csM, 1)
        with nc.Block() as b:
            @b.sync
            def _(sync):
                P.ld(sync, km16[0:1, :], km_out.ap())
                P.ldwait(sync)
        psB = ph1.enter_context(nc.psum_tensor(f"psB3_{P.uid()}", [128, 16], F32))
        with nc.Block() as b:
            @b.scalar
            def _(sc):
                sc = P.wrap(sc, "s")
                sc.activation(qn16[0:1, :], qn16[0:1, :], AF.Sqrt)
                sc.activation(km16[0:1, :], km16[0:1, :], AF.Sqrt)
        with nc.Block() as b:
            @b.vector
            def _(v):
                v = P.wrap(v, "v")
                v.tensor_tensor(qn16[0:1, :], qn16[0:1, :], km16[0:1, :], ALU.mult)
                v.tensor_single_scalar(qn16[0:1, :], qn16[0:1, :], -SCL, ALU.mult)
        with nc.Block() as b:
            @b.tensor
            def _(t):
                t.matmul(psB[:], ones1f[0:1, :], qn16[0:1, :], start=True, stop=True)
        with nc.Block() as b:
            @b.vector
            def _(v):
                v = P.wrap(v, "v")
                v.tensor_copy(negM[:], psB[:])
                for h in range(16):
                    v.tensor_scalar(negMr[:, 256 * h:256 * h + 256], rm_sb[:],
                                    negM[:, h:h + 1], None, ALU.add)
        ph1.close()
        P._scopes.pop()

        ph2 = contextlib.ExitStack()
        P._scopes.append(ph2)
        vband = P.sbuf("vband3", [128, 16 * 512], BF16)
        kbfm = P.sbuf("kbfm3", [128, 16 * 512], BF16)
        phkb = contextlib.ExitStack()
        kband = phkb.enter_context(nc.sbuf_tensor(f"kband3_{P.uid()}", [128, 16 * 512], BF16))
        with nc.Block() as b:
            @b.gpsimd
            def _(g):
                for qg in range(4):
                    for dl in range(4):
                        i_ = 4 * qg + dl
                        g.indirect_dma_start(
                            kband[:, 512 * i_:512 * i_ + 512], None,
                            ap3(Kt_seq, 0, [[512, 8192], [1, 512]]),
                            IndirectOffsetOnAxis(bidx_sb[:, i_:i_ + 1], 0)).then_inc(P.dsem, 16)
                        P.dn += 1
                        g.indirect_dma_start(
                            vband[:, 512 * i_:512 * i_ + 512], None,
                            ap3(V_seq, 0, [[512, 8192], [1, 512]]),
                            IndirectOffsetOnAxis(bidx_sb[:, i_:i_ + 1], 0)).then_inc(P.dsem, 16)
                        P.dn += 1
                g.wait_ge(P.dsem, 16 * P.dn)
        psT2 = ph2.enter_context(nc.psum_tensor(f"psT23_{P.uid()}", [128, 1024], BF16))
        for half in range(2):
            with nc.Block() as b:
                @b.tensor
                def _(t):
                    for j in range(8):
                        i_ = 8 * half + j
                        kh = 0
                    for j in range(8):
                        pass
                    for j in range(8):
                        i_ = 8 * half + j
                        pass
            break
        for i_ in range(16):
            with nc.Block() as b:
                @b.tensor
                def _(t):
                    for kh in range(4):
                        t.transpose(psT2[:, 128 * kh:128 * kh + 128],
                                    kband[:, 512 * i_ + 128 * kh:512 * i_ + 128 * kh + 128],
                                    sb_idb[:])
            with nc.Block() as b:
                @b.vector
                def _(v):
                    v = P.wrap(v, "v")
                    v.tensor_copy(kbfm[:, 512 * i_:512 * i_ + 512], psT2[:, 0:512])
        phkb.close()

        psS = [ph2.enter_context(nc.psum_tensor(f"psS3{i}_{P.uid()}", [128, 512], F32))
               for i in range(2)]
        psO = [ph2.enter_context(nc.psum_tensor(f"psO3{i}_{P.uid()}", [128, 512], F32))
               for i in range(4)]
        psL = ph2.enter_context(nc.psum_tensor(f"psL3_{P.uid()}", [128, 512], F32))
        kvt = P.sbuf("kvt3", [128, 4 * 128], BF16)
        pT = [P.sbuf(f"pT3{i}", [128, 512], BF16) for i in range(2)]
        pTm = P.sbuf("pTm3", [128, 512], BF16)
        oT = P.sbuf("oT3", [128, 16 * 2048], BF16)
        linv = P.sbuf("linv3", [1, 512], F32)
        lbc = P.sbuf("lbc3", [128, 512], F32)
        tsem = P.sem("fl3_t")
        s2sem = P.sem("fl3_s")
        osem2 = P.sem("fl3_o")
        dsem2 = P.sem("fl3_d")
        NB = 64
        for h in range(16):
            kh = h // 4
            with nc.Block() as b:
                @b.sync
                def _(sync):
                    for bb in range(NB):
                        gb = NB * h + bb
                        if gb >= 2:
                            sync.wait_ge(osem2, 4 * (gb - 1))
                        rr, tl0 = bb // 16, 128 * (bb % 16)
                        sync.dma_start(kvt[:, 256 * (gb % 2):256 * (gb % 2) + 128],
                                       K_seq.ap()[512 * rr + 128 * kh:512 * rr + 128 * kh + 128,
                                                  tl0:tl0 + 128]).then_inc(dsem2, 16)
                        sync.dma_start(kvt[:, 256 * (gb % 2) + 128:256 * (gb % 2) + 256],
                                       V_seq.ap()[2048 * rr + tl0:2048 * rr + tl0 + 128,
                                                  128 * kh:128 * kh + 128]).then_inc(dsem2, 16)
                @b.tensor
                def _(t):
                    for bb in range(NB):
                        gb = NB * h + bb
                        t.wait_ge(dsem2, 32 * (gb + 1))
                        for qg in range(4):
                            i = 4 * gb + qg
                            if i >= 2:
                                t.wait_ge(s2sem, i - 1)
                            t.matmul(psS[i % 2][:], kvt[:, 256 * (gb % 2):256 * (gb % 2) + 128],
                                     qT[:, 2048 * h + 512 * qg:2048 * h + 512 * qg + 512],
                                     start=True, stop=True).then_inc(tsem, 1)
                        for qg in range(4):
                            i = 4 * gb + qg
                            t.wait_ge(s2sem, i + 1)
                            t.matmul(psO[qg][:], kvt[:, 256 * (gb % 2) + 128:256 * (gb % 2) + 256],
                                     pT[i % 2][:], start=(bb == 0), stop=False)
                            t.matmul(psL[32 * qg:32 * qg + 1, :], onescol[:], pT[i % 2][:],
                                     start=(bb == 0), stop=False,
                                     tile_position=(0, 32 * qg)).then_inc(osem2, 1)
                @b.scalar
                def _(sc):
                    for bb in range(NB):
                        gb = NB * h + bb
                        for qg in range(4):
                            i = 4 * gb + qg
                            sc.wait_ge(tsem, i + 1)
                            if i >= 2:
                                sc.wait_ge(osem2, i - 1)
                            sc.activation(pT[i % 2][:], psS[i % 2][:], AF.Exp,
                                          bias=negMr[:, 256 * h + 4 * bb + qg:256 * h + 4 * bb + qg + 1],
                                          scale=SCL).then_inc(s2sem, 1)
            with nc.Block() as b:
                @b.tensor
                def _(t):
                    pass
            for qg in range(4):
                for dl in range(4):
                    i_ = 4 * qg + dl
                    with nc.Block() as b:
                        @b.tensor
                        def _(t):
                            t.matmul(psS[0][:], kbfm[:, 512 * i_ + 128 * kh:512 * i_ + 128 * kh + 128],
                                     qT[:, 2048 * h + 512 * qg:2048 * h + 512 * qg + 512],
                                     start=True, stop=True)
                    with nc.Block() as b:
                        @b.scalar
                        def _(sc):
                            sc = P.wrap(sc, "s")
                            sc.activation(pTm[:], psS[0][:], AF.Exp,
                                          bias=negM[:, h:h + 1], scale=SCL)
                    with nc.Block() as b:
                        @b.vector
                        def _(v):
                            v = P.wrap(v, "v")
                            v.tensor_tensor(pTm[:], pTm[:], bm_sb[:, 512 * dl:512 * dl + 512],
                                            ALU.mult)
                    with nc.Block() as b:
                        @b.tensor
                        def _(t):
                            t.matmul(psO[qg][:],
                                     vband[:, 512 * i_ + 128 * kh:512 * i_ + 128 * kh + 128],
                                     pTm[:], start=False, stop=(dl == 3))
                            t.matmul(psL[32 * qg:32 * qg + 1, :], onescol[:], pTm[:],
                                     start=False, stop=(dl == 3), tile_position=(0, 32 * qg))
            for qg in range(4):
                with nc.Block() as b:
                    @b.vector
                    def _(v):
                        v = P.wrap(v, "v")
                        v.reciprocal(linv[0:1, :], psL[32 * qg:32 * qg + 1, :])
                with nc.Block() as b:
                    @b.tensor
                    def _(t):
                        t.matmul(psS[0][:], ones1f[0:1, :], linv[0:1, :],
                                 start=True, stop=True)
                with nc.Block() as b:
                    @b.vector
                    def _(v):
                        v = P.wrap(v, "v")
                        v.tensor_copy(lbc[:], psS[0][:])
                        v.tensor_tensor(oT[:, 2048 * h + 512 * qg:2048 * h + 512 * qg + 512],
                                        psO[qg][:], lbc[:], ALU.mult)
        ph2.close()
        P._scopes.pop()

        yTl = P.sbuf("yTl3", [128, 512], BF16)
        ytokl = [P.sbuf(f"ytokl3{j}", [128, 2048], BF16) for j in range(4)]
        ph4 = contextlib.ExitStack()
        wtile = ph4.enter_context(nc.sbuf_tensor(f"wtile3c_{P.uid()}", [128, 2 * 2048], BF16))
        psY4 = ph4.enter_context(nc.psum_tensor(f"psY43_{P.uid()}", [128, 512], F32))
        psT4 = ph4.enter_context(nc.psum_tensor(f"psT43_{P.uid()}", [128, 1024], BF16))
        for tg in range(4):
            for dc in range(16):
                with nc.Block() as b:
                    @b.sync
                    def _(sync):
                        P.ld(sync, ap3(wtile, 0, [[4096, 128], [128, 16], [1, 128]]),
                             ap3(wo_bf, 128 * dc, [[2048, 128], [128 * 2048, 16], [1, 128]]))
                        P.ldwait(sync)
                    @b.tensor
                    def _(t):
                        t.wait_ge(P.dsem, 16 * P.dn)
                        for e in range(16):
                            t.matmul(psY4[:], wtile[:, 128 * e:128 * e + 128],
                                     oT[:, 2048 * e + 512 * tg:2048 * e + 512 * tg + 512],
                                     start=(e == 0), stop=(e == 15))
                with nc.Block() as b:
                    @b.vector
                    def _(v):
                        v = P.wrap(v, "v")
                        v.tensor_copy(yTl[:], psY4[:])
                with nc.Block() as b:
                    @b.tensor
                    def _(t):
                        for j in range(4):
                            t.transpose(psT4[:, 128 * j:128 * j + 128],
                                        yTl[:, 128 * j:128 * j + 128], sb_idb[:])
                with nc.Block() as b:
                    @b.vector
                    def _(v):
                        v = P.wrap(v, "v")
                        for j in range(4):
                            v.tensor_copy(ytokl[j][:, 128 * dc:128 * dc + 128],
                                          psT4[:, 128 * j:128 * j + 128])
            for j in range(4):
                ti = 4 * tg + j
                with nc.Block() as b:
                    @b.sync
                    def _(sync):
                        P.ld(sync, xt[:], x_t[ti])
                        P.ldwait(sync)
                with nc.Block() as b:
                    @b.vector
                    def _(v):
                        v = P.wrap(v, "v")
                        v.tensor_tensor(scr[:], xt[:], ytokl[j][:], ALU.add)
                with nc.Block() as b:
                    @b.sync
                    def _(sync):
                        P.st(sync, y_t[ti], scr[:])
                        P.stwait(sync)
        ph4.close()



def _emit_lut_tile(P, v, idx_i32, idxf, lutbc, wout, wout_fs, W, w0=0):
    v.tensor_copy(idxf[:, 0:W], idx_i32)
    msk = [P._lut_msk[j] for j in range(3)]
    acc = [P._lut_acc[j] for j in range(3)]
    for w in range(6):
        gs = [g for g in (3 * w, 3 * w + 1, 3 * w + 2) if g < 16]
        for g in gs:
            v.tensor_single_scalar(msk[g % 3][:, 0:W], idxf[:, 0:W], float(g), ALU.is_equal)
        for g in gs:
            ln = g % 3
            if g < 3:
                v.tensor_scalar(acc[ln][:, 0:W], msk[ln][:, 0:W], lutbc[:, g:g + 1],
                                None, ALU.mult)
            else:
                v.scalar_tensor_tensor(acc[ln][:, 0:W], msk[ln][:, 0:W],
                                       lutbc[:, g:g + 1], acc[ln][:, 0:W],
                                       ALU.mult, ALU.add)
    v.tensor_tensor(msk[0][:, 0:W], acc[0][:, 0:W], acc[1][:, 0:W], ALU.add)
    v.tensor_tensor(ap3(wout, w0, [[wout_fs, 128], [1, W]]), msk[0][:, 0:W],
                    acc[2][:, 0:W], ALU.add)


def build_l3ffn(P, x_in, y_out, wnp, tag):
    nc = P.nc
    import ml_dtypes
    I3 = 5504
    NHC = 43
    id128b = P.hconst("c_id128b", host_identity(128, np.float32).astype(ml_dtypes.bfloat16))
    P.need_consts([-1.0 / D, D * 1e-5, 0.5 * math.log(D), 0.5] +
                  [float(g) for g in range(2, 16)])

    dq = "l3w" not in P.cache
    if dq:
        gi_h = P.hconst("w3_gi", [np.ascontiguousarray(wnp["gate_idx"][:, 256 * c:256 * c + 256])
                                  for c in range(8)])
        ui_h = P.hconst("w3_ui", [np.ascontiguousarray(wnp["up_idx"][:, 256 * c:256 * c + 256])
                                  for c in range(8)])
        di_h = P.hconst("w3_di", [np.ascontiguousarray(wnp["down_idx"][:, 688 * c:688 * c + 688])
                                  for c in range(8)])
        lut_h = P.hconst("w3_lut", np.ascontiguousarray(
            np.broadcast_to(wnp["lut"].astype(np.float32)[None, :], (128, 16))))
        w_gT = P.dram("l3_wgT", [D, I3], BF16)
        w_uT = P.dram("l3_wuT", [D, I3], BF16)
        w_dT = P.dram("l3_wdT", [I3, D], BF16)
        gT_sh = P.dram("l3_gTsh", [256, I3], BF16)
        uT_sh = P.dram("l3_uTsh", [256, I3], BF16)
        dT_sh = P.dram("l3_dTsh", [688, D], BF16)
        P.cache["l3w"] = (w_gT, w_uT, w_dT)
    else:
        w_gT, w_uT, w_dT = P.cache["l3w"]

    x_t = x_in.ap().rearrange("(n p) d -> n p d", p=128)
    y_t = y_out.ap().rearrange("(n p) d -> n p d", p=128)

    with P.scope():
        sb_idb = P.sbuf("idb", [128, 128], BF16)
        with nc.Block() as b:
            @b.sync
            def _(sync):
                P.ld(sync, sb_idb[:], id128b.ap())
                P.ldwait(sync)
            @b.gpsimd
            def _(g):
                P.emit_consts(g)

        if dq:
            phDQ3 = contextlib.ExitStack()
            P._scopes.append(phDQ3)
            lutbc = P.sbuf("lutbc", [128, 16], F32)
            pk3 = P.sbuf("pk3", [128, 2 * 688], mybir.dt.int32)
            idxf = P.sbuf("idxf3", [128, 688], F32)
            P._lut_msk = [P.sbuf(f"lmsk{j}", [128, 688], F32) for j in range(3)]
            P._lut_acc = [P.sbuf(f"lacc{j}", [128, 688], F32) for j in range(3)]
            wde3 = P.sbuf("wde3", [128, 2 * 688], BF16)
            wtr3 = P.sbuf("wtr3", [128, 2 * 768], BF16)
            phW3 = contextlib.ExitStack()
            psW3 = [phW3.enter_context(nc.psum_tensor(f"psW3{i}_{P.uid()}", [128, 1024], BF16))
                    for i in range(2)]
            with nc.Block() as b:
                @b.sync
                def _(sync):
                    P.ld(sync, lutbc[:], lut_h.ap())
                    P.ldwait(sync)
            l3cnt = {}

            def tile_loop3(ntiles, W, src_h, tr_tile, cp_tile, st_tile):
                def ld_t(sync, i):
                    pp = i % 2
                    P.ld(sync, pk3[:, 688 * pp:688 * pp + W],
                         src_h.ap()[128 * i:128 * i + 128, :])
                    l3cnt[i] = P.dn
                with nc.Block() as b:
                    @b.sync
                    def _(sync):
                        ld_t(sync, 0)
                for i in range(ntiles + 3):
                    with nc.Block() as b:
                        @b.sync
                        def _(sync):
                            if i + 1 <= ntiles - 1:
                                ld_t(sync, i + 1)
                            if 3 <= i:
                                st_tile(sync, i - 3)
                        @b.vector
                        def _(v):
                            v = P.wrap(v, "v")
                            if i <= ntiles - 1:
                                v.wait_ge(P.dsem, 16 * l3cnt[i])
                                pp = i % 2
                                _emit_lut_tile(P, v, pk3[:, 688 * pp:688 * pp + W],
                                               idxf, lutbc, wde3, 1376, W, w0=688 * pp)
                            if 2 <= i:
                                cp_tile(v, (i - 2) % 2)
                        @b.tensor
                        def _(t):
                            if 1 <= i <= ntiles:
                                tr_tile(t, (i - 1) % 2)

            def tr_gu3(t, pp):
                t.transpose(psW3[pp][:, 0:128], wde3[:, 688 * pp:688 * pp + 128], sb_idb[:])
                t.transpose(psW3[pp][:, 128:256], wde3[:, 688 * pp + 128:688 * pp + 256],
                            sb_idb[:])

            def cp_gu3(v, pp):
                v.tensor_copy(wtr3[:, 768 * pp:768 * pp + 256], psW3[pp][:, 0:256])

            for mat, src_h, sh in [("g", gi_h, gT_sh), ("u", ui_h, uT_sh)]:
                def st_gu3(sync, i, sh=sh):
                    pp = i % 2
                    P.ld(sync, ap3(sh, 128 * i, [[I3, 128], [128 * I3, 2], [1, 128]]),
                         ap3(wtr3, 768 * pp, [[1536, 128], [128, 2], [1, 128]]))
                tile_loop3(NHC, 256, src_h, tr_gu3, cp_gu3, st_gu3)

            def tr_dn3(t, pp):
                for blk in range(5):
                    t.transpose(psW3[pp][:, 128 * blk:128 * blk + 128],
                                wde3[:, 688 * pp + 128 * blk:688 * pp + 128 * blk + 128],
                                sb_idb[:])
                t.transpose(psW3[pp][0:48, 640:768], wde3[:, 688 * pp + 640:688 * pp + 688],
                            sb_idb[:])

            def cp_dn3(v, pp):
                v.tensor_copy(wtr3[:, 768 * pp:768 * pp + 640], psW3[pp][:, 0:640])
                v.tensor_copy(wtr3[0:48, 768 * pp + 640:768 * pp + 768],
                              psW3[pp][0:48, 640:768])

            def st_dn3(sync, i):
                pp = i % 2
                P.ld(sync, ap3(dT_sh, 128 * i, [[2048, 128], [128 * 2048, 5], [1, 128]]),
                     ap3(wtr3, 768 * pp, [[1536, 128], [128, 5], [1, 128]]))
                P.ld(sync, ap3(dT_sh, 5 * 128 * 2048 + 128 * i, [[2048, 48], [1, 128]]),
                     ap3(wtr3, 768 * pp + 640, [[1536, 48], [1, 128]]))

            tile_loop3(16, 688, di_h, tr_dn3, cp_dn3, st_dn3)
            for nm, a, o in [("ag3g", gT_sh, w_gT), ("ag3u", uT_sh, w_uT), ("ag3d", dT_sh, w_dT)]:
                cs = P.sem(nm)
                with nc.Block() as b:
                    @b.gpsimd
                    def _(g):
                        g.collective_compute("AllGather", ALU.bypass, replica_groups=RG8,
                                             ins=[a.ap().opt()], outs=[o.ap().opt()]).then_inc(cs)
                        g.wait_ge(cs, 1)
            phW3.close()
            P._scopes.pop()
            phDQ3.close()

        xt = P.sbuf("xt", [128, D], F32)
        scr = P.sbuf("scr", [128, D], F32)
        zb = P.sbuf("zb", [128, D], BF16)
        stats = dict(negmu=P.sbuf("negmu", [128, 1], F32), rstd=P.sbuf("rstd", [128, 1], F32),
                     s1=P.sbuf("s1", [128, 1], F32), ssq=P.sbuf("ssq", [128, 1], F32))
        zT_half = P.sbuf("zTh3", [128, 16 * 1024], BF16)
        hT_half = P.sbuf("hTh3", [128, NHC * 1024], BF16)
        y0T = zT_half
        wg_sb = P.sbuf("wg3", [128, 2 * 16 * 128], BF16)
        wu_sb = P.sbuf("wu3", [128, 2 * 16 * 128], BF16)
        wd_sb = P.sbuf("wd3", [128, 2 * NHC * 128], BF16)
        sg0 = P.sbuf("sg03", [128, 512], F32)
        sg1 = P.sbuf("sg13", [128, 512], F32)
        y0tok = P.sbuf("y0tok3", [128, D], BF16)
        sil = P.sem(f"sil_{tag}")
        nsil = [0]
        for half in range(2):
            phA = contextlib.ExitStack()
            psT2 = phA.enter_context(nc.psum_tensor(f"psT2a3_{P.uid()}", [128, 1024], BF16))
            for j in range(8):
                ti = 8 * half + j
                with nc.Block() as b:
                    @b.sync
                    def _(sync):
                        P.ld(sync, xt[:], x_t[ti])
                        P.ldwait(sync)
                ln_stats(P, None, None, xt[:], scr[:], stats, 1.0 / D, 1e-5)
                with nc.Block() as b:
                    @b.vector
                    def _(v):
                        v = P.wrap(v, "v")
                        v.tensor_scalar(zb[:], xt[:], stats["negmu"][:], stats["rstd"][:],
                                        ALU.add, ALU.mult)
                for hf in range(2):
                    with nc.Block() as b:
                        @b.tensor
                        def _(t):
                            for k in range(8):
                                kk = 8 * hf + k
                                t.transpose(psT2[:, 128 * k:128 * k + 128],
                                            zb[:, 128 * kk:128 * kk + 128], sb_idb[:])
                    with nc.Block() as b:
                        @b.vector
                        def _(v):
                            v = P.wrap(v, "v")
                            for k in range(8):
                                kk = 8 * hf + k
                                v.tensor_copy(zT_half[:, 1024 * kk + 128 * j:1024 * kk + 128 * j + 128],
                                              psT2[:, 128 * k:128 * k + 128])
            phA.close()
            phB = contextlib.ExitStack()
            psG = [phB.enter_context(nc.psum_tensor(f"psG3{i}_{P.uid()}", [128, 512], F32))
                   for i in range(4)]
            psU2 = [phB.enter_context(nc.psum_tensor(f"psU23{i}_{P.uid()}", [128, 512], F32))
                    for i in range(4)]
            ldcnt = {}

            def ld_gu(sync, hc):
                pp = hc % 2
                P.ld(sync, ap3(wg_sb, 2048 * pp, [[4096, 128], [128, 16], [1, 128]]),
                     ap3(w_gT, 128 * hc, [[I3, 128], [128 * I3, 16], [1, 128]]))
                P.ld(sync, ap3(wu_sb, 2048 * pp, [[4096, 128], [128, 16], [1, 128]]),
                     ap3(w_uT, 128 * hc, [[I3, 128], [128 * I3, 16], [1, 128]]))
                ldcnt[hc] = P.dn
            with nc.Block() as b:
                @b.sync
                def _(sync):
                    ld_gu(sync, 0)
            for hc in range(NHC + 1):
                with nc.Block() as b:
                    @b.sync
                    def _(sync):
                        if hc + 1 <= NHC - 1:
                            ld_gu(sync, hc + 1)
                    @b.tensor
                    def _(t):
                        if hc <= NHC - 1:
                            pp = hc % 2
                            t.wait_ge(P.dsem, 16 * ldcnt[hc])
                            for k in range(16):
                                for tg in range(2):
                                    t.matmul(psG[2 * pp + tg][:],
                                             wg_sb[:, 2048 * pp + 128 * k:2048 * pp + 128 * k + 128],
                                             zT_half[:, 1024 * k + 512 * tg:1024 * k + 512 * tg + 512],
                                             start=(k == 0), stop=(k == 15))
                                for tg in range(2):
                                    t.matmul(psU2[2 * pp + tg][:],
                                             wu_sb[:, 2048 * pp + 128 * k:2048 * pp + 128 * k + 128],
                                             zT_half[:, 1024 * k + 512 * tg:1024 * k + 512 * tg + 512],
                                             start=(k == 0), stop=(k == 15))
                    @b.scalar
                    def _(sc):
                        if hc >= 1:
                            pq = (hc - 1) % 2
                            sc.activation(sg0[:], psG[2 * pq][:], AF.Silu)
                            sc.activation(sg1[:], psG[2 * pq + 1][:], AF.Silu).then_inc(sil, 1)
                            nsil[0] += 1
                    @b.vector
                    def _(v):
                        if hc >= 1:
                            pq = (hc - 1) % 2
                            v.wait_ge(sil, nsil[0])
                            v.tensor_tensor(hT_half[:, 1024 * (hc - 1):1024 * (hc - 1) + 512],
                                            sg0[:], psU2[2 * pq][:], ALU.mult)
                            v.tensor_tensor(hT_half[:, 1024 * (hc - 1) + 512:1024 * (hc - 1) + 1024],
                                            sg1[:], psU2[2 * pq + 1][:], ALU.mult)
            phB.close()
            phC = contextlib.ExitStack()
            psY = [phC.enter_context(nc.psum_tensor(f"psY3f{i}_{P.uid()}", [128, 512], F32))
                   for i in range(4)]
            ldcnt2 = {}

            def ld_dn(sync, dcs):
                pp = dcs % 2
                P.ld(sync, ap3(wd_sb, NHC * 128 * pp, [[2 * NHC * 128, 128], [128, NHC], [1, 128]]),
                     ap3(w_dT, 128 * dcs, [[2048, 128], [128 * 2048, NHC], [1, 128]]))
                ldcnt2[dcs] = P.dn
            with nc.Block() as b:
                @b.sync
                def _(sync):
                    ld_dn(sync, 0)
            for dcs in range(17):
                with nc.Block() as b:
                    @b.sync
                    def _(sync):
                        if dcs + 1 <= 15:
                            ld_dn(sync, dcs + 1)
                    @b.tensor
                    def _(t):
                        if dcs <= 15:
                            pp = dcs % 2
                            t.wait_ge(P.dsem, 16 * ldcnt2[dcs])
                            for ic in range(NHC):
                                for tg in range(2):
                                    t.matmul(psY[2 * pp + tg][:],
                                             wd_sb[:, NHC * 128 * pp + 128 * ic:NHC * 128 * pp + 128 * ic + 128],
                                             hT_half[:, 1024 * ic + 512 * tg:1024 * ic + 512 * tg + 512],
                                             start=(ic == 0), stop=(ic == NHC - 1))
                    @b.vector
                    def _(v):
                        if dcs >= 1:
                            pq = (dcs - 1) % 2
                            dc0 = dcs - 1
                            v.tensor_copy(y0T[:, 1024 * dc0:1024 * dc0 + 512], psY[2 * pq][:])
                            v.tensor_copy(y0T[:, 1024 * dc0 + 512:1024 * dc0 + 1024],
                                          psY[2 * pq + 1][:])
            phC.close()
            phD = contextlib.ExitStack()
            psT2 = phD.enter_context(nc.psum_tensor(f"psT2d3_{P.uid()}", [128, 1024], BF16))
            for j in range(8):
                ti = 8 * half + j
                for hf in range(2):
                    with nc.Block() as b:
                        @b.tensor
                        def _(t):
                            for k in range(8):
                                dc = 8 * hf + k
                                t.transpose(psT2[:, 128 * k:128 * k + 128],
                                            y0T[:, 1024 * dc + 128 * j:1024 * dc + 128 * j + 128],
                                            sb_idb[:])
                    with nc.Block() as b:
                        @b.vector
                        def _(v):
                            v = P.wrap(v, "v")
                            for k in range(8):
                                dc = 8 * hf + k
                                v.tensor_copy(y0tok[:, 128 * dc:128 * dc + 128],
                                              psT2[:, 128 * k:128 * k + 128])
                with nc.Block() as b:
                    @b.sync
                    def _(sync):
                        P.ld(sync, xt[:], x_t[ti])
                        P.ldwait(sync)
                with nc.Block() as b:
                    @b.vector
                    def _(v):
                        v = P.wrap(v, "v")
                        v.scalar_tensor_tensor(scr[:], y0tok[:], 0.5, xt[:], ALU.mult, ALU.add)
                with nc.Block() as b:
                    @b.sync
                    def _(sync):
                        P.st(sync, y_t[ti], scr[:])
                        P.stwait(sync)
            phD.close()



def kernel(**inputs):
    from concourse.bass_utils import run_bass_kernel_spmd
    x = np.ascontiguousarray(np.asarray(inputs["L0_attn_x"], np.float32).reshape(16384, 2048))
    P = Prog()
    x_in = P.dram("x", [TOK, D], F32, kind="ExternalInput")
    y_out = P.dram("y", [TOK, D], F32, kind="ExternalOutput")
    xa = P.dram("xa", [TOK, D], F32)
    xb = P.dram("xb", [TOK, D], F32)
    UNION = ([-1.0 / D, D * 1e-5, 0.5 * math.log(D), 0.5, -1.0, 100.0, 1e-8,
              1.0 - 1e-6, 16.0, -BIG, BIG, 2.0, 8.0, -8.0, -2.0, D * 1e-6,
              1.0 / math.sqrt(128.0), -1.0 / math.sqrt(128.0)] +
             [float(g) for g in range(2, 16)])
    P.need_consts(UNION)
    with P.nc.Block() as b:
        @b.gpsimd
        def _(g):
            P.emit_consts(g)
    w_l0f = dict(ln_gamma=inputs["L0_ffn_ln_gamma"], ln_beta=inputs["L0_ffn_ln_beta"],
                 bases=inputs["L0_ffn_spline_bases"], slopes=inputs["L0_ffn_spline_slopes"],
                 oscale=inputs["L0_ffn_output_scale"])
    w_l0a = dict(W_in=inputs["L0_attn_W_in"], b_in=inputs["L0_attn_b_in"],
                 W_out=inputs["L0_attn_W_out"], b_out=inputs["L0_attn_b_out"],
                 gamma=inputs["L0_attn_gamma"], beta=inputs["L0_attn_beta"])
    w_l1f = dict(gate_up_packed=inputs["L1_ffn_gate_up_packed"],
                 gate_up_scales=inputs["L1_ffn_gate_up_scales"],
                 down_packed=inputs["L1_ffn_down_packed"],
                 down_scales=inputs["L1_ffn_down_scales"])
    w_l1a = dict(w_qkv=inputs["L1_attn_w_qkv"], b_qkv=inputs["L1_attn_b_qkv"],
                 w_out=inputs["L1_attn_w_out"], b_out=inputs["L1_attn_b_out"])
    w_l2f = dict(select_weight=inputs["L2_ffn_select_weight"],
                 select_mean_diff=inputs["L2_ffn_select_mean_diff"],
                 imp_center=inputs["L2_ffn_imp_center"], imp_slope=inputs["L2_ffn_imp_slope"],
                 lora_update=inputs["L2_ffn_lora_update"], lora_degen=inputs["L2_ffn_lora_degen"],
                 bias_w=inputs["L2_ffn_bias_w"], debias_w=inputs["L2_ffn_debias_w"])
    w_l2a = dict(wq=inputs["L2_attn_wq"], wk=inputs["L2_attn_wk"],
                 wv=inputs["L2_attn_wv"], wo=inputs["L2_attn_wo"])
    w_l3f = dict(lut=inputs["L3_ffn_lut"], gate_idx=inputs["L3_ffn_gate_idx"],
                 up_idx=inputs["L3_ffn_up_idx"], down_idx=inputs["L3_ffn_down_idx"])
    w_l3a = dict(Wq=inputs["L3_attn_Wq"], Wk=inputs["L3_attn_Wk"],
                 Wv=inputs["L3_attn_Wv"], Wo=inputs["L3_attn_Wo"])
    seq = [
        (build_l0ffn, w_l0f, "l0f", x_in, xa), (build_l0attn, w_l0a, "l0a", xa, xb),
        (build_l0ffn, w_l0f, "l0f2", xb, xa), (build_l1ffn, w_l1f, "l1f", xa, xb),
        (build_l1attn, w_l1a, "l1a", xb, xa), (build_l1ffn, w_l1f, "l1f2", xa, xb),
        (build_l2ffn, w_l2f, "l2f", xb, xa), (build_l2attn, w_l2a, "l2a", xa, xb),
        (build_l2ffn, w_l2f, "l2f2", xb, xa), (build_l3ffn, w_l3f, "l3f", xa, xb),
        (build_l3attn, w_l3a, "l3a", xb, xa), (build_l3ffn, w_l3f, "l3f2", xa, y_out),
    ]
    for fn, w, tg, xi, xo in seq:
        with P.scope():
            fn(P, xi, xo, w, tg)
    finish(P)
    maps = in_maps_for(P, [{"x": np.ascontiguousarray(x[2048 * c:2048 * c + 2048])}
                           for c in range(8)])
    res = run_bass_kernel_spmd(P.nc, maps, core_ids=list(range(8)))
    out = np.concatenate([np.asarray(res.results[c]["y"]) for c in range(8)], axis=0)
    return out.reshape(2, 8192, 2048).astype(np.float32)


# version 07 (final)
# speedup vs baseline: 1.0248x; 1.0248x over previous
import contextlib
import math
import numpy as np
import concourse.bass as bass
import concourse.mybir as mybir
from concourse.bass_core_rust import IndirectOffsetOnAxis

F32 = mybir.dt.float32
BF16 = mybir.dt.bfloat16
U32 = mybir.dt.uint32
AF = mybir.ActivationFunctionType
ALU = mybir.AluOpType
AX = mybir.AxisListType
RG8 = [[0, 1, 2, 3, 4, 5, 6, 7]]

D = 2048
TOK = 2048
NT = TOK // 128
NSEQ = 2048 * 8
BIG = 1.0e30


class Prog:
    def __init__(self):
        self.nc = bass.Bass("TRN2", target_bir_lowering=False)
        self.stack = contextlib.ExitStack()
        self.stack.__enter__()
        self.host_consts = {}
        self.cache = {}
        self._cvals = {}
        self._cpending = []
        self._uid = 0
        self.dn = 0
        self.on = 0
        self.cn = 0
        self.dsem = self.sem("dsem_g")
        self.osem = self.sem("osem_g")
        self.csem = self.sem("csem_g")

    def uid(self):
        self._uid += 1
        return self._uid

    def _cur(self):
        return self._scopes[-1] if getattr(self, "_scopes", None) else self.stack

    def sbuf(self, name, shape, dtype):
        return self._cur().enter_context(self.nc.sbuf_tensor(f"{name}_{self.uid()}", shape, dtype))

    def psum(self, name, shape, dtype=F32):
        return self._cur().enter_context(self.nc.psum_tensor(f"{name}_{self.uid()}", shape, dtype))

    def sem(self, name):
        return self.stack.enter_context(self.nc.semaphore(f"{name}_{self.uid()}"))

    @contextlib.contextmanager
    def scope(self):
        if not hasattr(self, "_scopes"):
            self._scopes = []
        st = contextlib.ExitStack()
        self._scopes.append(st)
        try:
            with st:
                yield
        finally:
            self._scopes.pop()

    def dram(self, name, shape, dtype, kind="Internal"):
        return self.nc.dram_tensor(name, shape, dtype, kind=kind)

    def hconst(self, name, arr):
        if name in self.host_consts:
            return self._hconst_handles[name]
        a0 = arr[0] if isinstance(arr, list) else arr
        h = self.nc.dram_tensor(name, list(a0.shape), np_to_dt(a0.dtype), kind="ExternalInput")
        self.host_consts[name] = arr
        if not hasattr(self, "_hconst_handles"):
            self._hconst_handles = {}
        self._hconst_handles[name] = h
        return h

    def need_consts(self, vals):
        for v in vals:
            v = float(v)
            if v in (0.0, 1.0) or v in self._cvals:
                continue
            t = self.sbuf("constap", [128, 1], F32)
            self._cvals[v] = t
            self._cpending.append((t, v))
            self.nc.const_aps.register(F32, v, t[:])

    def emit_consts(self, g):
        for t, v in self._cpending:
            g.memset(t[:], v)
        self._cpending = []

    def dump(self, name, ap_):
        if not getattr(self, "dbg", False):
            return
        t = ap_.tensor if hasattr(ap_, "tensor") else ap_
        shape = list(ap_.shape) if hasattr(ap_, "shape") else None
        nc = self.nc
        pat = ap_.ap if hasattr(ap_, "ap") else None
        dims = [p[1] for p in pat] if pat is not None else shape
        o = nc.dram_tensor(f"dbg_{name}", dims, ap_.dtype, kind="ExternalOutput")
        with nc.Block() as b:
            @b.sync
            def _(sync):
                self.st(sync, o.ap(), ap_)
                self.stwait(sync)

    def ld(self, eng, dst, src):
        eng.dma_start(dst, src).then_inc(self.dsem, 16)
        self.dn += 1

    def ldwait(self, eng):
        eng.wait_ge(self.dsem, 16 * self.dn)

    def st(self, eng, dst, src):
        eng.dma_start(dst, src).then_inc(self.osem, 16)
        self.on += 1

    def stwait(self, eng):
        eng.wait_ge(self.osem, 16 * self.on)

    def wrap(self, eng, key):
        if not hasattr(self, "_ew"):
            self._ew = {}
        if key not in self._ew:
            self._ew[key] = EW(self, key)
        w = self._ew[key]
        w.bind(eng)
        return w


GUARD_DIST = 2


def _ap_name(a):
    if a is None or isinstance(a, (int, float)):
        return None
    t = getattr(a, "tensor", None)
    if t is None:
        t = a
    return getattr(t, "name", None)


class EW:
    OPS = {
        "tensor_tensor":        (lambda a: ([a[1], a[2]], [a[0]])),
        "tensor_scalar":        (lambda a: ([a[1], a[2], a[3]], [a[0]])),
        "tensor_single_scalar": (lambda a: ([a[1], a[2]], [a[0]])),
        "scalar_tensor_tensor": (lambda a: ([a[1], a[2], a[3]], [a[0]])),
        "tensor_tensor_scan":   (lambda a: ([a[1], a[2], a[3]], [a[0]])),
        "tensor_copy":          (lambda a: ([a[1]], [a[0]])),
        "tensor_reduce":        (lambda a: ([a[1]], [a[0]])),
        "reciprocal":           (lambda a: ([a[1]], [a[0]])),
        "max":                  (lambda a: ([a[1]], [a[0]])),
        "memset":               (lambda a: ([], [a[0]])),
        "iota":                 (lambda a: ([], [a[0]])),
        "transpose":            (lambda a: ([a[1]], [a[0]])),
    }

    def __init__(self, P, key):
        self.P = P
        self.key = key
        self.sem = P.sem(f"selfsem_{key}")
        self.cnt = 0
        self.idx = 0
        self.lastw = {}
        self.lastr = {}
        self.pend = {}
        self.inced = {}
        self.eng = None

    def bind(self, eng):
        self.eng = eng
        self.idx += GUARD_DIST + 1

    def _guard(self, reads, writes):
        hazard_idx = -10**9
        for n in reads:
            if n in self.lastw:
                hazard_idx = max(hazard_idx, self.lastw[n])
        for n in writes:
            if n in self.lastw:
                hazard_idx = max(hazard_idx, self.lastw[n])
            if n in self.lastr:
                hazard_idx = max(hazard_idx, self.lastr[n])
        if hazard_idx >= self.idx - GUARD_DIST and hazard_idx in self.pend:
            if hazard_idx in self.inced:
                self.eng.wait_ge(self.sem, self.inced[hazard_idx])
            else:
                self.cnt += 1
                self.pend[hazard_idx].then_inc(self.sem, 1)
                self.inced[hazard_idx] = self.cnt
                self.eng.wait_ge(self.sem, self.cnt)

    def _emit(self, name, args, kwargs):
        get = self.OPS[name]
        rina, wouta = get(list(args) + [None] * 8)
        reads = [x for x in (_ap_name(a) for a in rina) if x]
        writes = [x for x in (_ap_name(a) for a in wouta) if x]
        for k in ("bias", "scale", "accum_out", "identity"):
            if k in kwargs:
                n = _ap_name(kwargs[k])
                if n:
                    (writes if k == "accum_out" else reads).append(n)
        self._guard(reads, writes)
        ins = getattr(self.eng, name)(*args, **kwargs)
        self.pend[self.idx] = ins
        for n in writes:
            self.lastw[n] = self.idx
        for n in reads:
            self.lastr[n] = self.idx
        self.idx += 1
        return ins

    def activation(self, *args, **kwargs):
        a = list(args)
        reads = [_ap_name(a[1])]
        writes = [_ap_name(a[0])]
        for k, v in list(kwargs.items()) + list(zip(["bias", "scale"], a[3:5])):
            n = _ap_name(v)
            if n:
                (writes if k == "accum_out" else reads).append(n)
        reads = [x for x in reads if x]; writes = [x for x in writes if x]
        self._guard(reads, writes)
        ins = self.eng.activation(*args, **kwargs)
        self.pend[self.idx] = ins
        for n in writes: self.lastw[n] = self.idx
        for n in reads: self.lastr[n] = self.idx
        self.idx += 1
        return ins

    def wait_ge(self, *a):
        return self.eng.wait_ge(*a)

    def dma_start(self, *a, **k):
        return self.eng.dma_start(*a, **k)

    def __getattr__(self, name):
        if name in EW.OPS:
            return lambda *args, **kwargs: self._emit(name, args, kwargs)
        raise AttributeError(name)


def np_to_dt(dtype):
    import ml_dtypes
    dtype = np.dtype(dtype)
    if dtype == np.float32:
        return F32
    if dtype == ml_dtypes.bfloat16:
        return BF16
    if dtype == np.int32:
        return mybir.dt.int32
    if dtype == np.uint32:
        return U32
    if dtype == np.float16:
        return mybir.dt.float16
    raise ValueError(dtype)


def ap3(t, offset, dims):
    return bass.AP(t, offset, dims)



def v_argmax_onehot(v, oh_out, x_ap, n, scratch):
    m, dd, pfx, ones = scratch["m"], scratch["d"], scratch["pfx"], scratch["ones"]
    v.tensor_reduce(m[:], x_ap, AX.X, ALU.max)
    v.tensor_scalar(dd[:], x_ap, m[:], None, ALU.subtract)
    v.tensor_scalar(dd[:], dd[:], 0.0, None, ALU.is_equal)
    v.tensor_tensor_scan(pfx[:], ones[:], dd[:], 0.0, ALU.mult, ALU.add)
    v.tensor_scalar(pfx[:], pfx[:], 1.0, None, ALU.is_equal)
    v.tensor_tensor(oh_out, dd[:], pfx[:], ALU.mult)


def ln_stats(P, v_or_none, block, x_ap, scr_ap, stats, dinv, eps):
    nc = P.nc
    Dp = x_ap.shape[-1] if hasattr(x_ap, "shape") else None
    negmu, rstd, s1, ssq = stats["negmu"], stats["rstd"], stats["s1"], stats["ssq"]
    with nc.Block() as b:
        @b.vector
        def _(v):
            v = P.wrap(v, "v")
            v.tensor_reduce(s1[:], x_ap, AX.X, ALU.add)
            v.tensor_scalar(negmu[:], s1[:], -dinv, None, ALU.mult)
    with nc.Block() as b:
        @b.scalar
        def _(sc):
            sc = P.wrap(sc, "s")
            sc.activation(scr_ap, x_ap, AF.Square, bias=negmu[:], scale=1.0,
                          accum_out=ssq[:])
            sc.activation(ssq[:], ssq[:], AF.Ln, bias=(eps / dinv), scale=1.0)
            sc.activation(rstd[:], ssq[:], AF.Exp, bias=(0.5 * math.log(1.0 / dinv)),
                          scale=-0.5)



def host_identity(n, dtype):
    return np.eye(n, dtype=dtype)


def host_blockmask_l0():
    import ml_dtypes
    bm = np.zeros((128, 16 * 72), np.float32)
    for k in range(16):
        for r in range(128):
            d = 128 * k + r
            bm[r, 72 * k + d // 32] = 1.0
            bm[r, 72 * k + 64 + d // 256] = 1.0
    return bm.astype(ml_dtypes.bfloat16)



def build_l0ffn(P, x_in, y_out, wnp, tag):
    nc = P.nc
    import ml_dtypes
    id128b = P.hconst("c_id128b", host_identity(128, np.float32).astype(ml_dtypes.bfloat16))
    id128f = P.hconst("c_id128f", host_identity(128, np.float32))
    bmask = P.hconst("c_bmask_l0", host_blockmask_l0())

    P.need_consts([-1.0 / D, D * 1e-5, 0.5 * math.log(D), 0.5, -1.0, 100.0,
                   1e-8, 1.0 - 1e-6, 16.0, -BIG, BIG] + [float(g) for g in range(2, 16)])

    gb = P.sbuf("gb", [128, D], F32)
    bb = P.sbuf("bb", [128, D], F32)
    wsig = P.sbuf("wsig", [128, 16 * 72], BF16)
    tables = P.sbuf("tables", [64, 1024], F32)
    stat3 = P.sbuf("stat3", [64, 4], F32)
    vacc = P.sbuf("vacc", [128, 1], F32)
    big64 = P.sbuf("big64", [128, 64], F32)
    ones64 = P.sbuf("ones64", [128, 64], F32)
    slo = P.sbuf("slo", [64, 512], F32)
    ssum = P.sbuf("ssum", [64, 32], F32)
    ssq16 = P.sbuf("ssq16", [16, 128], F32)
    ssflat = P.sbuf("ssflat", [128, 16], F32)
    sb_id128b = P.sbuf("id128b", [128, 128], BF16)
    sb_id128f = P.sbuf("id128f", [128, 128], F32)
    sb_bmask = P.sbuf("bmask", [128, 16 * 72], BF16)
    oh16 = P.sbuf("oh16", [128, 64 * NT], F32)
    ohT16 = P.sbuf("ohT16", [64, 128 * NT], F32)
    xs16 = P.sbuf("xs16", [128, 32 * NT], F32)
    xt = P.sbuf("xt", [128, D], F32)
    zt = P.sbuf("zt", [128, D], F32)
    zn = P.sbuf("zn", [128, D], F32)
    scr = P.sbuf("scr", [128, D], F32)
    sg = P.sbuf("sg", [128, D], BF16)
    sgT = P.sbuf("sgT", [128, D], BF16)
    sc72 = P.sbuf("sc72", [128, 72], F32)
    ohc = P.sbuf("ohc", [128, 8], F32)
    inc64 = P.sbuf("inc64", [128, 64], F32)
    msc = P.sbuf("msc", [128, 64], F32)
    d64 = P.sbuf("d64", [128, 64], F32)
    pfx64 = P.sbuf("pfx64", [128, 64], F32)
    m1 = P.sbuf("m1", [128, 1], F32)
    Mcat = P.sbuf("Mcat", [128, 128], F32)
    stats = dict(negmu=P.sbuf("negmu", [128, 1], F32), rstd=P.sbuf("rstd", [128, 1], F32),
                 s1=P.sbuf("s1", [128, 1], F32), ssq=P.sbuf("ssq", [128, 1], F32))
    stats2 = dict(negmu=P.sbuf("negmu2", [128, 1], F32), rstd=P.sbuf("rstd2", [128, 1], F32),
                  s1=P.sbuf("s12", [128, 1], F32), ssq=P.sbuf("ssq2", [128, 1], F32))
    tmin = P.sbuf("tmin", [128, 1], F32)
    tmax = P.sbuf("tmax", [128, 1], F32)
    tmn = P.sbuf("tmn", [128, 1], F32)
    a64 = P.sbuf("a64", [128, 64], F32)
    b64 = P.sbuf("b64", [128, 64], F32)
    st3 = P.sbuf("st3", [128, 4], F32)
    mn_neg = P.sbuf("mn_neg", [128, 1], F32)
    dd1 = P.sbuf("dd1", [128, 1], F32)
    di1 = P.sbuf("di1", [128, 1], F32)
    uu = P.sbuf("uu", [128, 32], F32)
    geb = P.sbuf("geb", [128, 512], F32)
    ohgb = P.sbuf("ohgb", [128, 512], F32)
    bt512 = P.sbuf("bt512", [128, 512], F32)
    b32 = P.sbuf("b32", [128, 32], F32)
    s32 = P.sbuf("s32", [128, 32], F32)
    ger = P.sbuf("ger", [128, 32], F32)
    xl = P.sbuf("xl", [128, 32], F32)
    delta = P.sbuf("delta", [128, 32], F32)
    outacc = P.sbuf("outacc", [128, D], F32)

    psA = P.psum("psA", [128, 1024], BF16)
    psB1 = P.psum("psB1", [128, 512], F32)
    psB2 = P.psum("psB2", [128, 512], F32)
    psC = P.psum("psC", [128, 128], F32)
    psD = P.psum("psD", [64, 128], F32)
    psE = P.psum("psE", [128, 72], F32)
    psF = P.psum("psF", [128, 4], F32)

    z_dram = P.dram(f"z_dram_{tag}", [TOK, D], F32)
    cc_in = P.dram(f"ccin_{tag}", [128, 1], F32)
    cc_out = P.dram(f"ccout_{tag}", [128, 1], F32)

    x_t = x_in.ap().rearrange("(n p) d -> n p d", p=128)
    y_t = y_out.ap().rearrange("(n p) d -> n p d", p=128)
    z_t = z_dram.ap().rearrange("(n p) d -> n p d", p=128)

    slopes_flat = P.hconst(f"w_slopes_{tag}", wnp["slopes"].reshape(64, 512).astype(np.float32))
    bases_flat = P.hconst(f"w_bases_{tag}", wnp["bases"].reshape(64, 512).astype(np.float32))
    gb_h = P.hconst(f"w_gb_{tag}", np.ascontiguousarray(
        np.broadcast_to(wnp["ln_gamma"][None, :], (128, D))).astype(np.float32))
    bb_h = P.hconst(f"w_bb_{tag}", np.ascontiguousarray(
        np.broadcast_to(wnp["ln_beta"][None, :], (128, D))).astype(np.float32))
    osc_h = P.hconst(f"w_osc_{tag}", wnp["oscale"].reshape(64, 1).astype(np.float32))

    with nc.Block() as b:
        @b.sync
        def _(sync):
            P.ld(sync, slo[:], slopes_flat.ap())
            P.ld(sync, tables[:, 0:512], bases_flat.ap())
            P.ld(sync, tables[:, 512:1024], slopes_flat.ap())
            P.ld(sync, gb[:], gb_h.ap())
            P.ld(sync, bb[:], bb_h.ap())
            P.ld(sync, sb_id128b[:], id128b.ap())
            P.ld(sync, sb_id128f[:], id128f.ap())
            P.ld(sync, sb_bmask[:], bmask.ap())
            P.ldwait(sync)
        @b.gpsimd
        def _(g):
            P.emit_consts(g)
            g.memset(ones64[:], 1.0)
            g.memset(big64[:], BIG)
            g.memset(vacc[:], BIG)
            g.memset(stat3[:], 0.0)
    with nc.Block() as b:
        @b.vector
        def _(v):
            v = P.wrap(v, "v")
            v.tensor_reduce(ssum[:], ap3(slo, 0, [[512, 64], [16, 32], [1, 16]]), AX.X, ALU.add)
    with nc.Block() as b:
        @b.scalar
        def _(sc):
            sc = P.wrap(sc, "s")
            sc.activation(ssum[:], ssum[:], AF.Sign)
    ss_dram = P.dram(f"ssd_{tag}", [64, 32], F32)
    with nc.Block() as b:
        @b.sync
        def _(sync):
            P.ld(sync, ss_dram.ap(), ssum[:])
            P.ldwait(sync)
    with nc.Block() as b:
        @b.sync
        def _(sync):
            P.ld(sync, ssq16[:], ss_dram.ap().rearrange("a b -> (a b)").rearrange("(k r) -> k r", k=16))
            P.ldwait(sync)
    with nc.Block() as b:
        @b.tensor
        def _(t):
            t.transpose(psC[:, 0:16], ssq16[:], sb_id128f[0:16, 0:16])
    with nc.Block() as b:
        @b.vector
        def _(v):
            v = P.wrap(v, "v")
            v.tensor_copy(ssflat[:], psC[:, 0:16])
            for k in range(16):
                v.tensor_scalar(wsig[:, 72 * k:72 * k + 72], sb_bmask[:, 72 * k:72 * k + 72],
                                ssflat[:, k:k + 1], None, ALU.mult)

    for ti in range(NT):
        ohsl = oh16[:, 64 * ti:64 * ti + 64]
        with nc.Block() as b:
            @b.sync
            def _(sync):
                P.ld(sync, xt[:], x_t[ti])
                P.ldwait(sync)
        ln_stats(P, None, None, xt[:], scr[:], stats, 1.0 / D, 1e-5)
        with nc.Block() as b:
            @b.vector
            def _(v):
                v = P.wrap(v, "v")
                v.tensor_scalar(zt[:], xt[:], stats["negmu"][:], stats["rstd"][:], ALU.add, ALU.mult)
                v.tensor_reduce(stats2["s1"][:], zt[:], AX.X, ALU.add)
                v.tensor_scalar(stats2["negmu"][:], stats2["s1"][:], -1.0 / D, None, ALU.mult)
        with nc.Block() as b:
            @b.scalar
            def _(sc):
                sc = P.wrap(sc, "s")
                sc.activation(scr[:], zt[:], AF.Square, bias=stats2["negmu"][:], scale=1.0,
                              accum_out=stats2["ssq"][:])
                sc.activation(stats2["ssq"][:], stats2["ssq"][:], AF.Ln, bias=(D * 1e-5), scale=1.0)
                sc.activation(stats2["rstd"][:], stats2["ssq"][:], AF.Exp,
                              bias=(0.5 * math.log(D)), scale=-0.5)
            @b.sync
            def _(sync):
                P.ld(sync, z_t[ti], zt[:])
                P.ldwait(sync)
        with nc.Block() as b:
            @b.vector
            def _(v):
                v = P.wrap(v, "v")
                v.tensor_scalar(zn[:], zt[:], stats2["negmu"][:], stats2["rstd"][:], ALU.add, ALU.mult)
                v.tensor_tensor(zn[:], zn[:], gb[:], ALU.mult)
                v.tensor_tensor(zn[:], zn[:], bb[:], ALU.add)
        with nc.Block() as b:
            @b.scalar
            def _(sc):
                sc = P.wrap(sc, "s")
                sc.activation(sg[:], zn[:], AF.Sign)
        for half in range(2):
            with nc.Block() as b:
                @b.tensor
                def _(t):
                    for k in range(8):
                        kk = 8 * half + k
                        t.transpose(psA[:, 128 * k:128 * k + 128],
                                    sg[:, 128 * kk:128 * kk + 128], sb_id128b[:])
            with nc.Block() as b:
                @b.vector
                def _(v):
                    v = P.wrap(v, "v")
                    v.tensor_copy(sgT[:, 1024 * half:1024 * half + 1024], psA[:])
        with nc.Block() as b:
            @b.tensor
            def _(t):
                for k in range(16):
                    t.matmul(psE[:], sgT[:, 128 * k:128 * k + 128],
                             wsig[:, 72 * k:72 * k + 72], start=(k == 0), stop=(k == 15))
        with nc.Block() as b:
            @b.vector
            def _(v):
                v = P.wrap(v, "v")
                v.tensor_copy(sc72[:], psE[:])
                v.tensor_reduce(m1[:], sc72[:, 64:72], AX.X, ALU.max)
                v.tensor_scalar(d64[:, 0:8], sc72[:, 64:72], m1[:], None, ALU.subtract)
                v.tensor_scalar(d64[:, 0:8], d64[:, 0:8], 0.0, None, ALU.is_equal)
                v.tensor_tensor_scan(pfx64[:, 0:8], ones64[:, 0:8], d64[:, 0:8], 0.0,
                                     ALU.mult, ALU.add)
                v.tensor_scalar(pfx64[:, 0:8], pfx64[:, 0:8], 1.0, None, ALU.is_equal)
                v.tensor_tensor(ohc[:], d64[:, 0:8], pfx64[:, 0:8], ALU.mult)
                for c in range(8):
                    v.tensor_scalar(inc64[:, 8 * c:8 * c + 8], ones64[:, 0:8],
                                    ohc[:, c:c + 1], None, ALU.mult)
                v.scalar_tensor_tensor(msc[:], inc64[:], 100.0, sc72[:, 0:64],
                                       ALU.mult, ALU.add)
                v.tensor_reduce(m1[:], msc[:], AX.X, ALU.max)
                v.tensor_scalar(d64[:], msc[:], m1[:], None, ALU.subtract)
                v.tensor_scalar(d64[:], d64[:], 0.0, None, ALU.is_equal)
                v.tensor_tensor_scan(pfx64[:], ones64[:], d64[:], 0.0, ALU.mult, ALU.add)
                v.tensor_scalar(pfx64[:], pfx64[:], 1.0, None, ALU.is_equal)
                v.tensor_tensor(ohsl, d64[:], pfx64[:], ALU.mult)
        with nc.Block() as b:
            @b.vector
            def _(v):
                v = P.wrap(v, "v")
                xssl = xs16[:, 32 * ti:32 * ti + 32]
                v.tensor_scalar(xssl, zn[:, 0:32], ohsl[:, 0:1], None, ALU.mult)
                for t_ in range(1, 64):
                    v.scalar_tensor_tensor(xssl, zn[:, 32 * t_:32 * t_ + 32],
                                           ohsl[:, t_:t_ + 1], xssl, ALU.mult, ALU.add)
                v.tensor_reduce(tmin[:], xssl, AX.X, ALU.min)
                v.tensor_reduce(tmax[:], xssl, AX.X, ALU.max)
                v.tensor_scalar(tmn[:], tmax[:], -1.0, None, ALU.mult)
                v.tensor_scalar(a64[:], ohsl, tmin[:], None, ALU.mult)
                v.scalar_tensor_tensor(b64[:], ohsl, -BIG, big64[:], ALU.mult, ALU.add)
                v.tensor_tensor(Mcat[:, 0:64], a64[:], b64[:], ALU.add)
                v.tensor_scalar(a64[:], ohsl, tmn[:], None, ALU.mult)
                v.tensor_tensor(Mcat[:, 64:128], a64[:], b64[:], ALU.add)
        with nc.Block() as b:
            @b.tensor
            def _(t):
                t.transpose(psC[:], Mcat[:], sb_id128f[:])
                t.transpose(psD[:], ohsl, sb_id128f[:])
        with nc.Block() as b:
            @b.vector
            def _(v):
                v = P.wrap(v, "v")
                v.tensor_reduce(tmin[:], psC[:], AX.X, ALU.min)
                v.tensor_tensor(vacc[:], vacc[:], tmin[:], ALU.min)
                v.tensor_copy(ohT16[:, 128 * ti:128 * ti + 128], psD[:])
        if ti == 0:
            P.dump("zn0", zn[:]); P.dump("sg0", sg[:]); P.dump("sc72", sc72[:])
            P.dump("ohc", ohc[:]); P.dump("oh0", oh16[:, 0:64]); P.dump("xs0", xs16[:, 0:32])
            P.dump("vacc0", vacc[:]); P.dump("wsig", wsig[:]); P.dump("ssflat", ssflat[:])
            P.dump("zt0", zt[:])

    with nc.Block() as b:
        @b.sync
        def _(sync):
            P.ld(sync, cc_in.ap(), vacc[:])
            P.ldwait(sync)
    cs = P.sem("ccs")
    with nc.Block() as b:
        @b.gpsimd
        def _(g):
            g.collective_compute("AllReduce", ALU.min, replica_groups=RG8,
                                 ins=[cc_in.ap().opt()], outs=[cc_out.ap().opt()]).then_inc(cs)
            g.wait_ge(cs, 1)
    with nc.Block() as b:
        @b.sync
        def _(sync):
            P.ld(sync, stat3[:, 0:1], cc_out.ap()[0:64, 0:1])
            P.ld(sync, stat3[:, 1:2], cc_out.ap()[64:128, 0:1])
            P.ld(sync, stat3[:, 2:3], osc_h.ap())
            P.ldwait(sync)
    with nc.Block() as b:
        @b.vector
        def _(v):
            v = P.wrap(v, "v")
            v.tensor_scalar(stat3[:, 1:2], stat3[:, 1:2], -1.0, None, ALU.mult)
            v.tensor_scalar(stat3[:, 2:3], stat3[:, 2:3], 0.5, None, ALU.mult)

    for ti in range(NT):
        ohsl = oh16[:, 64 * ti:64 * ti + 64]
        ohTsl = ohT16[:, 128 * ti:128 * ti + 128]
        xssl = xs16[:, 32 * ti:32 * ti + 32]
        with nc.Block() as b:
            @b.sync
            def _(sync):
                P.ld(sync, xt[:], x_t[ti])
                P.ld(sync, zt[:], z_t[ti])
                P.ldwait(sync)
        with nc.Block() as b:
            @b.tensor
            def _(t):
                t.matmul(psF[:], ohTsl, stat3[:], start=True, stop=True)
                t.matmul(psB1[:], ohTsl, tables[:, 0:512], start=True, stop=True)
                t.matmul(psB2[:], ohTsl, tables[:, 512:1024], start=True, stop=True)
        with nc.Block() as b:
            @b.vector
            def _(v):
                v = P.wrap(v, "v")
                v.tensor_copy(st3[:], psF[:])
                v.tensor_scalar(mn_neg[:], st3[:, 0:1], -1.0, None, ALU.mult)
                v.tensor_tensor(dd1[:], st3[:, 1:2], st3[:, 0:1], ALU.subtract)
                v.tensor_scalar(dd1[:], dd1[:], 1e-8, None, ALU.add)
                v.reciprocal(di1[:], dd1[:])
                v.tensor_scalar(uu[:], xssl, mn_neg[:], di1[:], ALU.add, ALU.mult)
                v.tensor_scalar(uu[:], uu[:], 0.0, 1.0 - 1e-6, ALU.max, ALU.min)
                v.tensor_scalar(uu[:], uu[:], 16.0, None, ALU.mult)
                for g_ in range(16):
                    v.tensor_scalar(ap3(geb, g_, [[512, 128], [16, 32]]), uu[:],
                                    float(g_), None, ALU.is_ge)
                v.tensor_tensor(ap3(ohgb, 0, [[512, 128], [16, 32], [1, 15]]),
                                ap3(geb, 0, [[512, 128], [16, 32], [1, 15]]),
                                ap3(geb, 1, [[512, 128], [16, 32], [1, 15]]), ALU.subtract)
                v.tensor_copy(ap3(ohgb, 15, [[512, 128], [16, 32]]),
                              ap3(geb, 15, [[512, 128], [16, 32]]))
                v.tensor_reduce(ger[:], ap3(geb, 0, [[512, 128], [16, 32], [1, 16]]),
                                AX.X, ALU.add)
                v.tensor_tensor(xl[:], uu[:], ger[:], ALU.subtract)
                v.tensor_scalar(xl[:], xl[:], 1.0, None, ALU.add)
                v.tensor_tensor(bt512[:], psB1[:], ohgb[:], ALU.mult)
                v.tensor_reduce(b32[:], ap3(bt512, 0, [[512, 128], [16, 32], [1, 16]]),
                                AX.X, ALU.add)
                v.tensor_tensor(bt512[:], psB2[:], ohgb[:], ALU.mult)
                v.tensor_reduce(s32[:], ap3(bt512, 0, [[512, 128], [16, 32], [1, 16]]),
                                AX.X, ALU.add)
                v.tensor_tensor(delta[:], s32[:], xl[:], ALU.mult)
                v.tensor_tensor(delta[:], delta[:], b32[:], ALU.add)
                v.tensor_scalar(delta[:], delta[:], st3[:, 2:3], None, ALU.mult)
                for t_ in range(64):
                    v.tensor_scalar(outacc[:, 32 * t_:32 * t_ + 32], delta[:],
                                    ohsl[:, t_:t_ + 1], None, ALU.mult)
                v.scalar_tensor_tensor(scr[:], zt[:], 0.5, xt[:], ALU.mult, ALU.add)
                v.tensor_tensor(scr[:], scr[:], outacc[:], ALU.add)
        if ti == 0:
            P.dump("st3", st3[:]); P.dump("uu0", uu[:]); P.dump("geb0", geb[:])
            P.dump("ohgb0", ohgb[:]); P.dump("b32", b32[:]); P.dump("s32", s32[:])
            P.dump("xl", xl[:]); P.dump("delta", delta[:]); P.dump("outacc0", outacc[:, 0:256])
        with nc.Block() as b:
            @b.sync
            def _(sync):
                P.st(sync, y_t[ti], scr[:])
                P.stwait(sync)



def finish(P):
    with P.nc.Block() as b:
        @b.sync
        def _(sync):
            P.stwait(sync)


def in_maps_for(P, per_core_inputs):
    maps = []
    for c in range(8):
        m = dict(per_core_inputs[c])
        for name, arr in P.host_consts.items():
            m[name] = arr[c] if isinstance(arr, list) else arr
        maps.append(m)
    return maps


def run_sublayer(build_fn, x_full, wnp, tag):
    from concourse.bass_utils import run_bass_kernel_spmd
    P = Prog()
    x_in = P.dram("x", [TOK, D], F32, kind="ExternalInput")
    y_out = P.dram("y", [TOK, D], F32, kind="ExternalOutput")
    build_fn(P, x_in, y_out, wnp, tag)
    finish(P)
    shards = np.split(np.ascontiguousarray(x_full.reshape(8 * TOK, D)), 8, axis=0)
    maps = in_maps_for(P, [{"x": s} for s in shards])
    res = run_bass_kernel_spmd(P.nc, maps, core_ids=list(range(8)))
    return np.concatenate([res.results[c]["y"] for c in range(8)], axis=0)



def build_l0attn(P, x_in, y_out, wnp, tag):
    nc = P.nc
    import ml_dtypes
    H = 4096
    HS = 512
    TC = 512
    NCH = 8192 // TC
    id128b = P.hconst("c_id128b", host_identity(128, np.float32).astype(ml_dtypes.bfloat16))
    P.need_consts([-1.0 / D, D * 1e-5, 0.5 * math.log(D)])

    WinT = wnp["W_in"].T
    win_h = P.hconst(f"w_winT_{tag}",
                     [np.ascontiguousarray(WinT[:, 512 * c:512 * c + 512]).astype(np.float32)
                      for c in range(8)])
    bin_h = P.hconst(f"w_bin_{tag}",
                     [np.ascontiguousarray(wnp["b_in"][512 * c:512 * c + 512].reshape(4, 128).T)
                      for c in range(8)])
    beta_h = P.hconst(f"w_betasl_{tag}",
                      [np.ascontiguousarray(wnp["beta"][512 * c:512 * c + 512].reshape(4, 128).T)
                       for c in range(8)])
    gam_h = P.hconst(f"w_gamsl_{tag}",
                     [np.ascontiguousarray(wnp["gamma"][512 * c:512 * c + 512].reshape(4, 128).T)
                      for c in range(8)])
    gam8_h = P.hconst(f"w_gam8_{tag}",
                      [np.ascontiguousarray(wnp["gamma"][512 * c:512 * c + 512]
                                            .reshape(4, 128).T[:, [0, 0, 1, 1, 2, 2, 3, 3]])
                       for c in range(8)])
    woutT_h = P.hconst(f"w_woutT_{tag}", np.ascontiguousarray(wnp["W_out"].T).astype(np.float32))
    bout_h = P.hconst(f"w_bout_{tag}", np.ascontiguousarray(wnp["b_out"].reshape(16, 128).T))

    zT_shard = P.dram(f"zTs_{tag}", [D, TOK], BF16)
    zT_all = P.dram(f"zTa_{tag}", [8 * D, TOK], BF16)
    st_shard = P.dram(f"sts_{tag}", [HS, 2 * 8192], BF16)
    st_all = P.dram(f"sta_{tag}", [8 * HS, 2 * 8192], BF16)
    wout_bf = P.dram(f"woutbf_{tag}", [H, D], BF16)
    def _mkidx(c):
        h = np.arange(4096).reshape(32, 128).T
        out = np.zeros((128, 128), np.int32)
        for tg in range(4):
            out[:, 32 * tg:32 * tg + 32] = h * 32 + 4 * c + tg
        return np.ascontiguousarray(out)
    idx_h = P.hconst("c_hcidx", [_mkidx(c) for c in range(8)])

    sb_idb = P.sbuf("idb", [128, 128], BF16)
    xt = P.sbuf("xt", [128, D], F32)
    scr = P.sbuf("scr", [128, D], F32)
    zb = P.sbuf("zb", [128, D], BF16)
    zTt = P.sbuf("zTt", [128, D], BF16)
    stats = dict(negmu=P.sbuf("negmu", [128, 1], F32), rstd=P.sbuf("rstd", [128, 1], F32),
                 s1=P.sbuf("s1", [128, 1], F32), ssq=P.sbuf("ssq", [128, 1], F32))
    winT_sb = P.sbuf("winT", [128, 16 * 512], BF16)
    ubias = P.sbuf("ubias", [128, 4], F32)
    beta_sb = P.sbuf("betasb", [128, 4], F32)
    gam_sb = P.sbuf("gam", [128, 4], F32)
    gam8_sb = P.sbuf("gam8", [128, 8], F32)
    rhsz = P.sbuf("rhsz", [128, 32 * 512], BF16)
    u_sb = [P.sbuf("u0", [128, 4096], F32), P.sbuf("u1", [128, 4096], F32)]
    stt_sb = [P.sbuf("st0", [128, 4096], F32), P.sbuf("st1", [128, 4096], F32)]
    z4 = P.sbuf("z4", [128, 8], F32)
    z4m = P.sbuf("z4m", [128, 16], F32)
    s8 = P.sbuf("s8", [128, 16], F32)
    wcol = P.sbuf("wcol", [128, 32 * 128], BF16)
    states_tg = P.sbuf("stg", [128, 32 * 512], BF16)
    yT_sb = P.sbuf("yTsb", [128, 512], BF16)
    ytok = [P.sbuf(f"ytok{j}", [128, 2048], BF16) for j in range(4)]
    bout_sb = P.sbuf("bout", [128, 16], F32)
    wtmp = P.sbuf("wtmp", [128, 2048], BF16)
    idx32 = P.sbuf("idx32", [128, 128], mybir.dt.int32)


    x_t = x_in.ap().rearrange("(n p) d -> n p d", p=128)
    y_t = y_out.ap().rearrange("(n p) d -> n p d", p=128)

    esem = P.sem("evac")
    vsem = P.sem("vscan")
    ssem = P.sem("sscan")

    with nc.Block() as b:
        @b.sync
        def _(sync):
            P.ld(sync, sb_idb[:], id128b.ap())
            P.ld(sync, ubias[:], bin_h.ap())
            P.ld(sync, beta_sb[:], beta_h.ap())
            P.ld(sync, gam_sb[:], gam_h.ap())
            P.ld(sync, gam8_sb[:], gam8_h.ap())
            P.ld(sync, bout_sb[:], bout_h.ap())
            P.ld(sync, idx32[:], idx_h.ap())
            P.ldwait(sync)
        @b.gpsimd
        def _(g):
            P.emit_consts(g)
            for k in range(16):
                g.dma_start(winT_sb[:, 512 * k:512 * k + 512],
                            win_h.ap()[128 * k:128 * k + 128, :]).then_inc(P.dsem, 16)
                P.dn += 1
            g.wait_ge(P.dsem, 16 * P.dn)
    with nc.Block() as b:
        @b.vector
        def _(v):
            v = P.wrap(v, "v")
            v.tensor_tensor(ubias[:], ubias[:], beta_sb[:], ALU.add)
        @b.gpsimd
        def _(g):
            for hc in range(32):
                g.dma_start(wtmp[:], woutT_h.ap()[128 * hc:128 * hc + 128, :]).then_inc(P.dsem, 16)
                P.dn += 1
                g.wait_ge(P.dsem, 16 * P.dn)
                g.dma_start(wout_bf.ap()[128 * hc:128 * hc + 128, :], wtmp[:]).then_inc(P.dsem, 16)
                P.dn += 1
                g.wait_ge(P.dsem, 16 * P.dn)

    ph1 = contextlib.ExitStack()
    psT = ph1.enter_context(nc.psum_tensor(f"psT_{P.uid()}", [128, 1024], BF16))
    for ti in range(NT):
        with nc.Block() as b:
            @b.sync
            def _(sync):
                P.ld(sync, xt[:], x_t[ti])
                P.ldwait(sync)
        ln_stats(P, None, None, xt[:], scr[:], stats, 1.0 / D, 1e-5)
        with nc.Block() as b:
            @b.vector
            def _(v):
                v = P.wrap(v, "v")
                v.tensor_scalar(zb[:], xt[:], stats["negmu"][:], stats["rstd"][:],
                                ALU.add, ALU.mult)
        for half in range(2):
            with nc.Block() as b:
                @b.tensor
                def _(t):
                    for k in range(8):
                        kk = 8 * half + k
                        t.transpose(psT[:, 128 * k:128 * k + 128],
                                    zb[:, 128 * kk:128 * kk + 128], sb_idb[:])
            with nc.Block() as b:
                @b.vector
                def _(v):
                    v = P.wrap(v, "v")
                    v.tensor_copy(zTt[:, 1024 * half:1024 * half + 1024], psT[:])
        with nc.Block() as b:
            @b.sync
            def _(sync):
                P.ld(sync, ap3(zT_shard, 128 * ti, [[TOK, 128], [128 * TOK, 16], [1, 128]]),
                     ap3(zTt, 0, [[2048, 128], [128, 16], [1, 128]]))
                P.ldwait(sync)
    cs1 = P.sem("agz")
    with nc.Block() as b:
        @b.gpsimd
        def _(g):
            g.collective_compute("AllGather", ALU.bypass, replica_groups=RG8,
                                 ins=[zT_shard.ap().opt()], outs=[zT_all.ap().opt()]).then_inc(cs1)
            g.wait_ge(cs1, 1)

    ph1.close()
    ph2 = contextlib.ExitStack()
    psU = [ph2.enter_context(nc.psum_tensor(f"psU{i}_{P.uid()}", [128, 512], F32))
           for i in range(8)]
    def rhs_slice(seq, c, k):
        g0 = seq * 8192 + 512 * c
        r = g0 // 2048
        tl = g0 - 2048 * r
        return zT_all.ap()[2048 * r + 128 * k: 2048 * r + 128 * k + 128, tl:tl + 512]

    for c in range(NCH + 2):
        with nc.Block() as b:
            @b.sync
            def _(sync):
                if c <= NCH - 1:
                    for seq in range(2):
                        for k in range(16):
                            P.ld(sync, rhsz[:, 512 * (16 * seq + k):512 * (16 * seq + k) + 512],
                                 rhs_slice(seq, c, k))
                    P.ldwait(sync)
            @b.tensor
            def _(t):
                if c <= NCH - 1:
                    t.wait_ge(P.dsem, 16 * P.dn)
                    if c > 0:
                        t.wait_ge(esem, 8 * c)
                    for k in range(16):
                        for hb in range(4):
                            for seq in range(2):
                                t.matmul(psU[2 * hb + seq][:],
                                         winT_sb[:, 512 * k + 128 * hb:512 * k + 128 * hb + 128],
                                         rhsz[:, 512 * (16 * seq + k):512 * (16 * seq + k) + 512],
                                         start=(k == 0), stop=(k == 15))
            @b.vector
            def _(v):
                if 1 <= c <= NCH:
                    cc = c - 1
                    ub = u_sb[cc % 2]
                    for hb in range(4):
                        for seq in range(2):
                            v.tensor_scalar(
                                ub[:, 1024 * hb + 512 * seq:1024 * hb + 512 * seq + 512],
                                psU[2 * hb + seq][:], ubias[:, hb:hb + 1], None,
                                ALU.add).then_inc(esem, 1)
                    for tl in range(TC):
                        i = 512 * cc + tl + 1
                        ucols = ap3(ub, tl, [[4096, 128], [512, 8]])
                        if i == 1:
                            v.wait_ge(esem, 8)
                            v.tensor_copy(z4[:, 0:8], ucols).then_inc(vsem, 1)
                            continue
                        v.wait_ge(ssem, i - 1)
                        pq = (i - 1) % 2
                        v.tensor_tensor(z4m[:, 8 * pq:8 * pq + 8], s8[:, 8 * pq:8 * pq + 8],
                                        gam8_sb[:], ALU.mult)
                        j = i - 1
                        ccj, tlj = (j - 1) // TC, (j - 1) % TC
                        v.tensor_copy(ap3(stt_sb[ccj % 2], tlj, [[4096, 128], [512, 8]]),
                                      s8[:, 8 * pq:8 * pq + 8])
                        v.tensor_tensor(z4[:, 0:8], z4m[:, 8 * pq:8 * pq + 8], ucols,
                                        ALU.add).then_inc(vsem, 1)
                elif c == NCH + 1:
                    j = 512 * NCH
                    pq = j % 2
                    ccj, tlj = (j - 1) // TC, (j - 1) % TC
                    v.wait_ge(ssem, j)
                    v.tensor_copy(ap3(stt_sb[ccj % 2], tlj, [[4096, 128], [512, 8]]),
                                  s8[:, 8 * pq:8 * pq + 8]).then_inc(vsem, 1)
            @b.scalar
            def _(sc):
                if 1 <= c <= NCH:
                    cc = c - 1
                    for tl in range(TC):
                        i = 512 * cc + tl + 1
                        sc.wait_ge(vsem, i)
                        sc.activation(s8[:, 8 * (i % 2):8 * (i % 2) + 8],
                                      z4[:, 0:8], AF.Tanh).then_inc(ssem, 1)
            @b.gpsimd
            def _(g):
                if c >= 2:
                    cc = c - 2
                    stb = stt_sb[cc % 2]
                    g.wait_ge(vsem, 512 * (cc + 1) + 1)
                    for hb in range(4):
                        g.dma_start(
                            ap3(st_shard, 128 * hb * 16384 + 512 * cc,
                                [[16384, 128], [8192, 2], [1, 512]]),
                            ap3(stb, 1024 * hb, [[4096, 128], [512, 2], [1, 512]])
                        ).then_inc(P.dsem, 16)
                        P.dn += 1
                    g.wait_ge(P.dsem, 16 * P.dn)
        if c == 1:
            P.dump("u0", u_sb[0][:])
            P.dump("stt0", stt_sb[0][:])

    cs2 = P.sem("agst")
    with nc.Block() as b:
        @b.gpsimd
        def _(g):
            g.collective_compute("AllGather", ALU.bypass, replica_groups=RG8,
                                 ins=[st_shard.ap().opt()], outs=[st_all.ap().opt()]).then_inc(cs2)
            g.wait_ge(cs2, 1)

    if getattr(P, "dbg", False):
        dbg_sb = P.sbuf("dbgsb", [128, 512], BF16)
        with nc.Block() as b:
            @b.sync
            def _(sync):
                P.ld(sync, dbg_sb[:], st_all.ap()[0:128, 0:512])
                P.ldwait(sync)
        P.dump("stall0", dbg_sb[:])
    ph2.close()
    psY = P.psum("psY", [128, 512], F32)
    psY2 = P.psum("psY2", [128, 512], BF16)
    for tg in range(4):
        with nc.Block() as b:
            @b.gpsimd
            def _(g):
                for hc in range(32):
                    g.indirect_dma_start(
                        states_tg[:, 512 * hc:512 * hc + 512], None,
                        ap3(st_all, 0, [[512, 131072], [1, 512]]),
                        IndirectOffsetOnAxis(idx32[:, 32 * tg + hc:32 * tg + hc + 1], 0)
                    ).then_inc(P.dsem, 16)
                    P.dn += 1
                g.wait_ge(P.dsem, 16 * P.dn)
        if tg == 0:
            P.dump("stg0", states_tg[:, 0:1024])
        for dc in range(16):
            with nc.Block() as b:
                @b.sync
                def _(sync):
                    P.ld(sync, ap3(wcol, 0, [[4096, 128], [128, 32], [1, 128]]),
                         ap3(wout_bf, 128 * dc, [[2048, 128], [128 * 2048, 32], [1, 128]]))
                    P.ldwait(sync)
                @b.tensor
                def _(t):
                    t.wait_ge(P.dsem, 16 * P.dn)
                    for hc in range(32):
                        t.matmul(psY[:], wcol[:, 128 * hc:128 * hc + 128],
                                 states_tg[:, 512 * hc:512 * hc + 512],
                                 start=(hc == 0), stop=(hc == 31))
            with nc.Block() as b:
                @b.vector
                def _(v):
                    v = P.wrap(v, "v")
                    v.tensor_scalar(yT_sb[:], psY[:], bout_sb[:, dc:dc + 1], None, ALU.add)
            with nc.Block() as b:
                @b.tensor
                def _(t):
                    for j in range(4):
                        t.transpose(psY2[:, 128 * j:128 * j + 128],
                                    yT_sb[:, 128 * j:128 * j + 128], sb_idb[:])
            with nc.Block() as b:
                @b.vector
                def _(v):
                    v = P.wrap(v, "v")
                    for j in range(4):
                        v.tensor_copy(ytok[j][:, 128 * dc:128 * dc + 128],
                                      psY2[:, 128 * j:128 * j + 128])
        if tg == 0:
            P.dump("ytok0", ytok[0][:])
        for j in range(4):
            ti = 4 * tg + j
            with nc.Block() as b:
                @b.sync
                def _(sync):
                    P.ld(sync, xt[:], x_t[ti])
                    P.ldwait(sync)
            with nc.Block() as b:
                @b.vector
                def _(v):
                    v = P.wrap(v, "v")
                    v.tensor_tensor(scr[:], xt[:], ytok[j][:], ALU.add)
            with nc.Block() as b:
                @b.sync
                def _(sync):
                    P.st(sync, y_t[ti], scr[:])
                    P.stwait(sync)



def _emit_dequant_tile(P, v, pk_i32, scf, scf_fs, wout_bf, wout_fs, scratch, ncols,
                       scf0=0, w0=0):
    W = ncols
    S = [tuple(scratch[k + sfx] for k in ("nib", "nf", "sbit", "m", "mag", "t1", "t2"))
         for sfx in ("", "2", "3", "4")]
    for gp in range(2):
        gg = (4 * gp, 4 * gp + 1, 4 * gp + 2, 4 * gp + 3)
        def both(fn):
            for j in range(4):
                nib, nf, sbit, m, mag, t1, t2 = S[j]
                fn(gg[j], nib[:, 0:W], nf[:, 0:W], sbit[:, 0:W], m[:, 0:W],
                   mag[:, 0:W], t1[:, 0:W], t2[:, 0:W])
        both(lambda g, nib, nf, sbit, m, mag, t1, t2:
             v.tensor_single_scalar(nib, pk_i32, 4 * g, ALU.logical_shift_right))
        both(lambda g, nib, nf, sbit, m, mag, t1, t2:
             v.tensor_single_scalar(nib, nib, 15, ALU.bitwise_and))
        both(lambda g, nib, nf, sbit, m, mag, t1, t2:
             v.tensor_copy(nf, nib))
        both(lambda g, nib, nf, sbit, m, mag, t1, t2:
             v.tensor_single_scalar(sbit, nf, 8.0, ALU.is_ge))
        both(lambda g, nib, nf, sbit, m, mag, t1, t2:
             v.scalar_tensor_tensor(m, sbit, -8.0, nf, ALU.mult, ALU.add))
        both(lambda g, nib, nf, sbit, m, mag, t1, t2:
             v.tensor_single_scalar(t1, m, 2.0, ALU.subtract))
        both(lambda g, nib, nf, sbit, m, mag, t1, t2:
             v.scalar_tensor_tensor(mag, m, 0.5, t1, ALU.mult, ALU.max))
        both(lambda g, nib, nf, sbit, m, mag, t1, t2:
             v.tensor_scalar(t2, m, 2.0, 8.0, ALU.mult, ALU.subtract))
        both(lambda g, nib, nf, sbit, m, mag, t1, t2:
             v.tensor_tensor(mag, mag, t2, ALU.max))
        both(lambda g, nib, nf, sbit, m, mag, t1, t2:
             v.tensor_scalar(t1, sbit, -2.0, 1.0, ALU.mult, ALU.add))
        both(lambda g, nib, nf, sbit, m, mag, t1, t2:
             v.tensor_tensor(t2, mag, t1, ALU.mult))
        both(lambda g, nib, nf, sbit, m, mag, t1, t2:
             v.tensor_tensor(ap3(wout_bf, w0 + g, [[wout_fs, 128], [8, W]]), t2,
                             ap3(scf, scf0 + g, [[scf_fs, 128], [8, W]]), ALU.mult))


def build_l1ffn(P, x_in, y_out, wnp, tag):
    nc = P.nc
    import ml_dtypes
    I = 5632
    id128b = P.hconst("c_id128b", host_identity(128, np.float32).astype(ml_dtypes.bfloat16))
    P.need_consts([-1.0 / D, D * 1e-5, 0.5 * math.log(D), 0.5, 2.0, 8.0, -8.0, -2.0,
                   D * 1e-6])

    dq = "l1w" not in P.cache
    if dq:
        gup = wnp["gate_up_packed"]
        dnp_ = wnp["down_packed"]
        gsc = wnp["gate_up_scales"].astype(np.float32)
        dsc = wnp["down_scales"].astype(np.float32)
        gup_h = P.hconst("w_gup_sl", [np.ascontiguousarray(gup[:, 32 * c:32 * c + 32])
                                      for c in range(8)])
        dnp_h = P.hconst("w_dnp_sl", [np.ascontiguousarray(dnp_[:, 88 * c:88 * c + 88])
                                      for c in range(8)])
        gsc_h = P.hconst("w_gsc_sl",
                         [np.ascontiguousarray(gsc.T[:, (256 * c + np.arange(256)) // 128])
                          for c in range(8)])
        dsc_h = P.hconst("w_dsc_sl",
                         [np.ascontiguousarray(dsc.T[:, (704 * c + np.arange(704)) // 128])
                          for c in range(8)])
        w_guT = P.dram("l1_wguT", [D, 11264], BF16)
        w_dnT = P.dram("l1_wdnT", [I, D], BF16)
        guT_sh = P.dram("l1_guTsh", [256, 11264], BF16)
        dnT_sh = P.dram("l1_dnTsh", [704, D], BF16)
        P.cache["l1w"] = (w_guT, w_dnT)
    else:
        w_guT, w_dnT = P.cache["l1w"]

    x_t = x_in.ap().rearrange("(n p) d -> n p d", p=128)
    y_t = y_out.ap().rearrange("(n p) d -> n p d", p=128)

    with P.scope():
        sb_idb = P.sbuf("idb", [128, 128], BF16)
        with nc.Block() as b:
            @b.sync
            def _(sync):
                P.ld(sync, sb_idb[:], id128b.ap())
                P.ldwait(sync)
            @b.gpsimd
            def _(g):
                P.emit_consts(g)

        if dq:
            phDQ = contextlib.ExitStack()
            P._scopes.append(phDQ)
            scratch = {}
            for sfx in ("", "2", "3", "4"):
                for k in ("nf", "sbit", "m", "mag", "t1", "t2"):
                    scratch[k + sfx] = P.sbuf(k + sfx, [128, 88], F32)
                scratch["nib" + sfx] = P.sbuf("nib" + sfx, [128, 88], mybir.dt.int32)
            pk = P.sbuf("pk", [128, 2 * 88], mybir.dt.int32)
            scf = P.sbuf("scf", [128, 2 * 704], F32)
            wde = P.sbuf("wde", [128, 2 * 704], BF16)
            wtr = P.sbuf("wtr", [128, 2 * 768], BF16)
            phW = contextlib.ExitStack()
            psW = [phW.enter_context(nc.psum_tensor(f"psW{i}_{P.uid()}", [128, 1024], BF16))
                   for i in range(2)]
            ngu = getattr(P, "ngu_tiles", 88)
            ndn = getattr(P, "ndn_tiles", 16)
            dqcnt = {}

            def tile_loop(ntiles, W, ld_tile, tr_tile, cp_tile, st_tile):
                with nc.Block() as b:
                    @b.sync
                    def _(sync):
                        ld_tile(sync, None, 0)
                    @b.gpsimd
                    def _(g):
                        ld_tile(None, g, 0)
                for i in range(ntiles + 3):
                    with nc.Block() as b:
                        @b.sync
                        def _(sync):
                            if i + 1 <= ntiles - 1:
                                ld_tile(sync, None, i + 1)
                            if 3 <= i:
                                st_tile(sync, i - 3)
                        @b.gpsimd
                        def _(g):
                            if i + 1 <= ntiles - 1:
                                ld_tile(None, g, i + 1)
                        @b.vector
                        def _(v):
                            v = P.wrap(v, "v")
                            if i <= ntiles - 1:
                                v.wait_ge(P.dsem, 16 * dqcnt[i])
                                pp = i % 2
                                _emit_dequant_tile(P, v, pk[:, 88 * pp:88 * pp + W],
                                                   scf, 1408, wde, 1408, scratch, W,
                                                   scf0=704 * pp, w0=704 * pp)
                            if 2 <= i:
                                cp_tile(v, (i - 2) % 2)
                        @b.tensor
                        def _(t):
                            if 1 <= i <= ntiles:
                                tr_tile(t, (i - 1) % 2)

            def ld_gu(sync, g, i):
                pp = i % 2
                if sync is not None:
                    P.ld(sync, pk[:, 88 * pp:88 * pp + 32], gup_h.ap()[128 * i:128 * i + 128, :])
                if g is not None:
                    g.dma_start(scf[:, 704 * pp:704 * pp + 256],
                                gsc_h.ap()[128 * i:128 * i + 128, :]).then_inc(P.dsem, 16)
                    P.dn += 1
                    dqcnt[i] = P.dn

            def tr_gu(t, pp):
                t.transpose(psW[pp][:, 0:128], wde[:, 704 * pp:704 * pp + 128], sb_idb[:])
                t.transpose(psW[pp][:, 128:256], wde[:, 704 * pp + 128:704 * pp + 256], sb_idb[:])

            def cp_gu(v, pp):
                v.tensor_copy(wtr[:, 768 * pp:768 * pp + 256], psW[pp][:, 0:256])

            def st_gu(sync, i):
                pp = i % 2
                P.ld(sync, ap3(guT_sh, 128 * i, [[11264, 128], [128 * 11264, 2], [1, 128]]),
                     ap3(wtr, 768 * pp, [[1536, 128], [128, 2], [1, 128]]))

            tile_loop(ngu, 32, ld_gu, tr_gu, cp_gu, st_gu)
            if getattr(P, "stop_after", None) == "gu_tiles":
                P.dump("wde", wde[:])
                P.dump("wtr", wtr[:])
                return

            def ld_dn2(sync, g, i):
                pp = i % 2
                if sync is not None:
                    P.ld(sync, pk[:, 88 * pp:88 * pp + 88], dnp_h.ap()[128 * i:128 * i + 128, :])
                if g is not None:
                    g.dma_start(scf[:, 704 * pp:704 * pp + 704],
                                dsc_h.ap()[128 * i:128 * i + 128, :]).then_inc(P.dsem, 16)
                    P.dn += 1
                    dqcnt[i] = P.dn

            def tr_dn(t, pp):
                for blk in range(5):
                    t.transpose(psW[pp][:, 128 * blk:128 * blk + 128],
                                wde[:, 704 * pp + 128 * blk:704 * pp + 128 * blk + 128],
                                sb_idb[:])
                t.transpose(psW[pp][0:64, 640:768], wde[:, 704 * pp + 640:704 * pp + 704],
                            sb_idb[:])

            def cp_dn(v, pp):
                v.tensor_copy(wtr[:, 768 * pp:768 * pp + 640], psW[pp][:, 0:640])
                v.tensor_copy(wtr[0:64, 768 * pp + 640:768 * pp + 768],
                              psW[pp][0:64, 640:768])

            def st_dn(sync, i):
                pp = i % 2
                P.ld(sync, ap3(dnT_sh, 128 * i, [[2048, 128], [128 * 2048, 5], [1, 128]]),
                     ap3(wtr, 768 * pp, [[1536, 128], [128, 5], [1, 128]]))
                P.ld(sync, ap3(dnT_sh, 5 * 128 * 2048 + 128 * i, [[2048, 64], [1, 128]]),
                     ap3(wtr, 768 * pp + 640, [[1536, 64], [1, 128]]))

            tile_loop(ndn, 88, ld_dn2, tr_dn, cp_dn, st_dn)
            if getattr(P, "stop_after", None) == "dn_tiles":
                P.dump("wde", wde[:])
                return
            cs1 = P.sem("agw1")
            cs1b = P.sem("agw1b")
            with nc.Block() as b:
                @b.gpsimd
                def _(g):
                    g.collective_compute("AllGather", ALU.bypass, replica_groups=RG8,
                                         ins=[guT_sh.ap().opt()],
                                         outs=[w_guT.ap().opt()]).then_inc(cs1)
                    g.wait_ge(cs1, 1)
                    g.collective_compute("AllGather", ALU.bypass, replica_groups=RG8,
                                         ins=[dnT_sh.ap().opt()],
                                         outs=[w_dnT.ap().opt()]).then_inc(cs1b)
                    g.wait_ge(cs1b, 1)
            phW.close()
            P._scopes.pop()
            phDQ.close()

        if getattr(P, "stop_after", None) == "dequant":
            og = nc.dram_tensor("dbg_wgu", [256, 2048], F32, kind="ExternalOutput")
            od = nc.dram_tensor("dbg_wdn", [256, 2048], F32, kind="ExternalOutput")
            tmpb = P.sbuf("tmpb", [128, 2048], BF16)
            tmpf = P.sbuf("tmpf", [128, 2048], F32)
            for rr in range(2):
                with nc.Block() as b:
                    @b.sync
                    def _(sync):
                        P.ld(sync, tmpb[:], w_guT.ap()[128 * rr:128 * rr + 128, 0:2048])
                        P.ldwait(sync)
                with nc.Block() as b:
                    @b.vector
                    def _(v):
                        v = P.wrap(v, "v")
                        v.tensor_copy(tmpf[:], tmpb[:])
                with nc.Block() as b:
                    @b.sync
                    def _(sync):
                        P.st(sync, og.ap()[128 * rr:128 * rr + 128, :], tmpf[:])
                        P.stwait(sync)
            for rr in range(2):
                with nc.Block() as b:
                    @b.sync
                    def _(sync):
                        P.ld(sync, tmpb[:], w_dnT.ap()[128 * rr:128 * rr + 128, 0:2048])
                        P.ldwait(sync)
                with nc.Block() as b:
                    @b.vector
                    def _(v):
                        v = P.wrap(v, "v")
                        v.tensor_copy(tmpf[:], tmpb[:])
                with nc.Block() as b:
                    @b.sync
                    def _(sync):
                        P.st(sync, od.ap()[128 * rr:128 * rr + 128, :], tmpf[:])
                        P.stwait(sync)
            return
        xt = P.sbuf("xt", [128, D], F32)
        scr = P.sbuf("scr", [128, D], F32)
        zb = P.sbuf("zb", [128, D], BF16)
        stats = dict(negmu=P.sbuf("negmu", [128, 1], F32), rstd=P.sbuf("rstd", [128, 1], F32),
                     s1=P.sbuf("s1", [128, 1], F32), ssq=P.sbuf("ssq", [128, 1], F32))
        zT_half = P.sbuf("zTh", [128, 16 * 1024], BF16)
        hT_half = P.sbuf("hTh", [128, 44 * 1024], BF16)
        y0T = zT_half
        wg_sb = P.sbuf("wg", [128, 2 * 16 * 128], BF16)
        wu_sb = P.sbuf("wu", [128, 2 * 16 * 128], BF16)
        wd_sb = P.sbuf("wd", [128, 2 * 44 * 128], BF16)
        sg0 = P.sbuf("sg0", [128, 512], F32)
        sg1 = P.sbuf("sg1", [128, 512], F32)
        y0tok = P.sbuf("y0tok", [128, D], BF16)
        rmsi = P.sbuf("rmsi", [128, 1], F32)
        sil = P.sem(f"sil_{tag}")
        nsil = [0]
        for half in range(2):
            phA = contextlib.ExitStack()
            psT2 = phA.enter_context(nc.psum_tensor(f"psT2a_{P.uid()}", [128, 1024], BF16))
            for j in range(8):
                ti = 8 * half + j
                with nc.Block() as b:
                    @b.sync
                    def _(sync):
                        P.ld(sync, xt[:], x_t[ti])
                        P.ldwait(sync)
                ln_stats(P, None, None, xt[:], scr[:], stats, 1.0 / D, 1e-5)
                with nc.Block() as b:
                    @b.vector
                    def _(v):
                        v = P.wrap(v, "v")
                        v.tensor_scalar(zb[:], xt[:], stats["negmu"][:], stats["rstd"][:],
                                        ALU.add, ALU.mult)
                for hf in range(2):
                    with nc.Block() as b:
                        @b.tensor
                        def _(t):
                            for k in range(8):
                                kk = 8 * hf + k
                                t.transpose(psT2[:, 128 * k:128 * k + 128],
                                            zb[:, 128 * kk:128 * kk + 128], sb_idb[:])
                    with nc.Block() as b:
                        @b.vector
                        def _(v):
                            v = P.wrap(v, "v")
                            for k in range(8):
                                kk = 8 * hf + k
                                v.tensor_copy(zT_half[:, 1024 * kk + 128 * j:1024 * kk + 128 * j + 128],
                                              psT2[:, 128 * k:128 * k + 128])
            phA.close()
            phB = contextlib.ExitStack()
            psG = [phB.enter_context(nc.psum_tensor(f"psG{i}_{P.uid()}", [128, 512], F32))
                   for i in range(4)]
            psU2 = [phB.enter_context(nc.psum_tensor(f"psU2{i}_{P.uid()}", [128, 512], F32))
                    for i in range(4)]
            ldcnt = {}

            def ld_gu(sync, hc):
                pp = hc % 2
                P.ld(sync, ap3(wg_sb, 2048 * pp, [[4096, 128], [128, 16], [1, 128]]),
                     ap3(w_guT, 128 * hc, [[11264, 128], [128 * 11264, 16], [1, 128]]))
                P.ld(sync, ap3(wu_sb, 2048 * pp, [[4096, 128], [128, 16], [1, 128]]),
                     ap3(w_guT, I + 128 * hc, [[11264, 128], [128 * 11264, 16], [1, 128]]))
                ldcnt[hc] = P.dn
            with nc.Block() as b:
                @b.sync
                def _(sync):
                    ld_gu(sync, 0)
            for hc in range(45):
                with nc.Block() as b:
                    @b.sync
                    def _(sync):
                        if hc + 1 <= 43:
                            ld_gu(sync, hc + 1)
                    @b.tensor
                    def _(t):
                        if hc <= 43:
                            pp = hc % 2
                            t.wait_ge(P.dsem, 16 * ldcnt[hc])
                            for k in range(16):
                                for tg in range(2):
                                    t.matmul(psG[2 * pp + tg][:],
                                             wg_sb[:, 2048 * pp + 128 * k:2048 * pp + 128 * k + 128],
                                             zT_half[:, 1024 * k + 512 * tg:1024 * k + 512 * tg + 512],
                                             start=(k == 0), stop=(k == 15))
                                for tg in range(2):
                                    t.matmul(psU2[2 * pp + tg][:],
                                             wu_sb[:, 2048 * pp + 128 * k:2048 * pp + 128 * k + 128],
                                             zT_half[:, 1024 * k + 512 * tg:1024 * k + 512 * tg + 512],
                                             start=(k == 0), stop=(k == 15))
                    @b.scalar
                    def _(sc):
                        if hc >= 1:
                            pq = (hc - 1) % 2
                            sc.activation(sg0[:], psG[2 * pq][:], AF.Silu)
                            sc.activation(sg1[:], psG[2 * pq + 1][:], AF.Silu).then_inc(sil, 1)
                            nsil[0] += 1
                    @b.vector
                    def _(v):
                        if hc >= 1:
                            pq = (hc - 1) % 2
                            v.wait_ge(sil, nsil[0])
                            v.tensor_tensor(hT_half[:, 1024 * (hc - 1):1024 * (hc - 1) + 512],
                                            sg0[:], psU2[2 * pq][:], ALU.mult)
                            v.tensor_tensor(hT_half[:, 1024 * (hc - 1) + 512:1024 * (hc - 1) + 1024],
                                            sg1[:], psU2[2 * pq + 1][:], ALU.mult)
            phB.close()
            phC = contextlib.ExitStack()
            psY = [phC.enter_context(nc.psum_tensor(f"psYl1{i}_{P.uid()}", [128, 512], F32))
                   for i in range(4)]
            ldcnt2 = {}

            def ld_dn(sync, dcs):
                pp = dcs % 2
                P.ld(sync, ap3(wd_sb, 5632 * pp, [[11264, 128], [128, 44], [1, 128]]),
                     ap3(w_dnT, 128 * dcs, [[2048, 128], [128 * 2048, 44], [1, 128]]))
                ldcnt2[dcs] = P.dn
            with nc.Block() as b:
                @b.sync
                def _(sync):
                    ld_dn(sync, 0)
            for dcs in range(17):
                with nc.Block() as b:
                    @b.sync
                    def _(sync):
                        if dcs + 1 <= 15:
                            ld_dn(sync, dcs + 1)
                    @b.tensor
                    def _(t):
                        if dcs <= 15:
                            pp = dcs % 2
                            t.wait_ge(P.dsem, 16 * ldcnt2[dcs])
                            for ic in range(44):
                                for tg in range(2):
                                    t.matmul(psY[2 * pp + tg][:],
                                             wd_sb[:, 5632 * pp + 128 * ic:5632 * pp + 128 * ic + 128],
                                             hT_half[:, 1024 * ic + 512 * tg:1024 * ic + 512 * tg + 512],
                                             start=(ic == 0), stop=(ic == 43))
                    @b.vector
                    def _(v):
                        if dcs >= 1:
                            pq = (dcs - 1) % 2
                            dc0 = dcs - 1
                            v.tensor_copy(y0T[:, 512 * 2 * dc0:512 * 2 * dc0 + 512],
                                          psY[2 * pq][:])
                            v.tensor_copy(y0T[:, 512 * 2 * dc0 + 512:512 * 2 * dc0 + 1024],
                                          psY[2 * pq + 1][:])
            phC.close()
            phD = contextlib.ExitStack()
            psT2 = phD.enter_context(nc.psum_tensor(f"psT2d_{P.uid()}", [128, 1024], BF16))
            for j in range(8):
                ti = 8 * half + j
                for hf in range(2):
                    with nc.Block() as b:
                        @b.tensor
                        def _(t):
                            for k in range(8):
                                dc = 8 * hf + k
                                t.transpose(psT2[:, 128 * k:128 * k + 128],
                                            y0T[:, 1024 * dc + 128 * j:1024 * dc + 128 * j + 128],
                                            sb_idb[:])
                    with nc.Block() as b:
                        @b.vector
                        def _(v):
                            v = P.wrap(v, "v")
                            for k in range(8):
                                dc = 8 * hf + k
                                v.tensor_copy(y0tok[:, 128 * dc:128 * dc + 128],
                                              psT2[:, 128 * k:128 * k + 128])
                with nc.Block() as b:
                    @b.scalar
                    def _(sc):
                        sc = P.wrap(sc, "s")
                        sc.activation(scr[:], y0tok[:], AF.Square, bias=0.0, scale=1.0,
                                      accum_out=rmsi[:])
                        sc.activation(rmsi[:], rmsi[:], AF.Ln, bias=(D * 1e-6), scale=1.0)
                        sc.activation(rmsi[:], rmsi[:], AF.Exp, bias=(0.5 * math.log(D)),
                                      scale=-0.5)
                    @b.sync
                    def _(sync):
                        P.ld(sync, xt[:], x_t[ti])
                        P.ldwait(sync)
                with nc.Block() as b:
                    @b.vector
                    def _(v):
                        v = P.wrap(v, "v")
                        v.tensor_single_scalar(rmsi[:], rmsi[:], 0.5, ALU.mult)
                        v.tensor_scalar(scr[:], y0tok[:], rmsi[:], None, ALU.mult)
                        v.tensor_tensor(scr[:], scr[:], xt[:], ALU.add)
                with nc.Block() as b:
                    @b.sync
                    def _(sync):
                        P.st(sync, y_t[ti], scr[:])
                        P.stwait(sync)
            phD.close()



RG2x4 = [[0, 1, 2, 3], [4, 5, 6, 7]]


def build_l1attn(P, x_in, y_out, wnp, tag):
    nc = P.nc
    import ml_dtypes
    id128b = P.hconst("c_id128b", host_identity(128, np.float32).astype(ml_dtypes.bfloat16))
    SCL = 1.0 / math.sqrt(128.0)
    P.need_consts([-1.0 / D, D * 1e-5, 0.5 * math.log(D), SCL, -1.0])

    wqkv_h = P.hconst(f"w_qkv_{tag}", wnp["w_qkv"].astype(np.float32))
    bq_h = P.hconst(f"w_bqkv_{tag}",
                    np.ascontiguousarray(wnp["b_qkv"].reshape(48, 128).T))
    vbias_h = P.hconst(f"w_vbias_{tag}", np.ascontiguousarray(
        np.broadcast_to(wnp["b_qkv"][4096:6144][None, :], (128, 2048))).astype(np.float32))
    wout_h = P.hconst(f"w_wo_{tag}", wnp["w_out"].astype(np.float32))
    bout_h = P.hconst(f"w_bo_{tag}", np.ascontiguousarray(wnp["b_out"].reshape(16, 128).T))

    K_sh = P.dram(f"Ksh_{tag}", [D, TOK], BF16)
    V_sh = P.dram(f"Vsh_{tag}", [TOK, D], BF16)
    K_seq = P.dram(f"Kseq_{tag}", [4 * D, TOK], BF16)
    V_seq = P.dram(f"Vseq_{tag}", [4 * TOK, D], BF16)
    km_in = P.dram(f"kmin_{tag}", [1, 16], F32)
    km_out = P.dram(f"kmout_{tag}", [1, 16], F32)
    wqkv_bf = P.dram(f"wqkvbf_{tag}", [D, 6144], BF16)
    wo_bf = P.dram(f"wobf_{tag}", [D, D], BF16)

    x_t = x_in.ap().rearrange("(n p) d -> n p d", p=128)
    y_t = y_out.ap().rearrange("(n p) d -> n p d", p=128)

    with P.scope():
        sb_idb = P.sbuf("idb", [128, 128], BF16)
        xt = P.sbuf("xt", [128, D], F32)
        scr = P.sbuf("scr", [128, D], F32)
        zb = P.sbuf("zb", [128, D], BF16)
        stats = dict(negmu=P.sbuf("negmu", [128, 1], F32), rstd=P.sbuf("rstd", [128, 1], F32),
                     s1=P.sbuf("s1", [128, 1], F32), ssq=P.sbuf("ssq", [128, 1], F32))
        qT = P.sbuf("qT", [128, 16 * 2048], BF16)
        wtile = P.sbuf("wtile", [128, 2 * 2048], BF16)
        bq_sb = P.sbuf("bq", [128, 48], F32)
        vb_sb = P.sbuf("vb", [128, 2048], F32)
        bo_sb = P.sbuf("bo", [128, 16], F32)
        ksh_t = P.sbuf("ksht", [128, 512], BF16)
        kn = P.sbuf("kn", [128, 1], F32)
        knacc = P.sbuf("knacc", [128, 16], F32)
        km16 = P.sbuf("km16", [1, 16], F32)
        qm16 = P.sbuf("qm16", [1, 16], F32)
        negM = P.sbuf("negM", [128, 16], F32)
        ones1f = P.sbuf("ones1f", [1, 128], F32)
        onescol = P.sbuf("onescol", [128, 1], BF16)
        sq2 = P.sbuf("sq2", [128, 2048], F32)
        m1 = P.sbuf("m1", [128, 1], F32)

        with nc.Block() as b:
            @b.sync
            def _(sync):
                P.ld(sync, sb_idb[:], id128b.ap())
                P.ld(sync, bq_sb[:], bq_h.ap())
                P.ld(sync, vb_sb[:], vbias_h.ap())
                P.ld(sync, bo_sb[:], bout_h.ap())
                P.ldwait(sync)
            @b.gpsimd
            def _(g):
                P.emit_consts(g)
                g.memset(ones1f[:], 1.0)
                g.memset(onescol[:], 1.0)
                g.memset(knacc[:], 0.0)
                for r in range(16):
                    g.dma_start(wtile[:, 0:2048],
                                wqkv_h.ap()[128 * r:128 * r + 128, 0:2048]).then_inc(P.dsem, 16)
                    g.dma_start(wtile[:, 2048:4096],
                                wqkv_h.ap()[128 * r:128 * r + 128, 2048:4096]).then_inc(P.dsem, 16)
                    P.dn += 2
                    g.wait_ge(P.dsem, 16 * P.dn)
                    g.dma_start(wqkv_bf.ap()[128 * r:128 * r + 128, 0:2048],
                                wtile[:, 0:2048]).then_inc(P.dsem, 16)
                    g.dma_start(wqkv_bf.ap()[128 * r:128 * r + 128, 2048:4096],
                                wtile[:, 2048:4096]).then_inc(P.dsem, 16)
                    P.dn += 2
                    g.wait_ge(P.dsem, 16 * P.dn)
                    g.dma_start(wtile[:, 0:2048],
                                wqkv_h.ap()[128 * r:128 * r + 128, 4096:6144]).then_inc(P.dsem, 16)
                    g.dma_start(wtile[:, 2048:4096],
                                wout_h.ap()[128 * r:128 * r + 128, :]).then_inc(P.dsem, 16)
                    P.dn += 2
                    g.wait_ge(P.dsem, 16 * P.dn)
                    g.dma_start(wqkv_bf.ap()[128 * r:128 * r + 128, 4096:6144],
                                wtile[:, 0:2048]).then_inc(P.dsem, 16)
                    g.dma_start(wo_bf.ap()[128 * r:128 * r + 128, :],
                                wtile[:, 2048:4096]).then_inc(P.dsem, 16)
                    P.dn += 2
                    g.wait_ge(P.dsem, 16 * P.dn)

        ph12 = contextlib.ExitStack()
        P._scopes.append(ph12)
        zT = P.sbuf("zT", [128, 16 * 2048], BF16)
        ph1 = contextlib.ExitStack()
        psT = ph1.enter_context(nc.psum_tensor(f"psTl1a_{P.uid()}", [128, 1024], BF16))
        for ti in range(NT):
            with nc.Block() as b:
                @b.sync
                def _(sync):
                    P.ld(sync, xt[:], x_t[ti])
                    P.ldwait(sync)
            ln_stats(P, None, None, xt[:], scr[:], stats, 1.0 / D, 1e-5)
            with nc.Block() as b:
                @b.vector
                def _(v):
                    v = P.wrap(v, "v")
                    v.tensor_scalar(zb[:], xt[:], stats["negmu"][:], stats["rstd"][:],
                                    ALU.add, ALU.mult)
            for hf in range(2):
                with nc.Block() as b:
                    @b.tensor
                    def _(t):
                        for k in range(8):
                            kk = 8 * hf + k
                            t.transpose(psT[:, 128 * k:128 * k + 128],
                                        zb[:, 128 * kk:128 * kk + 128], sb_idb[:])
                with nc.Block() as b:
                    @b.vector
                    def _(v):
                        v = P.wrap(v, "v")
                        for k in range(8):
                            kk = 8 * hf + k
                            v.tensor_copy(zT[:, 2048 * kk + 128 * ti:2048 * kk + 128 * ti + 128],
                                          psT[:, 128 * k:128 * k + 128])
        ph1.close()
        P.dump("zT0", zT[:, 0:2048])

        ph2 = contextlib.ExitStack()
        psP = [ph2.enter_context(nc.psum_tensor(f"psP{i}_{P.uid()}", [128, 512], F32))
               for i in range(4)]
        psN = ph2.enter_context(nc.psum_tensor(f"psN_{P.uid()}", [1, 2048], F32))
        kevac = P.sbuf("kevac", [128, 2048], BF16)
        onescolf = P.sbuf("onescolf", [128, 1], F32)
        qn16 = P.sbuf("qn16", [1, 16], F32)
        with nc.Block() as b:
            @b.gpsimd
            def _(g):
                g.memset(onescolf[:], 1.0)
        for oc in range(32):
            with nc.Block() as b:
                @b.sync
                def _(sync):
                    P.ld(sync, ap3(wtile, 0, [[4096, 128], [128, 16], [1, 128]]),
                         ap3(wqkv_bf, 128 * oc, [[6144, 128], [128 * 6144, 16], [1, 128]]))
                    P.ldwait(sync)
                @b.tensor
                def _(t):
                    t.wait_ge(P.dsem, 16 * P.dn)
                    for k in range(16):
                        for tg in range(4):
                            t.matmul(psP[tg][:], wtile[:, 128 * k:128 * k + 128],
                                     zT[:, 2048 * k + 512 * tg:2048 * k + 512 * tg + 512],
                                     start=(k == 0), stop=(k == 15))
            with nc.Block() as b:
                @b.vector
                def _(v):
                    v = P.wrap(v, "v")
                    for tg in range(4):
                        if oc < 16:
                            v.tensor_scalar(qT[:, 2048 * oc + 512 * tg:2048 * oc + 512 * tg + 512],
                                            psP[tg][:], bq_sb[:, oc:oc + 1], None, ALU.add)
                        else:
                            v.tensor_scalar(kevac[:, 512 * tg:512 * tg + 512],
                                            psP[tg][:], bq_sb[:, oc:oc + 1], None, ALU.add)
            src = (qT[:, 2048 * oc:2048 * oc + 2048] if oc < 16 else kevac[:])
            with nc.Block() as b:
                @b.scalar
                def _(sc):
                    sc = P.wrap(sc, "s")
                    sc.activation(sq2[:], src, AF.Square)
            with nc.Block() as b:
                @b.tensor
                def _(t):
                    for tg in range(4):
                        t.matmul(psN[0:1, 512 * tg:512 * tg + 512], onescolf[:],
                                 sq2[:, 512 * tg:512 * tg + 512], start=True, stop=True)
            with nc.Block() as b:
                @b.vector
                def _(v):
                    v = P.wrap(v, "v")
                    v.tensor_reduce(m1[0:1, 0:1], psN[0:1, 0:2048], AX.X, ALU.max)
                    dst = qn16 if oc < 16 else km16
                    v.tensor_copy(dst[0:1, (oc % 16):(oc % 16) + 1], m1[0:1, 0:1])
                if oc >= 16:
                    @b.sync
                    def _(sync):
                        P.ld(sync, K_sh.ap()[128 * (oc - 16):128 * (oc - 16) + 128, :], kevac[:])
                        P.ldwait(sync)
            if oc == 0:
                P.dump("qT0", qT[:, 0:2048])
            if oc == 16:
                P.dump("kevac0", kevac[:])
        P.dump("qn16", qn16[0:1, :])
        P.dump("km16pre", km16[0:1, :])
        P.dump("qTfull", qT[:])
        ph2.close()

        ph2b = contextlib.ExitStack()
        psV = [ph2b.enter_context(nc.psum_tensor(f"psV{i}_{P.uid()}", [128, 512], F32))
               for i in range(8)]
        wvrow = P.sbuf("wvrow", [128, 2 * 2048], BF16)
        vevac = P.sbuf("vevac", [128, 2048], BF16)
        vqack = P.sem("vqack")
        vqd = P.sem("vqd")
        vvs = P.sem("vvs")
        vq_n = [0]
        for ttp in range(8):
            with nc.Block() as b:
                @b.sync
                def _(sync):
                    for k in range(16):
                        kg = 16 * ttp + k
                        if kg >= 2:
                            sync.wait_ge(vqack, kg - 1)
                        sync.dma_start(wvrow[:, 2048 * (k % 2):2048 * (k % 2) + 2048],
                                       wqkv_bf.ap()[128 * k:128 * k + 128, 4096:6144]
                                       ).then_inc(vqd, 16)
                @b.tensor
                def _(t):
                    for k in range(16):
                        kg = 16 * ttp + k
                        t.wait_ge(vqd, 16 * (kg + 1))
                        for tt in range(2):
                            for vc in range(4):
                                ins = t.matmul(psV[4 * tt + vc][:],
                                               zT[:, 2048 * k + 128 * (2 * ttp + tt):2048 * k + 128 * (2 * ttp + tt) + 128],
                                               wvrow[:, 2048 * (k % 2) + 512 * vc:2048 * (k % 2) + 512 * vc + 512],
                                               start=(k == 0), stop=(k == 15))
                        ins.then_inc(vqack, 1)
            for tt in range(2):
                with nc.Block() as b:
                    @b.vector
                    def _(v):
                        v = P.wrap(v, "v")
                        for vc in range(4):
                            ins = v.tensor_tensor(vevac[:, 512 * vc:512 * vc + 512],
                                                  psV[4 * tt + vc][:],
                                                  vb_sb[:, 512 * vc:512 * vc + 512], ALU.add)
                        ins.then_inc(vvs, 1)
                        vq_n[0] += 1
                    @b.sync
                    def _(sync):
                        sync.wait_ge(vvs, vq_n[0])
                        P.ld(sync, V_sh.ap()[128 * (2 * ttp + tt):128 * (2 * ttp + tt) + 128, :],
                             vevac[:])
                        P.ldwait(sync)
        ph2b.close()
        P._scopes.pop()
        ph12.close()

        with nc.Block() as b:
            @b.sync
            def _(sync):
                P.ld(sync, km_in.ap(), km16[0:1, :])
                P.ldwait(sync)
        csK = P.sem("agK"); csV = P.sem("agV"); csM = P.sem("arM")
        with nc.Block() as b:
            @b.gpsimd
            def _(g):
                g.collective_compute("AllGather", ALU.bypass, replica_groups=RG2x4,
                                     ins=[K_sh.ap().opt()], outs=[K_seq.ap().opt()]).then_inc(csK)
                g.wait_ge(csK, 1)
                g.collective_compute("AllGather", ALU.bypass, replica_groups=RG2x4,
                                     ins=[V_sh.ap().opt()], outs=[V_seq.ap().opt()]).then_inc(csV)
                g.wait_ge(csV, 1)
                g.collective_compute("AllReduce", ALU.max, replica_groups=RG2x4,
                                     ins=[km_in.ap().opt()], outs=[km_out.ap().opt()]).then_inc(csM)
                g.wait_ge(csM, 1)
        with nc.Block() as b:
            @b.sync
            def _(sync):
                P.ld(sync, km16[0:1, :], km_out.ap())
                P.ldwait(sync)
        ph2c = contextlib.ExitStack()
        psB = ph2c.enter_context(nc.psum_tensor(f"psB_{P.uid()}", [128, 16], F32))
        with nc.Block() as b:
            @b.scalar
            def _(sc):
                sc = P.wrap(sc, "s")
                sc.activation(qn16[0:1, :], qn16[0:1, :], AF.Sqrt)
                sc.activation(km16[0:1, :], km16[0:1, :], AF.Sqrt)
        with nc.Block() as b:
            @b.vector
            def _(v):
                v = P.wrap(v, "v")
                v.tensor_tensor(qn16[0:1, :], qn16[0:1, :], km16[0:1, :], ALU.mult)
                v.tensor_single_scalar(qn16[0:1, :], qn16[0:1, :], -SCL, ALU.mult)
        with nc.Block() as b:
            @b.tensor
            def _(t):
                t.matmul(psB[:], ones1f[0:1, :], qn16[0:1, :], start=True, stop=True)
        with nc.Block() as b:
            @b.vector
            def _(v):
                v = P.wrap(v, "v")
                v.tensor_copy(negM[:], psB[:])
        ph2c.close()
        P.dump("negM", negM[:])

        ph3 = contextlib.ExitStack()
        psS = [ph3.enter_context(nc.psum_tensor(f"psS{i}_{P.uid()}", [128, 512], F32))
               for i in range(2)]
        psO = [ph3.enter_context(nc.psum_tensor(f"psO{i}_{P.uid()}", [128, 512], F32))
               for i in range(4)]
        psL = ph3.enter_context(nc.psum_tensor(f"psL_{P.uid()}", [128, 512], F32))
        psLB = ph3.enter_context(nc.psum_tensor(f"psLB_{P.uid()}", [128, 512], F32))
        kvt = P.sbuf("kvt", [128, 4 * 128], BF16)
        pT = [P.sbuf(f"pT{i}", [128, 512], BF16) for i in range(2)]
        oT = P.sbuf("oT", [128, 16 * 2048], BF16)
        linv = P.sbuf("linv", [1, 2048], F32)
        lbc = P.sbuf("lbc", [128, 512], F32)
        NB = 64
        tsem = P.sem("fl_t")
        s2sem = P.sem("fl_s")
        osem2 = P.sem("fl_o")
        dsem2 = P.sem("fl_d")
        for h in range(16):
            with nc.Block() as b:
                @b.sync
                def _(sync):
                    for bb in range(NB):
                        gb = NB * h + bb
                        if gb >= 2:
                            sync.wait_ge(osem2, 4 * gb if getattr(P, "flash_safe", False)
                                         else 4 * (gb - 1))
                        rr, tl0 = bb // 16, 128 * (bb % 16)
                        sync.dma_start(kvt[:, 256 * (gb % 2):256 * (gb % 2) + 128],
                                       K_seq.ap()[2048 * rr + 128 * h:2048 * rr + 128 * h + 128,
                                                  tl0:tl0 + 128]).then_inc(dsem2, 16)
                        sync.dma_start(kvt[:, 256 * (gb % 2) + 128:256 * (gb % 2) + 256],
                                       V_seq.ap()[2048 * rr + tl0:2048 * rr + tl0 + 128,
                                                  128 * h:128 * h + 128]).then_inc(dsem2, 16)
                @b.tensor
                def _(t):
                    for bb in range(NB):
                        gb = NB * h + bb
                        t.wait_ge(dsem2, 32 * (gb + 1))
                        for qg in range(4):
                            i = 4 * gb + qg
                            if i >= 2:
                                t.wait_ge(s2sem, i - 1)
                            t.matmul(psS[i % 2][:], kvt[:, 256 * (gb % 2):256 * (gb % 2) + 128],
                                     qT[:, 2048 * h + 512 * qg:2048 * h + 512 * qg + 512],
                                     start=True, stop=True).then_inc(tsem, 1)
                        for qg in range(4):
                            i = 4 * gb + qg
                            t.wait_ge(s2sem, i + 1)
                            t.matmul(psO[qg][:], kvt[:, 256 * (gb % 2) + 128:256 * (gb % 2) + 256],
                                     pT[i % 2][:], start=(bb == 0), stop=(bb == NB - 1))
                            t.matmul(psL[32 * qg:32 * qg + 1, :], onescol[:], pT[i % 2][:],
                                     start=(bb == 0), stop=(bb == NB - 1),
                                     tile_position=(0, 32 * qg)).then_inc(osem2, 1)
                @b.scalar
                def _(sc):
                    for bb in range(NB):
                        gb = NB * h + bb
                        for qg in range(4):
                            i = 4 * gb + qg
                            sc.wait_ge(tsem, i + 1)
                            if i >= 2:
                                sc.wait_ge(osem2, i - 1)
                            sc.activation(pT[i % 2][:], psS[i % 2][:], AF.Exp,
                                          bias=negM[:, h:h + 1], scale=SCL).then_inc(s2sem, 1)
            with nc.Block() as b:
                @b.vector
                def _(v):
                    v = P.wrap(v, "v")
                    for qg in range(4):
                        v.reciprocal(linv[0:1, 512 * qg:512 * qg + 512],
                                     psL[32 * qg:32 * qg + 1, :])
            for qg in range(4):
                with nc.Block() as b:
                    @b.tensor
                    def _(t):
                        t.matmul(psLB[:], ones1f[0:1, :], linv[0:1, 512 * qg:512 * qg + 512],
                                 start=True, stop=True)
                with nc.Block() as b:
                    @b.vector
                    def _(v):
                        v = P.wrap(v, "v")
                        v.tensor_copy(lbc[:], psLB[:])
                        v.tensor_tensor(oT[:, 2048 * h + 512 * qg:2048 * h + 512 * qg + 512],
                                        psO[qg][:], lbc[:], ALU.mult)
            if h == 0:
                P.dump("linv0", linv[0:1, :])
                P.dump("oT0", oT[:, 0:2048])
        P.dump("oTfull", oT[:])
        if getattr(P, "dbg", False):
            kdbg = zb
            for nm, rr in [("Kseq_h5_r0", 0), ("Kseq_h5_r2", 2)]:
                with nc.Block() as b:
                    @b.sync
                    def _(sync):
                        P.ld(sync, kdbg[:], K_seq.ap()[2048 * rr + 128 * 5:2048 * rr + 128 * 5 + 128, :])
                        P.ldwait(sync)
                P.dump(nm, kdbg[:])
            with nc.Block() as b:
                @b.sync
                def _(sync):
                    P.ld(sync, kdbg[:], V_seq.ap()[2048 * 2 + 0:2048 * 2 + 128, :])
                    P.ldwait(sync)
            P.dump("Vseq_r2", kdbg[:])
            with nc.Block() as b:
                @b.sync
                def _(sync):
                    P.ld(sync, kdbg[:], K_sh.ap()[128 * 5:128 * 5 + 128, :])
                    P.ldwait(sync)
            P.dump("Ksh_h5", kdbg[:])
            for rw in range(4):
                with nc.Block() as b:
                    @b.sync
                    def _(sync):
                        P.ld(sync, kdbg[:], V_sh.ap()[512 * rw:512 * rw + 128, :])
                        P.ldwait(sync)
                P.dump(f"Vsh_r{rw}", kdbg[:])
        ph3.close()

        ph4 = contextlib.ExitStack()
        psY4 = [ph4.enter_context(nc.psum_tensor(f"psY4{i}_{P.uid()}", [128, 512], F32))
                for i in range(4)]
        psT4 = ph4.enter_context(nc.psum_tensor(f"psT4_{P.uid()}", [128, 1024], BF16))
        yTl = P.sbuf("yTl", [128, 512], BF16)
        ytokl = [P.sbuf(f"ytokl{j}", [128, 2048], BF16) for j in range(4)]
        for tg in range(4):
            for dc in range(16):
                with nc.Block() as b:
                    @b.sync
                    def _(sync):
                        P.ld(sync, ap3(wtile, 0, [[4096, 128], [128, 16], [1, 128]]),
                             ap3(wo_bf, 128 * dc, [[2048, 128], [128 * 2048, 16], [1, 128]]))
                        P.ldwait(sync)
                    @b.tensor
                    def _(t):
                        t.wait_ge(P.dsem, 16 * P.dn)
                        for e in range(16):
                            t.matmul(psY4[0][:], wtile[:, 128 * e:128 * e + 128],
                                     oT[:, 2048 * e + 512 * tg:2048 * e + 512 * tg + 512],
                                     start=(e == 0), stop=(e == 15))
                with nc.Block() as b:
                    @b.vector
                    def _(v):
                        v = P.wrap(v, "v")
                        v.tensor_scalar(yTl[:], psY4[0][:], bo_sb[:, dc:dc + 1], None, ALU.add)
                with nc.Block() as b:
                    @b.tensor
                    def _(t):
                        for j in range(4):
                            t.transpose(psT4[:, 128 * j:128 * j + 128],
                                        yTl[:, 128 * j:128 * j + 128], sb_idb[:])
                with nc.Block() as b:
                    @b.vector
                    def _(v):
                        v = P.wrap(v, "v")
                        for j in range(4):
                            v.tensor_copy(ytokl[j][:, 128 * dc:128 * dc + 128],
                                          psT4[:, 128 * j:128 * j + 128])
            for j in range(4):
                ti = 4 * tg + j
                with nc.Block() as b:
                    @b.sync
                    def _(sync):
                        P.ld(sync, xt[:], x_t[ti])
                        P.ldwait(sync)
                with nc.Block() as b:
                    @b.vector
                    def _(v):
                        v = P.wrap(v, "v")
                        v.tensor_tensor(scr[:], xt[:], ytokl[j][:], ALU.add)
                with nc.Block() as b:
                    @b.sync
                    def _(sync):
                        P.st(sync, y_t[ti], scr[:])
                        P.stwait(sync)
        ph4.close()



def build_l2ffn(P, x_in, y_out, wnp, tag):
    nc = P.nc
    import ml_dtypes
    P.need_consts([-1.0 / D, D * 1e-5, 0.5 * math.log(D), 0.5, -2.0, -1.0, 100.0])
    id128f_h = P.hconst("c_id128f", host_identity(128, np.float32))

    sw = wnp["select_weight"].astype(np.float32)
    md = wnp["select_mean_diff"].astype(np.float32)
    upd = wnp["lora_update"].astype(np.float32)
    dgn = wnp["lora_degen"].astype(np.float32)
    Wcat = np.zeros((D, 136), np.float32)
    Wcat[:, 0:64] = sw.transpose(1, 0, 2).reshape(D, 64)
    Wcat[:, 64:72] = md.T
    Wcat[:, 72:136] = upd.transpose(1, 0, 2).reshape(D, 64)
    mde_cat = np.zeros((D, 136), np.float32)
    mde_cat[:, 0:64] = np.repeat(md, 8, axis=0).T.reshape(D, 64)
    mde_cat[:, 64:72] = md.T
    mde_cat[:, 72:136] = np.repeat(wnp["debias_w"].astype(np.float32), 8, axis=0).T.reshape(D, 64)
    B_cat = np.zeros((72, D), np.float32)
    B_cat[0:64, :] = dgn.transpose(0, 2, 1).reshape(64, D)
    B_cat[64:72, :] = wnp["bias_w"].astype(np.float32)
    Wcat_h = P.hconst(f"w_wcat_{tag}", Wcat)
    mde_h = P.hconst(f"w_mde_{tag}", mde_cat)
    bcat_h = P.hconst(f"w_bcat_{tag}", B_cat)
    slc_h = P.hconst(f"w_slc_{tag}", np.ascontiguousarray(
        np.broadcast_to(wnp["imp_slope"].astype(np.float32)[None, :], (128, 8))))
    ctr_h = P.hconst(f"w_ctr_{tag}", np.ascontiguousarray(
        np.broadcast_to(wnp["imp_center"].astype(np.float32)[None, :], (128, 8))))

    x_t = x_in.ap().rearrange("(n p) d -> n p d", p=128)
    y_t = y_out.ap().rearrange("(n p) d -> n p d", p=128)

    with P.scope():
        idf = P.sbuf("idf", [128, 128], F32)
        xt = P.sbuf("xt", [128, D], F32)
        zt = P.sbuf("zt", [128, D], F32)
        scr = P.sbuf("scr", [128, D], F32)
        zTt = P.sbuf("zTt", [128, D], F32)
        stats = dict(negmu=P.sbuf("negmu", [128, 1], F32), rstd=P.sbuf("rstd", [128, 1], F32),
                     s1=P.sbuf("s1", [128, 1], F32), ssq=P.sbuf("ssq", [128, 1], F32))
        wcat_sb = P.sbuf("wcat", [128, 16 * 136], F32)
        wtmp = P.sbuf("wtmp2", [128, 136], F32)
        bcat_sb = P.sbuf("bcat", [72, D], F32)
        bc136 = P.sbuf("bc136", [128, 136], F32)
        slc_sb = P.sbuf("slc", [128, 8], F32)
        ctr_sb = P.sbuf("ctr", [128, 8], F32)
        onescolf = P.sbuf("onescolf2", [128, 1], F32)
        ones1f = P.sbuf("ones1f2", [1, 128], F32)
        ones8 = P.sbuf("ones82", [128, 8], F32)
        sqz = P.sbuf("sqz", [128, 1], F32)
        a136 = P.sbuf("a136", [128, 136], F32)
        num8 = P.sbuf("num8", [128, 8], F32)
        den8 = P.sbuf("den8", [128, 8], F32)
        gate8 = P.sbuf("gate8", [128, 8], F32)
        oh8 = P.sbuf("oh8", [128, 8], F32)
        d8 = P.sbuf("d8", [128, 8], F32)
        pfx8 = P.sbuf("pfx8", [128, 8], F32)
        gmax = P.sbuf("gmax", [128, 1], F32)
        mod8 = P.sbuf("mod8", [128, 8], F32)
        Maug = P.sbuf("Maug", [128, 72], F32)
        MT = P.sbuf("MT", [72, 128], F32)
        sq64 = P.sbuf("sq64", [128, 64], F32)

        psA = P.psum("psA2", [128, 136], F32)
        psR = P.psum("psR2", [1, 136], F32)
        psB2 = P.psum("psB22", [128, 136], F32)
        psMT = P.psum("psMT2", [72, 128], F32)
        psMB = [P.psum(f"psMB{i}", [128, 512], F32) for i in range(2)]
        psTz = P.psum("psTz2", [128, 1024], F32)

        with nc.Block() as b:
            @b.sync
            def _(sync):
                P.ld(sync, idf[:], id128f_h.ap())
                P.ld(sync, ap3(wcat_sb, 0, [[16 * 136, 128], [136, 16], [1, 136]]),
                     ap3(Wcat_h, 0, [[136, 128], [128 * 136, 16], [1, 136]]))
                P.ld(sync, bcat_sb[:], bcat_h.ap())
                P.ld(sync, slc_sb[:], slc_h.ap())
                P.ld(sync, ctr_sb[:], ctr_h.ap())
                P.ldwait(sync)
            @b.gpsimd
            def _(g):
                P.emit_consts(g)
                g.memset(onescolf[:], 1.0)
                g.memset(ones1f[:], 1.0)
                g.memset(ones8[:], 1.0)
        for k in range(16):
            with nc.Block() as b:
                @b.sync
                def _(sync):
                    P.ld(sync, wtmp[:], mde_h.ap()[128 * k:128 * k + 128, :])
                    P.ldwait(sync)
            with nc.Block() as b:
                @b.vector
                def _(v):
                    v = P.wrap(v, "v")
                    v.tensor_tensor(wtmp[:], wtmp[:], wcat_sb[:, 136 * k:136 * k + 136],
                                    ALU.mult)
            with nc.Block() as b:
                @b.tensor
                def _(t):
                    t.matmul(psR[:], onescolf[:], wtmp[:], start=(k == 0), stop=(k == 15))
        with nc.Block() as b:
            @b.vector
            def _(v):
                v = P.wrap(v, "v")
                v.tensor_copy(wtmp[0:1, :], psR[0:1, :])
        with nc.Block() as b:
            @b.tensor
            def _(t):
                t.matmul(psB2[:], ones1f[0:1, :], wtmp[0:1, :], start=True, stop=True)
        with nc.Block() as b:
            @b.vector
            def _(v):
                v = P.wrap(v, "v")
                v.tensor_copy(bc136[:], psB2[:])

        for ti in range(NT):
            with nc.Block() as b:
                @b.sync
                def _(sync):
                    P.ld(sync, xt[:], x_t[ti])
                    P.ldwait(sync)
            ln_stats(P, None, None, xt[:], scr[:], stats, 1.0 / D, 1e-5)
            with nc.Block() as b:
                @b.vector
                def _(v):
                    v = P.wrap(v, "v")
                    v.tensor_scalar(zt[:], xt[:], stats["negmu"][:], stats["rstd"][:],
                                    ALU.add, ALU.mult)
            for hf in range(2):
                with nc.Block() as b:
                    @b.tensor
                    def _(t):
                        for k in range(8):
                            kk = 8 * hf + k
                            t.transpose(psTz[:, 128 * k:128 * k + 128],
                                        zt[:, 128 * kk:128 * kk + 128], idf[:])
                with nc.Block() as b:
                    @b.vector
                    def _(v):
                        v = P.wrap(v, "v")
                        v.tensor_copy(zTt[:, 1024 * hf:1024 * hf + 1024], psTz[:])
            with nc.Block() as b:
                @b.scalar
                def _(sc):
                    sc = P.wrap(sc, "s")
                    sc.activation(scr[:], zt[:], AF.Square, bias=0.0, scale=1.0,
                                  accum_out=sqz[:])
                @b.tensor
                def _(t):
                    for k in range(16):
                        t.matmul(psA[:], zTt[:, 128 * k:128 * k + 128],
                                 wcat_sb[:, 136 * k:136 * k + 136],
                                 start=(k == 0), stop=(k == 15))
            with nc.Block() as b:
                @b.vector
                def _(v):
                    v = P.wrap(v, "v")
                    v.tensor_copy(a136[:], psA[:])
                    v.tensor_tensor(sq64[:], a136[:, 0:64], bc136[:, 0:64], ALU.subtract)
            with nc.Block() as b:
                @b.scalar
                def _(sc):
                    sc = P.wrap(sc, "s")
                    sc.activation(sq64[:], sq64[:], AF.Square)
            with nc.Block() as b:
                @b.vector
                def _(v):
                    v = P.wrap(v, "v")
                    v.tensor_reduce(num8[:], ap3(sq64, 0, [[64, 128], [8, 8], [1, 8]]),
                                    AX.X, ALU.add)
                    v.tensor_single_scalar(den8[:], a136[:, 64:72], -2.0, ALU.mult)
                    v.tensor_scalar(den8[:], den8[:], sqz[:], None, ALU.add)
                    v.tensor_tensor(den8[:], den8[:], bc136[:, 64:72], ALU.add)
                    v.reciprocal(den8[:], den8[:])
                    v.tensor_tensor(num8[:], num8[:], den8[:], ALU.mult)
                    v.tensor_tensor(gate8[:], num8[:], ctr_sb[:], ALU.subtract)
                    v.tensor_tensor(gate8[:], gate8[:], slc_sb[:], ALU.mult)
            with nc.Block() as b:
                @b.scalar
                def _(sc):
                    sc = P.wrap(sc, "s")
                    sc.activation(gate8[:], gate8[:], AF.Sigmoid)
            with nc.Block() as b:
                @b.vector
                def _(v):
                    v = P.wrap(v, "v")
                    v.tensor_reduce(gmax[:], gate8[:], AX.X, ALU.max)
                    v.tensor_scalar(d8[:], gate8[:], gmax[:], None, ALU.subtract)
                    v.tensor_single_scalar(d8[:], d8[:], 0.0, ALU.is_equal)
                    v.tensor_tensor_scan(pfx8[:], ones8[:], d8[:], 0.0, ALU.mult, ALU.add)
                    v.tensor_single_scalar(pfx8[:], pfx8[:], 1.0, ALU.is_equal)
                    v.tensor_tensor(oh8[:], d8[:], pfx8[:], ALU.mult)
                    v.tensor_tensor(sq64[:], a136[:, 72:136], bc136[:, 72:136], ALU.subtract)
                    v.tensor_scalar(mod8[:], sq64[:, 0:8], oh8[:, 0:1], None, ALU.mult)
                    for n in range(1, 8):
                        v.scalar_tensor_tensor(mod8[:], sq64[:, 8 * n:8 * n + 8],
                                               oh8[:, n:n + 1], mod8[:], ALU.mult, ALU.add)
                    for n in range(8):
                        v.tensor_scalar(Maug[:, 8 * n:8 * n + 8], mod8[:],
                                        oh8[:, n:n + 1], None, ALU.mult)
                    v.tensor_copy(Maug[:, 64:72], oh8[:])
            with nc.Block() as b:
                @b.tensor
                def _(t):
                    t.transpose(psMT[:], Maug[:], idf[:])
            with nc.Block() as b:
                @b.vector
                def _(v):
                    v = P.wrap(v, "v")
                    v.tensor_copy(MT[:], psMT[:])
            for c4 in range(4):
                with nc.Block() as b:
                    @b.tensor
                    def _(t):
                        t.matmul(psMB[0][:], MT[:], bcat_sb[:, 512 * c4:512 * c4 + 512],
                                 start=True, stop=True)
                with nc.Block() as b:
                    @b.vector
                    def _(v):
                        v = P.wrap(v, "v")
                        sl = slice(512 * c4, 512 * c4 + 512)
                        v.tensor_tensor(scr[:, sl], psMB[0][:], zt[:, sl], ALU.subtract)
                        v.tensor_scalar(scr[:, sl], scr[:, sl], gmax[:], None, ALU.mult)
                        v.tensor_tensor(scr[:, sl], scr[:, sl], zt[:, sl], ALU.add)
                        v.scalar_tensor_tensor(scr[:, sl], scr[:, sl], 0.5, xt[:, sl],
                                               ALU.mult, ALU.add)
            with nc.Block() as b:
                @b.sync
                def _(sync):
                    P.st(sync, y_t[ti], scr[:])
                    P.stwait(sync)



def host_rope_tables(pos0, ntok):
    dh = 128
    inv_freq = 1.0 / 10000.0 ** (np.arange(0, dh, 2, dtype=np.float64) / dh)
    pos = np.arange(pos0, pos0 + ntok, dtype=np.float64)
    fr = pos[:, None] * inv_freq[None, :]
    cos = np.concatenate([np.cos(fr), np.cos(fr)], axis=-1)
    sin = np.concatenate([np.sin(fr), np.sin(fr)], axis=-1)
    return cos.astype(np.float32), sin.astype(np.float32)


def build_l2attn(P, x_in, y_out, wnp, tag):
    nc = P.nc
    import ml_dtypes
    CH = 128
    NCH2 = 8192 // CH
    lam = np.exp2(-8.0 * np.arange(1, 17) / 16.0)
    id128b = P.hconst("c_id128b", host_identity(128, np.float32).astype(ml_dtypes.bfloat16))
    P.need_consts([-1.0 / D, D * 1e-5, 0.5 * math.log(D)])

    wqT_h = P.hconst(f"w_wqT_{tag}", np.ascontiguousarray(wnp["wq"].T).astype(np.float32))
    wkT_h = P.hconst(f"w_wkT_{tag}", np.ascontiguousarray(wnp["wk"].T).astype(np.float32))
    wvT_h = P.hconst(f"w_wvT_{tag}", np.ascontiguousarray(wnp["wv"].T).astype(np.float32))
    woT_h = P.hconst(f"w_woT_{tag}", np.ascontiguousarray(wnp["wo"].T).astype(np.float32))
    cosF, sinF = [], []
    cosT, sinT = [], []
    for c in range(8):
        pos0 = 2048 * (c % 4)
        co, si = host_rope_tables(pos0, 2048)
        cosF.append(np.ascontiguousarray(co.T))
        ss_ = si.T.copy()
        ss_[0:64, :] *= -1.0
        sinF.append(np.ascontiguousarray(ss_))
        cosT.append(np.ascontiguousarray(co.reshape(16, 128, 128).transpose(1, 0, 2)
                                         .reshape(128, 16 * 128)))
        sinT.append(np.ascontiguousarray(si.reshape(16, 128, 128).transpose(1, 0, 2)
                                         .reshape(128, 16 * 128)))
    cosF_h = P.hconst(f"c_cosF_{tag}", cosF)
    sinF_h = P.hconst(f"c_sinF_{tag}", sinF)
    cosT_h = P.hconst(f"c_cosT_{tag}", cosT)
    sinT_h = P.hconst(f"c_sinT_{tag}", sinT)
    r = np.arange(CH, dtype=np.float64)
    diagcat = np.zeros((128, 16 * 128), np.float32)
    for h in range(16):
        dm = np.where(r[None, :] - r[:, None] >= 0,
                      np.exp(-lam[h] * (r[None, :] - r[:, None])), 0.0)
        diagcat[:, 128 * h:128 * h + 128] = dm
    diag_h = P.hconst(f"c_diag_{tag}", diagcat)
    qdec = np.zeros((128, 16 * 128), np.float32)
    for h in range(16):
        qdec[:, 128 * h:128 * h + 128] = np.exp(-lam[h] * (r[None, :] + 1.0))
    qdec_h = P.hconst(f"c_qdec_{tag}", qdec)
    kdb = np.zeros((128, 16), np.float32)
    for h in range(16):
        kdb[:, h] = np.exp(-lam[h] * (CH - 1.0 - r))
    kdb_h = P.hconst(f"c_kdb_{tag}", kdb)
    blkb = np.zeros((128, 16 * 128), np.float32)
    for h in range(16):
        blkb[:, 128 * h:128 * h + 128] = np.exp(-lam[h] * CH)
    blkb_h = P.hconst(f"c_blkb_{tag}", blkb)
    oidx = []
    for c in range(8):
        pos0 = 2048 * (c % 4)
        a = np.zeros((128, 64), np.int32)
        for ec in range(16):
            for tg in range(4):
                a[:, 4 * ec + tg] = (128 * ec + np.arange(128)) * 16 + pos0 // 512 + tg
        oidx.append(np.ascontiguousarray(a))
    oidx_h = P.hconst(f"c_oidx_{tag}", oidx)

    qsh = P.dram(f"qsh_{tag}", [D, TOK], BF16)
    ksh = P.dram(f"ksh_{tag}", [512, TOK], BF16)
    ktsh = P.dram(f"ktsh_{tag}", [TOK, 512], F32)
    vsh = P.dram(f"vsh_{tag}", [TOK, 512], F32)
    q_seq = P.dram(f"qseq_{tag}", [4 * D, TOK], BF16)
    k_seq = P.dram(f"kseq_{tag}", [4 * 512, TOK], BF16)
    kt_seq = P.dram(f"ktseq_{tag}", [4 * TOK, 512], F32)
    v_seq = P.dram(f"vseq_{tag}", [4 * TOK, 512], F32)
    oT_seq = P.dram(f"oTseq_{tag}", [D, 8192], BF16)
    wq_bf = P.dram(f"wqbf_{tag}", [D, D], BF16)
    wo_bf = P.dram(f"wobf_{tag}", [D, D], BF16)

    x_t = x_in.ap().rearrange("(n p) d -> n p d", p=128)
    y_t = y_out.ap().rearrange("(n p) d -> n p d", p=128)

    with P.scope():
        sb_idb = P.sbuf("idb", [128, 128], BF16)
        xt = P.sbuf("xt", [128, D], F32)
        scr = P.sbuf("scr", [128, D], F32)
        zb = P.sbuf("zb", [128, D], BF16)
        stats = dict(negmu=P.sbuf("negmu", [128, 1], F32), rstd=P.sbuf("rstd", [128, 1], F32),
                     s1=P.sbuf("s1", [128, 1], F32), ssq=P.sbuf("ssq", [128, 1], F32))
        wtile = P.sbuf("wtile2", [128, 2 * 2048], BF16)
        oidx_sb = P.sbuf("oidx", [128, 64], mybir.dt.int32)
        ph_mid = contextlib.ExitStack()
        P._scopes.append(ph_mid)
        wkv_sb = P.sbuf("wkv", [128, 16 * 1024], BF16)
        cosF_sb = P.sbuf("cosF", [128, 2048], F32)
        sinF_sb = P.sbuf("sinF", [128, 2048], F32)
        cosT_sb = P.sbuf("cosT", [128, 2048], F32)
        sinT_sb = P.sbuf("sinT", [128, 2048], F32)
        diag_sb = P.sbuf("diag", [128, 2048], F32)
        qdec_sb = P.sbuf("qdec", [128, 2048], F32)
        kdb_sb = P.sbuf("kdb", [128, 16], F32)
        blkb_sb = P.sbuf("blkb", [128, 2048], F32)
        rt1 = P.sbuf("rt1", [128, 2048], F32)
        rt2 = P.sbuf("rt2", [128, 2048], F32)

        with nc.Block() as b:
            @b.sync
            def _(sync):
                P.ld(sync, sb_idb[:], id128b.ap())
                P.ld(sync, cosF_sb[:], cosF_h.ap())
                P.ld(sync, sinF_sb[:], sinF_h.ap())
                P.ld(sync, cosT_sb[:], cosT_h.ap())
                P.ld(sync, sinT_sb[:], sinT_h.ap())
                P.ld(sync, diag_sb[:], diag_h.ap())
                P.ld(sync, qdec_sb[:], qdec_h.ap())
                P.ld(sync, kdb_sb[:], kdb_h.ap())
                P.ld(sync, blkb_sb[:], blkb_h.ap())
                P.ld(sync, oidx_sb[:], oidx_h.ap())
                P.ldwait(sync)
            @b.gpsimd
            def _(g):
                P.emit_consts(g)
                for r_ in range(16):
                    g.dma_start(wtile[:, 0:2048],
                                wqT_h.ap()[128 * r_:128 * r_ + 128, :]).then_inc(P.dsem, 16)
                    g.dma_start(wtile[:, 2048:4096],
                                woT_h.ap()[128 * r_:128 * r_ + 128, :]).then_inc(P.dsem, 16)
                    P.dn += 2
                    g.wait_ge(P.dsem, 16 * P.dn)
                    g.dma_start(wq_bf.ap()[128 * r_:128 * r_ + 128, :],
                                wtile[:, 0:2048]).then_inc(P.dsem, 16)
                    g.dma_start(wo_bf.ap()[128 * r_:128 * r_ + 128, :],
                                wtile[:, 2048:4096]).then_inc(P.dsem, 16)
                    P.dn += 2
                    g.wait_ge(P.dsem, 16 * P.dn)
                    g.dma_start(wkv_sb[:, 1024 * r_:1024 * r_ + 512],
                                wkT_h.ap()[128 * r_:128 * r_ + 128, :]).then_inc(P.dsem, 16)
                    g.dma_start(wkv_sb[:, 1024 * r_ + 512:1024 * r_ + 1024],
                                wvT_h.ap()[128 * r_:128 * r_ + 128, :]).then_inc(P.dsem, 16)
                    P.dn += 2
                    g.wait_ge(P.dsem, 16 * P.dn)

        ph1 = contextlib.ExitStack()
        P._scopes.append(ph1)
        zT = P.sbuf("zT2", [128, 16 * 2048], BF16)
        ktstage = P.sbuf("ktstage", [128, 2048], F32)
        idf2 = P.sbuf("idf2", [128, 128], F32)
        psT = ph1.enter_context(nc.psum_tensor(f"psTl2_{P.uid()}", [128, 1024], BF16))
        psTf = ph1.enter_context(nc.psum_tensor(f"psTfl2_{P.uid()}", [128, 1024], F32))
        id128f_h2 = P.hconst("c_id128f", host_identity(128, np.float32))
        with nc.Block() as b:
            @b.sync
            def _(sync):
                P.ld(sync, idf2[:], id128f_h2.ap())
                P.ldwait(sync)
        for ti in range(NT):
            with nc.Block() as b:
                @b.sync
                def _(sync):
                    P.ld(sync, xt[:], x_t[ti])
                    P.ldwait(sync)
            ln_stats(P, None, None, xt[:], scr[:], stats, 1.0 / D, 1e-5)
            with nc.Block() as b:
                @b.vector
                def _(v):
                    v = P.wrap(v, "v")
                    v.tensor_scalar(zb[:], xt[:], stats["negmu"][:], stats["rstd"][:],
                                    ALU.add, ALU.mult)
            for hf in range(2):
                with nc.Block() as b:
                    @b.tensor
                    def _(t):
                        for k in range(8):
                            kk = 8 * hf + k
                            t.transpose(psT[:, 128 * k:128 * k + 128],
                                        zb[:, 128 * kk:128 * kk + 128], sb_idb[:])
                with nc.Block() as b:
                    @b.vector
                    def _(v):
                        v = P.wrap(v, "v")
                        for k in range(8):
                            kk = 8 * hf + k
                            v.tensor_copy(zT[:, 2048 * kk + 128 * ti:2048 * kk + 128 * ti + 128],
                                          psT[:, 128 * k:128 * k + 128])

        psP = [ph1.enter_context(nc.psum_tensor(f"psPl2{i}_{P.uid()}", [128, 512], F32))
               for i in range(4)]
        for oc in range(20):
            with nc.Block() as b:
                @b.sync
                def _(sync):
                    if oc < 16:
                        P.ld(sync, ap3(wtile, 0, [[4096, 128], [128, 16], [1, 128]]),
                             ap3(wq_bf, 128 * oc, [[2048, 128], [128 * 2048, 16], [1, 128]]))
                        P.ldwait(sync)
                @b.tensor
                def _(t):
                    t.wait_ge(P.dsem, 16 * P.dn)
                    for k in range(16):
                        if oc < 16:
                            lh = wtile[:, 128 * k:128 * k + 128]
                        else:
                            lh = wkv_sb[:, 1024 * k + 128 * (oc - 16):1024 * k + 128 * (oc - 16) + 128]
                        for tg in range(4):
                            t.matmul(psP[tg][:], lh,
                                     zT[:, 2048 * k + 512 * tg:2048 * k + 512 * tg + 512],
                                     start=(k == 0), stop=(k == 15))
            with nc.Block() as b:
                @b.vector
                def _(v):
                    v = P.wrap(v, "v")
                    for tg in range(4):
                        v.tensor_copy(rt1[:, 512 * tg:512 * tg + 512], psP[tg][:])
            with nc.Block() as b:
                @b.sync
                def _(sync):
                    P.ld(sync, rt2[0:64, :], rt1[64:128, :])
                    P.ld(sync, rt2[64:128, :], rt1[0:64, :])
                    P.ldwait(sync)
            with nc.Block() as b:
                @b.vector
                def _(v):
                    v = P.wrap(v, "v")
                    v.tensor_tensor(rt1[:], rt1[:], cosF_sb[:], ALU.mult)
                    v.tensor_tensor(rt2[:], rt2[:], sinF_sb[:], ALU.mult)
                    v.tensor_tensor(rt1[:], rt1[:], rt2[:], ALU.add)
            if oc == 0:
                P.dump("q0fm", rt1[:])
            if oc == 16:
                P.dump("k0fm", rt1[:])
            with nc.Block() as b:
                @b.gpsimd
                def _(g):
                    if oc < 16:
                        g.dma_start(qsh.ap()[128 * oc:128 * oc + 128, :],
                                    rt1[:]).then_inc(P.dsem, 16)
                    else:
                        g.dma_start(ksh.ap()[128 * (oc - 16):128 * (oc - 16) + 128, :],
                                    rt1[:]).then_inc(P.dsem, 16)
                    P.dn += 1
                    g.wait_ge(P.dsem, 16 * P.dn)
            if oc >= 16:
                kh = oc - 16
                for hf in range(2):
                    with nc.Block() as b:
                        @b.tensor
                        def _(t):
                            for j in range(8):
                                jj = 8 * hf + j
                                t.transpose(psTf[:, 128 * j:128 * j + 128],
                                            rt1[:, 128 * jj:128 * jj + 128], idf2[:])
                    with nc.Block() as b:
                        @b.vector
                        def _(v):
                            v = P.wrap(v, "v")
                            v.tensor_copy(ktstage[:, 1024 * hf:1024 * hf + 1024], psTf[:])
                with nc.Block() as b:
                    @b.sync
                    def _(sync):
                        P.ld(sync, ap3(ktsh, 128 * kh, [[512, 128], [512 * 128, 16], [1, 128]]),
                             ap3(ktstage, 0, [[2048, 128], [128, 16], [1, 128]]))
                        P.ldwait(sync)
        for ttp in range(8):
            with nc.Block() as b:
                @b.tensor
                def _(t):
                    for k in range(16):
                        for tt in range(2):
                            t.matmul(psP[2 * tt][:] if False else psP[tt][:],
                                     zT[:, 2048 * k + 128 * (2 * ttp + tt):2048 * k + 128 * (2 * ttp + tt) + 128],
                                     wkv_sb[:, 1024 * k + 512:1024 * k + 1024],
                                     start=(k == 0), stop=(k == 15))
            with nc.Block() as b:
                @b.vector
                def _(v):
                    v = P.wrap(v, "v")
                    for tt in range(2):
                        v.tensor_copy(ktstage[:, 512 * tt:512 * tt + 512], psP[tt][:])
            with nc.Block() as b:
                @b.sync
                def _(sync):
                    P.ld(sync, vsh.ap()[256 * ttp:256 * ttp + 128, :], ktstage[:, 0:512])
                    P.ld(sync, vsh.ap()[256 * ttp + 128:256 * ttp + 256, :], ktstage[:, 512:1024])
                    P.ldwait(sync)
        ph1.close()
        P._scopes.pop()

        for nm, a, o in [("agq2", qsh, q_seq), ("agk2", ksh, k_seq),
                         ("agkt2", ktsh, kt_seq), ("agv2", vsh, v_seq)]:
            cs = P.sem(nm)
            with nc.Block() as b:
                @b.gpsimd
                def _(g):
                    g.collective_compute("AllGather", ALU.bypass, replica_groups=RG2x4,
                                         ins=[a.ap().opt()], outs=[o.ap().opt()]).then_inc(cs)
                    g.wait_ge(cs, 1)

        ph4 = contextlib.ExitStack()
        psS4 = ph4.enter_context(nc.psum_tensor(f"psS4_{P.uid()}", [128, 512], F32))
        psO4 = ph4.enter_context(nc.psum_tensor(f"psO4_{P.uid()}", [128, 512], F32))
        psKV = ph4.enter_context(nc.psum_tensor(f"psKV_{P.uid()}", [128, 512], F32))
        kvt2 = P.sbuf("kvt2", [128, 2 * 5 * 128], BF16)
        kvf2 = P.sbuf("kvf2", [128, 2 * 2 * 128], F32)
        kvcat = P.sbuf("kvcat", [128, 512], F32)
        kv2 = P.sbuf("kv2", [128, 512], F32)
        qp4 = P.sbuf("qp4", [128, 512], F32)
        ktd4 = P.sbuf("ktd4", [128, 512], F32)
        pT4 = P.sbuf("pT4", [128, 512], F32)
        oT4 = P.sbuf("oT4", [128, 4 * 1024], BF16)
        if getattr(P, "dbg", False):
            P.dbg_s = P.sbuf("dbgs", [128, 512], F32)
            P.dbg_p = P.sbuf("dbgp", [128, 512], F32)
            P.dbg_q = P.sbuf("dbgq", [128, 128], F32)
            P.dbg_v = P.sbuf("dbgv", [128, 128], F32)
        l2t = P.sem("l2t")
        l2v = P.sem("l2v")
        l2d = P.sem("l2d")
        l2f = P.sem("l2f")
        nflush = [0]
        for hg in range(4):
            with nc.Block() as b:
                @b.gpsimd
                def _(g):
                    g.memset(kvcat[:], 0.0)
            with nc.Block() as b:
                @b.sync
                def _(sync):
                    for c in range(NCH2):
                        gb = NCH2 * hg + c
                        par = 640 * (gb % 2)
                        if gb >= 2:
                            sync.wait_ge(l2v, 4 * (gb - 2) + 4)
                            sync.wait_ge(l2t, 3 * (gb - 2) + 3)
                        rr, tl0 = c // 16, 128 * (c % 16)
                        parf = 256 * (gb % 2)
                        sync.dma_start(kvt2[:, par:par + 128],
                                       k_seq.ap()[512 * rr + 128 * hg:512 * rr + 128 * hg + 128,
                                                  tl0:tl0 + 128]).then_inc(l2d, 16)
                        sync.dma_start(kvf2[:, parf:parf + 128],
                                       kt_seq.ap()[2048 * rr + tl0:2048 * rr + tl0 + 128,
                                                   128 * hg:128 * hg + 128]).then_inc(l2d, 16)
                        sync.dma_start(kvf2[:, parf + 128:parf + 256],
                                       v_seq.ap()[2048 * rr + tl0:2048 * rr + tl0 + 128,
                                                  128 * hg:128 * hg + 128]).then_inc(l2d, 16)
                        for hi in range(4):
                            h = 4 * hg + hi
                            sync.dma_start(kvt2[:, par + 128 + 128 * hi:par + 256 + 128 * hi],
                                           q_seq.ap()[2048 * rr + 128 * h:2048 * rr + 128 * h + 128,
                                                      tl0:tl0 + 128]).then_inc(l2d, 16)
                @b.vector
                def _(v):
                    for c in range(NCH2):
                        gb = NCH2 * hg + c
                        par = 640 * (gb % 2)
                        parf = 256 * (gb % 2)
                        if c % 8 == 0 and (8 * hg + c // 8) > 0:
                            v.wait_ge(l2f, 64 * (8 * hg + c // 8))
                        v.wait_ge(l2d, 16 * 7 * (gb + 1))
                        for hi in range(4):
                            h = 4 * hg + hi
                            v.tensor_tensor(qp4[:, 128 * hi:128 * hi + 128],
                                            kvt2[:, par + 128 + 128 * hi:par + 256 + 128 * hi],
                                            qdec_sb[:, 128 * h:128 * h + 128], ALU.mult)
                        for hi in range(4):
                            h = 4 * hg + hi
                            ins = v.tensor_scalar(ktd4[:, 128 * hi:128 * hi + 128],
                                                  kvf2[:, parf:parf + 128],
                                                  kdb_sb[:, h:h + 1], None, ALU.mult)
                        ins.then_inc(l2v, 1)
                        v.wait_ge(l2t, 3 * gb + 1)
                        if gb == 0 and getattr(P, "dbg", False):
                            v.tensor_copy(P.dbg_s[:], psS4[:])
                        v.tensor_tensor(pT4[:], psS4[:], diag_sb[:, 512 * hg:512 * hg + 512],
                                        ALU.mult).then_inc(l2v, 1)
                        if gb == 0 and getattr(P, "dbg", False):
                            v.tensor_copy(P.dbg_p[:], pT4[:])
                            v.tensor_copy(P.dbg_q[:], kvt2[:, par + 128:par + 256])
                            v.tensor_copy(P.dbg_v[:], kvf2[:, parf + 128:parf + 256])
                        v.wait_ge(l2t, 3 * gb + 2)
                        for hi in range(4):
                            ins = v.tensor_copy(
                                oT4[:, 1024 * hi + 128 * (c % 8):1024 * hi + 128 * (c % 8) + 128],
                                psO4[:, 128 * hi:128 * hi + 128])
                        ins.then_inc(l2v, 1)
                        v.wait_ge(l2t, 3 * gb + 3)
                        v.tensor_tensor(kv2[:], kvcat[:], blkb_sb[:, 512 * hg:512 * hg + 512],
                                        ALU.mult)
                        v.tensor_tensor(kvcat[:], kv2[:], psKV[:], ALU.add).then_inc(l2v, 1)
                @b.tensor
                def _(t):
                    for c in range(NCH2):
                        gb = NCH2 * hg + c
                        par = 640 * (gb % 2)
                        parf = 256 * (gb % 2)
                        t.wait_ge(l2v, 4 * gb + 1)
                        for hi in range(4):
                            ins = t.matmul(psS4[:, 128 * hi:128 * hi + 128],
                                           kvt2[:, par:par + 128],
                                           kvt2[:, par + 128 + 128 * hi:par + 256 + 128 * hi],
                                           start=True, stop=True)
                        ins.then_inc(l2t, 1)
                        t.wait_ge(l2v, 4 * gb + 2)
                        for hi in range(4):
                            t.matmul(psO4[:, 128 * hi:128 * hi + 128],
                                     kvf2[:, parf + 128:parf + 256],
                                     pT4[:, 128 * hi:128 * hi + 128],
                                     start=True, stop=False)
                            ins = t.matmul(psO4[:, 128 * hi:128 * hi + 128],
                                           kvcat[:, 128 * hi:128 * hi + 128],
                                           qp4[:, 128 * hi:128 * hi + 128],
                                           start=False, stop=True)
                        ins.then_inc(l2t, 1)
                        for hi in range(4):
                            ins = t.matmul(psKV[:, 128 * hi:128 * hi + 128],
                                           ktd4[:, 128 * hi:128 * hi + 128],
                                           kvf2[:, parf + 128:parf + 256],
                                           start=True, stop=True)
                        ins.then_inc(l2t, 1)
                @b.gpsimd
                def _(g):
                    for c in range(NCH2):
                        gb = NCH2 * hg + c
                        if c % 8 == 7:
                            g.wait_ge(l2v, 4 * gb + 3)
                            for hi in range(4):
                                h = 4 * hg + hi
                                g.dma_start(
                                    oT_seq.ap()[128 * h:128 * h + 128,
                                                128 * (c - 7):128 * (c + 1)],
                                    oT4[:, 1024 * hi:1024 * hi + 1024]).then_inc(l2f, 16)
                            nflush[0] += 1
        if getattr(P, "dbg", False):
            P.dump("ps0", P.dbg_s[:]); P.dump("pt0", P.dbg_p[:])
            P.dump("qt0", P.dbg_q[:]); P.dump("vt0", P.dbg_v[:])
            dbg2 = P.sbuf("dbg2", [128, 512], BF16)
            for nm, src_ in [("oseq0", oT_seq), ("kts0", kt_seq), ("vs0", v_seq)]:
                with nc.Block() as b:
                    @b.gpsimd
                    def _(g):
                        g.dma_start(dbg2[:], src_.ap()[0:128, 0:512]).then_inc(P.dsem, 16)
                        P.dn += 1
                        g.wait_ge(P.dsem, 16 * P.dn)
                P.dump(nm, dbg2[:])
        ph4.close()
        P._scopes.pop()
        ph_mid.close()

        ph5 = contextlib.ExitStack()
        psY5 = ph5.enter_context(nc.psum_tensor(f"psY5_{P.uid()}", [128, 512], F32))
        psT5 = ph5.enter_context(nc.psum_tensor(f"psT5_{P.uid()}", [128, 1024], BF16))
        oTmy = P.sbuf("oTmy", [128, 16 * 2048], BF16)
        yT5 = P.sbuf("yT5", [128, 512], BF16)
        ytok5 = [P.sbuf(f"ytok5{j}", [128, 2048], BF16) for j in range(4)]
        with nc.Block() as b:
            @b.gpsimd
            def _(g):
                for ec in range(16):
                    for tg in range(4):
                        g.indirect_dma_start(
                            oTmy[:, 2048 * ec + 512 * tg:2048 * ec + 512 * tg + 512], None,
                            ap3(oT_seq, 0, [[512, 32768], [1, 512]]),
                            IndirectOffsetOnAxis(oidx_sb[:, 4 * ec + tg:4 * ec + tg + 1], 0)
                        ).then_inc(P.dsem, 16)
                        P.dn += 1
                g.wait_ge(P.dsem, 16 * P.dn)
        for tg in range(4):
            for dc in range(16):
                with nc.Block() as b:
                    @b.sync
                    def _(sync):
                        P.ld(sync, ap3(wtile, 0, [[4096, 128], [128, 16], [1, 128]]),
                             ap3(wo_bf, 128 * dc, [[2048, 128], [128 * 2048, 16], [1, 128]]))
                        P.ldwait(sync)
                    @b.tensor
                    def _(t):
                        t.wait_ge(P.dsem, 16 * P.dn)
                        for ec in range(16):
                            t.matmul(psY5[:], wtile[:, 128 * ec:128 * ec + 128],
                                     oTmy[:, 2048 * ec + 512 * tg:2048 * ec + 512 * tg + 512],
                                     start=(ec == 0), stop=(ec == 15))
                with nc.Block() as b:
                    @b.vector
                    def _(v):
                        v = P.wrap(v, "v")
                        v.tensor_copy(yT5[:], psY5[:])
                with nc.Block() as b:
                    @b.tensor
                    def _(t):
                        for j in range(4):
                            t.transpose(psT5[:, 128 * j:128 * j + 128],
                                        yT5[:, 128 * j:128 * j + 128], sb_idb[:])
                with nc.Block() as b:
                    @b.vector
                    def _(v):
                        v = P.wrap(v, "v")
                        for j in range(4):
                            v.tensor_copy(ytok5[j][:, 128 * dc:128 * dc + 128],
                                          psT5[:, 128 * j:128 * j + 128])
            for j in range(4):
                ti = 4 * tg + j
                with nc.Block() as b:
                    @b.sync
                    def _(sync):
                        P.ld(sync, xt[:], x_t[ti])
                        P.ldwait(sync)
                with nc.Block() as b:
                    @b.vector
                    def _(v):
                        v = P.wrap(v, "v")
                        v.tensor_tensor(scr[:], xt[:], ytok5[j][:], ALU.add)
                with nc.Block() as b:
                    @b.sync
                    def _(sync):
                        P.st(sync, y_t[ti], scr[:])
                        P.stwait(sync)
        ph5.close()



def build_l3attn(P, x_in, y_out, wnp, tag):
    nc = P.nc
    import ml_dtypes
    id128b = P.hconst("c_id128b", host_identity(128, np.float32).astype(ml_dtypes.bfloat16))
    SCL = 1.0 / math.sqrt(128.0)
    P.need_consts([-1.0 / D, D * 1e-5, 0.5 * math.log(D), SCL, -1.0])

    wq_h = P.hconst(f"w_q3_{tag}", wnp["Wq"].astype(np.float32))
    wk_h = P.hconst(f"w_k3_{tag}", wnp["Wk"].astype(np.float32))
    wv_h = P.hconst(f"w_v3_{tag}", wnp["Wv"].astype(np.float32))
    wo_h = P.hconst(f"w_o3_{tag}", wnp["Wo"].astype(np.float32))
    rm = []
    bidx = []
    for c in range(8):
        pos0 = 2048 * (c % 4)
        a = np.zeros((128, 256), np.float32)
        for b_ in range(64):
            for qg in range(4):
                cut = pos0 + 512 * qg
                a[:, 4 * b_ + qg] = np.where(128 * b_ + np.arange(128) < cut, 0.0, -1.0e30)
        rm.append(np.ascontiguousarray(a))
        bi = np.zeros((128, 16), np.int32)
        for qg in range(4):
            for dl in range(4):
                bi[:, 4 * qg + dl] = pos0 + 512 * qg + 128 * dl + np.arange(128)
        bidx.append(np.ascontiguousarray(bi))
    rm_h = P.hconst(f"c_rm3_{tag}", rm)
    bidx_h = P.hconst(f"c_bidx3_{tag}", bidx)
    bm = np.zeros((128, 4 * 512), np.float32)
    for dl in range(4):
        bm[:, 512 * dl:512 * dl + 512] = (128 * dl + np.arange(128)[:, None]
                                          <= np.arange(512)[None, :])
    import ml_dtypes as _mld
    bm_h = P.hconst(f"c_bm3_{tag}", bm.astype(_mld.bfloat16))

    K_sh = P.dram(f"Ksh3_{tag}", [512, TOK], BF16)
    Kt_sh = P.dram(f"Ktsh3_{tag}", [TOK, 512], BF16)
    V_sh = P.dram(f"Vsh3_{tag}", [TOK, 512], BF16)
    K_seq = P.dram(f"Kseq3_{tag}", [4 * 512, TOK], BF16)
    Kt_seq = P.dram(f"Ktseq3_{tag}", [4 * TOK, 512], BF16)
    V_seq = P.dram(f"Vseq3_{tag}", [4 * TOK, 512], BF16)
    km_in = P.dram(f"kmin3_{tag}", [1, 16], F32)
    km_out = P.dram(f"kmout3_{tag}", [1, 16], F32)
    wq_bf = P.dram(f"wqbf3_{tag}", [D, D], BF16)
    wkv_bf = P.dram(f"wkvbf3_{tag}", [D, 1024], BF16)
    wo_bf = P.dram(f"wobf3_{tag}", [D, D], BF16)

    x_t = x_in.ap().rearrange("(n p) d -> n p d", p=128)
    y_t = y_out.ap().rearrange("(n p) d -> n p d", p=128)

    with P.scope():
        sb_idb = P.sbuf("idb", [128, 128], BF16)
        xt = P.sbuf("xt", [128, D], F32)
        scr = P.sbuf("scr", [128, D], F32)
        stats = dict(negmu=P.sbuf("negmu", [128, 1], F32), rstd=P.sbuf("rstd", [128, 1], F32),
                     s1=P.sbuf("s1", [128, 1], F32), ssq=P.sbuf("ssq", [128, 1], F32))
        qT = P.sbuf("qT3", [128, 16 * 2048], BF16)
        onescolf = P.sbuf("onescolf3", [128, 1], F32)
        ones1f = P.sbuf("ones1f3", [1, 128], F32)
        onescol = P.sbuf("onescol3", [128, 1], BF16)
        qn16 = P.sbuf("qn163", [1, 16], F32)
        km16 = P.sbuf("km163", [1, 16], F32)
        negM = P.sbuf("negM3", [128, 16], F32)
        negMr = P.sbuf("negMr3", [128, 16 * 256], F32)
        rm_sb = P.sbuf("rmsb3", [128, 256], F32)
        bidx_sb = P.sbuf("bidx3", [128, 16], mybir.dt.int32)
        bm_sb = P.sbuf("bmsb3", [128, 2048], BF16)
        m1 = P.sbuf("m13", [128, 2], F32)
        ph0 = contextlib.ExitStack()
        wtile = ph0.enter_context(nc.sbuf_tensor(f"wtile3a_{P.uid()}", [128, 2 * 2048], BF16))

        with nc.Block() as b:
            @b.sync
            def _(sync):
                P.ld(sync, sb_idb[:], id128b.ap())
                P.ld(sync, rm_sb[:], rm_h.ap())
                P.ld(sync, bidx_sb[:], bidx_h.ap())
                P.ld(sync, bm_sb[:], bm_h.ap())
                P.ldwait(sync)
            @b.gpsimd
            def _(g):
                P.emit_consts(g)
                g.memset(onescolf[:], 1.0)
                g.memset(ones1f[:], 1.0)
                g.memset(onescol[:], 1.0)
                for r_ in range(16):
                    g.dma_start(wtile[:, 0:2048],
                                wq_h.ap()[128 * r_:128 * r_ + 128, :]).then_inc(P.dsem, 16)
                    g.dma_start(wtile[:, 2048:4096],
                                wo_h.ap()[128 * r_:128 * r_ + 128, :]).then_inc(P.dsem, 16)
                    P.dn += 2
                    g.wait_ge(P.dsem, 16 * P.dn)
                    g.dma_start(wq_bf.ap()[128 * r_:128 * r_ + 128, :],
                                wtile[:, 0:2048]).then_inc(P.dsem, 16)
                    g.dma_start(wo_bf.ap()[128 * r_:128 * r_ + 128, :],
                                wtile[:, 2048:4096]).then_inc(P.dsem, 16)
                    P.dn += 2
                    g.wait_ge(P.dsem, 16 * P.dn)
                    g.dma_start(wtile[:, 0:512],
                                wk_h.ap()[128 * r_:128 * r_ + 128, :]).then_inc(P.dsem, 16)
                    g.dma_start(wtile[:, 512:1024],
                                wv_h.ap()[128 * r_:128 * r_ + 128, :]).then_inc(P.dsem, 16)
                    P.dn += 2
                    g.wait_ge(P.dsem, 16 * P.dn)
                    g.dma_start(wkv_bf.ap()[128 * r_:128 * r_ + 128, :],
                                wtile[:, 0:1024]).then_inc(P.dsem, 16)
                    P.dn += 1
                    g.wait_ge(P.dsem, 16 * P.dn)

        ph0.close()
        ph1 = contextlib.ExitStack()
        P._scopes.append(ph1)
        wtile = ph1.enter_context(nc.sbuf_tensor(f"wtile3b_{P.uid()}", [128, 2 * 2048], BF16))
        zT = P.sbuf("zT3", [128, 16 * 2048], BF16)
        zb = P.sbuf("zb", [128, D], BF16)
        kevac = P.sbuf("kevac3", [128, 2048], BF16)
        sq2 = P.sbuf("sq23", [128, 2048], F32)
        psT = ph1.enter_context(nc.psum_tensor(f"psTl3_{P.uid()}", [128, 1024], BF16))
        for ti in range(NT):
            with nc.Block() as b:
                @b.sync
                def _(sync):
                    P.ld(sync, xt[:], x_t[ti])
                    P.ldwait(sync)
            ln_stats(P, None, None, xt[:], scr[:], stats, 1.0 / D, 1e-5)
            with nc.Block() as b:
                @b.vector
                def _(v):
                    v = P.wrap(v, "v")
                    v.tensor_scalar(zb[:], xt[:], stats["negmu"][:], stats["rstd"][:],
                                    ALU.add, ALU.mult)
            for hf in range(2):
                with nc.Block() as b:
                    @b.tensor
                    def _(t):
                        for k in range(8):
                            kk = 8 * hf + k
                            t.transpose(psT[:, 128 * k:128 * k + 128],
                                        zb[:, 128 * kk:128 * kk + 128], sb_idb[:])
                with nc.Block() as b:
                    @b.vector
                    def _(v):
                        v = P.wrap(v, "v")
                        for k in range(8):
                            kk = 8 * hf + k
                            v.tensor_copy(zT[:, 2048 * kk + 128 * ti:2048 * kk + 128 * ti + 128],
                                          psT[:, 128 * k:128 * k + 128])

        psP = [ph1.enter_context(nc.psum_tensor(f"psP3{i}_{P.uid()}", [128, 512], F32))
               for i in range(4)]
        psN = ph1.enter_context(nc.psum_tensor(f"psN3_{P.uid()}", [1, 512], F32))
        for oc in range(24):
            with nc.Block() as b:
                @b.sync
                def _(sync):
                    if oc < 16:
                        P.ld(sync, ap3(wtile, 0, [[4096, 128], [128, 16], [1, 128]]),
                             ap3(wq_bf, 128 * oc, [[2048, 128], [128 * 2048, 16], [1, 128]]))
                    else:
                        P.ld(sync, ap3(wtile, 0, [[4096, 128], [128, 16], [1, 128]]),
                             ap3(wkv_bf, 128 * (oc - 16), [[1024, 128], [128 * 1024, 16], [1, 128]]))
                    P.ldwait(sync)
                @b.tensor
                def _(t):
                    t.wait_ge(P.dsem, 16 * P.dn)
                    for k in range(16):
                        for tg in range(4):
                            t.matmul(psP[tg][:], wtile[:, 128 * k:128 * k + 128],
                                     zT[:, 2048 * k + 512 * tg:2048 * k + 512 * tg + 512],
                                     start=(k == 0), stop=(k == 15))
            with nc.Block() as b:
                @b.vector
                def _(v):
                    v = P.wrap(v, "v")
                    for tg in range(4):
                        if oc < 16:
                            v.tensor_copy(qT[:, 2048 * oc + 512 * tg:2048 * oc + 512 * tg + 512],
                                          psP[tg][:])
                        else:
                            v.tensor_copy(kevac[:, 512 * tg:512 * tg + 512], psP[tg][:])
            src = (qT[:, 2048 * oc:2048 * oc + 2048] if oc < 16 else kevac[:])
            if oc < 16 or oc - 16 < 4:
                with nc.Block() as b:
                    @b.scalar
                    def _(sc):
                        sc = P.wrap(sc, "s")
                        sc.activation(sq2[:], src, AF.Square)
                for tg in range(4):
                    with nc.Block() as b:
                        @b.tensor
                        def _(t):
                            t.matmul(psN[0:1, :], onescolf[:],
                                     sq2[:, 512 * tg:512 * tg + 512], start=True, stop=True)
                    with nc.Block() as b:
                        @b.vector
                        def _(v):
                            v = P.wrap(v, "v")
                            v.tensor_reduce(m1[0:1, 1:2], psN[0:1, :], AX.X, ALU.max)
                            if tg == 0:
                                v.tensor_copy(m1[0:1, 0:1], m1[0:1, 1:2])
                            else:
                                v.tensor_tensor(m1[0:1, 0:1], m1[0:1, 0:1], m1[0:1, 1:2],
                                                ALU.max)
                with nc.Block() as b:
                    @b.vector
                    def _(v):
                        v = P.wrap(v, "v")
                        dst = qn16 if oc < 16 else km16
                        if oc < 16:
                            v.tensor_copy(dst[0:1, oc:oc + 1], m1[0:1, 0:1])
                        else:
                            for g_ in range(4):
                                v.tensor_copy(dst[0:1, 4 * (oc - 16) + g_:4 * (oc - 16) + g_ + 1],
                                              m1[0:1, 0:1])
            if oc >= 16:
                kh = oc - 16
                if kh < 4:
                    with nc.Block() as b:
                        @b.sync
                        def _(sync):
                            P.ld(sync, K_sh.ap()[128 * kh:128 * kh + 128, :], kevac[:])
                            P.ldwait(sync)
                for hf in range(2):
                    with nc.Block() as b:
                        @b.tensor
                        def _(t):
                            for j in range(8):
                                jj = 8 * hf + j
                                t.transpose(psT[:, 128 * j:128 * j + 128],
                                            kevac[:, 128 * jj:128 * jj + 128], sb_idb[:])
                    with nc.Block() as b:
                        @b.vector
                        def _(v):
                            v = P.wrap(v, "v")
                            v.tensor_copy(zb[:, 1024 * hf:1024 * hf + 1024], psT[:])
                with nc.Block() as b:
                    @b.sync
                    def _(sync):
                        dstt = Kt_sh if kh < 4 else V_sh
                        P.ld(sync, ap3(dstt, 128 * (kh % 4), [[512, 128], [512 * 128, 16], [1, 128]]),
                             ap3(zb, 0, [[2048, 128], [128, 16], [1, 128]]))
                        P.ldwait(sync)
        with nc.Block() as b:
            @b.sync
            def _(sync):
                P.ld(sync, km_in.ap(), km16[0:1, :])
                P.ldwait(sync)
        for nm, a, o in [("agK3", K_sh, K_seq), ("agKt3", Kt_sh, Kt_seq), ("agV3", V_sh, V_seq)]:
            cs = P.sem(nm)
            with nc.Block() as b:
                @b.gpsimd
                def _(g):
                    g.collective_compute("AllGather", ALU.bypass, replica_groups=RG2x4,
                                         ins=[a.ap().opt()], outs=[o.ap().opt()]).then_inc(cs)
                    g.wait_ge(cs, 1)
        csM = P.sem("arM3")
        with nc.Block() as b:
            @b.gpsimd
            def _(g):
                g.collective_compute("AllReduce", ALU.max, replica_groups=RG2x4,
                                     ins=[km_in.ap().opt()], outs=[km_out.ap().opt()]).then_inc(csM)
                g.wait_ge(csM, 1)
        with nc.Block() as b:
            @b.sync
            def _(sync):
                P.ld(sync, km16[0:1, :], km_out.ap())
                P.ldwait(sync)
        psB = ph1.enter_context(nc.psum_tensor(f"psB3_{P.uid()}", [128, 16], F32))
        with nc.Block() as b:
            @b.scalar
            def _(sc):
                sc = P.wrap(sc, "s")
                sc.activation(qn16[0:1, :], qn16[0:1, :], AF.Sqrt)
                sc.activation(km16[0:1, :], km16[0:1, :], AF.Sqrt)
        with nc.Block() as b:
            @b.vector
            def _(v):
                v = P.wrap(v, "v")
                v.tensor_tensor(qn16[0:1, :], qn16[0:1, :], km16[0:1, :], ALU.mult)
                v.tensor_single_scalar(qn16[0:1, :], qn16[0:1, :], -SCL, ALU.mult)
        with nc.Block() as b:
            @b.tensor
            def _(t):
                t.matmul(psB[:], ones1f[0:1, :], qn16[0:1, :], start=True, stop=True)
        with nc.Block() as b:
            @b.vector
            def _(v):
                v = P.wrap(v, "v")
                v.tensor_copy(negM[:], psB[:])
                for h in range(16):
                    v.tensor_scalar(negMr[:, 256 * h:256 * h + 256], rm_sb[:],
                                    negM[:, h:h + 1], None, ALU.add)
        ph1.close()
        P._scopes.pop()

        ph2 = contextlib.ExitStack()
        P._scopes.append(ph2)
        vband = P.sbuf("vband3", [128, 16 * 512], BF16)
        kbfm = P.sbuf("kbfm3", [128, 16 * 512], BF16)
        phkb = contextlib.ExitStack()
        kband = phkb.enter_context(nc.sbuf_tensor(f"kband3_{P.uid()}", [128, 16 * 512], BF16))
        with nc.Block() as b:
            @b.gpsimd
            def _(g):
                for qg in range(4):
                    for dl in range(4):
                        i_ = 4 * qg + dl
                        g.indirect_dma_start(
                            kband[:, 512 * i_:512 * i_ + 512], None,
                            ap3(Kt_seq, 0, [[512, 8192], [1, 512]]),
                            IndirectOffsetOnAxis(bidx_sb[:, i_:i_ + 1], 0)).then_inc(P.dsem, 16)
                        P.dn += 1
                        g.indirect_dma_start(
                            vband[:, 512 * i_:512 * i_ + 512], None,
                            ap3(V_seq, 0, [[512, 8192], [1, 512]]),
                            IndirectOffsetOnAxis(bidx_sb[:, i_:i_ + 1], 0)).then_inc(P.dsem, 16)
                        P.dn += 1
                g.wait_ge(P.dsem, 16 * P.dn)
        psT2 = ph2.enter_context(nc.psum_tensor(f"psT23_{P.uid()}", [128, 1024], BF16))
        for half in range(2):
            with nc.Block() as b:
                @b.tensor
                def _(t):
                    for j in range(8):
                        i_ = 8 * half + j
                        kh = 0
                    for j in range(8):
                        pass
                    for j in range(8):
                        i_ = 8 * half + j
                        pass
            break
        for i_ in range(16):
            with nc.Block() as b:
                @b.tensor
                def _(t):
                    for kh in range(4):
                        t.transpose(psT2[:, 128 * kh:128 * kh + 128],
                                    kband[:, 512 * i_ + 128 * kh:512 * i_ + 128 * kh + 128],
                                    sb_idb[:])
            with nc.Block() as b:
                @b.vector
                def _(v):
                    v = P.wrap(v, "v")
                    v.tensor_copy(kbfm[:, 512 * i_:512 * i_ + 512], psT2[:, 0:512])
        phkb.close()

        psS = [ph2.enter_context(nc.psum_tensor(f"psS3{i}_{P.uid()}", [128, 512], F32))
               for i in range(2)]
        psO = [ph2.enter_context(nc.psum_tensor(f"psO3{i}_{P.uid()}", [128, 512], F32))
               for i in range(4)]
        psL = ph2.enter_context(nc.psum_tensor(f"psL3_{P.uid()}", [128, 512], F32))
        kvt = P.sbuf("kvt3", [128, 4 * 128], BF16)
        pT = [P.sbuf(f"pT3{i}", [128, 512], BF16) for i in range(2)]
        pTm = P.sbuf("pTm3", [128, 512], BF16)
        oT = P.sbuf("oT3", [128, 16 * 2048], BF16)
        linv = P.sbuf("linv3", [1, 512], F32)
        lbc = P.sbuf("lbc3", [128, 512], F32)
        tsem = P.sem("fl3_t")
        s2sem = P.sem("fl3_s")
        osem2 = P.sem("fl3_o")
        dsem2 = P.sem("fl3_d")
        aS = P.sem("bd3_S")
        aE = P.sem("bd3_E")
        aV = P.sem("bd3_V")
        aO = P.sem("bd3_O")
        bg = [0]
        NB = 64
        for h in range(16):
            kh = h // 4
            with nc.Block() as b:
                @b.sync
                def _(sync):
                    for bb in range(NB):
                        gb = NB * h + bb
                        if gb >= 2:
                            sync.wait_ge(osem2, 4 * (gb - 1))
                        rr, tl0 = bb // 16, 128 * (bb % 16)
                        sync.dma_start(kvt[:, 256 * (gb % 2):256 * (gb % 2) + 128],
                                       K_seq.ap()[512 * rr + 128 * kh:512 * rr + 128 * kh + 128,
                                                  tl0:tl0 + 128]).then_inc(dsem2, 16)
                        sync.dma_start(kvt[:, 256 * (gb % 2) + 128:256 * (gb % 2) + 256],
                                       V_seq.ap()[2048 * rr + tl0:2048 * rr + tl0 + 128,
                                                  128 * kh:128 * kh + 128]).then_inc(dsem2, 16)
                @b.tensor
                def _(t):
                    for bb in range(NB):
                        gb = NB * h + bb
                        t.wait_ge(dsem2, 32 * (gb + 1))
                        for qg in range(4):
                            i = 4 * gb + qg
                            if i >= 2:
                                t.wait_ge(s2sem, i - 1)
                            t.matmul(psS[i % 2][:], kvt[:, 256 * (gb % 2):256 * (gb % 2) + 128],
                                     qT[:, 2048 * h + 512 * qg:2048 * h + 512 * qg + 512],
                                     start=True, stop=True).then_inc(tsem, 1)
                        for qg in range(4):
                            i = 4 * gb + qg
                            t.wait_ge(s2sem, i + 1)
                            t.matmul(psO[qg][:], kvt[:, 256 * (gb % 2) + 128:256 * (gb % 2) + 256],
                                     pT[i % 2][:], start=(bb == 0), stop=False)
                            t.matmul(psL[32 * qg:32 * qg + 1, :], onescol[:], pT[i % 2][:],
                                     start=(bb == 0), stop=False,
                                     tile_position=(0, 32 * qg)).then_inc(osem2, 1)
                @b.scalar
                def _(sc):
                    for bb in range(NB):
                        gb = NB * h + bb
                        for qg in range(4):
                            i = 4 * gb + qg
                            sc.wait_ge(tsem, i + 1)
                            if i >= 2:
                                sc.wait_ge(osem2, i - 1)
                            sc.activation(pT[i % 2][:], psS[i % 2][:], AF.Exp,
                                          bias=negMr[:, 256 * h + 4 * bb + qg:256 * h + 4 * bb + qg + 1],
                                          scale=SCL).then_inc(s2sem, 1)
            g0 = bg[0]
            with nc.Block() as b:
                @b.tensor
                def _(t):
                    for qg in range(4):
                        for dl in range(4):
                            i_ = 4 * qg + dl
                            g = g0 + i_
                            if g >= 2:
                                t.wait_ge(aE, g - 1)
                            t.matmul(psS[g % 2][:],
                                     kbfm[:, 512 * i_ + 128 * kh:512 * i_ + 128 * kh + 128],
                                     qT[:, 2048 * h + 512 * qg:2048 * h + 512 * qg + 512],
                                     start=True, stop=True).then_inc(aS, 1)
                            t.wait_ge(aV, g + 1)
                            t.matmul(psO[qg][:],
                                     vband[:, 512 * i_ + 128 * kh:512 * i_ + 128 * kh + 128],
                                     pTm[:], start=False, stop=(dl == 3))
                            t.matmul(psL[32 * qg:32 * qg + 1, :], onescol[:], pTm[:],
                                     start=False, stop=(dl == 3),
                                     tile_position=(0, 32 * qg)).then_inc(aO, 1)
                @b.scalar
                def _(sc):
                    for i_ in range(16):
                        g = g0 + i_
                        sc.wait_ge(aS, g + 1)
                        if g >= 1:
                            sc.wait_ge(aO, g)
                        sc.activation(pTm[:], psS[g % 2][:], AF.Exp,
                                      bias=negM[:, h:h + 1], scale=SCL).then_inc(aE, 1)
                @b.vector
                def _(v):
                    for i_ in range(16):
                        g = g0 + i_
                        dl = i_ % 4
                        v.wait_ge(aE, g + 1)
                        v.tensor_tensor(pTm[:], pTm[:], bm_sb[:, 512 * dl:512 * dl + 512],
                                        ALU.mult).then_inc(aV, 1)
            bg[0] += 16
            for qg in range(4):
                with nc.Block() as b:
                    @b.vector
                    def _(v):
                        v = P.wrap(v, "v")
                        v.reciprocal(linv[0:1, :], psL[32 * qg:32 * qg + 1, :])
                with nc.Block() as b:
                    @b.tensor
                    def _(t):
                        t.matmul(psS[0][:], ones1f[0:1, :], linv[0:1, :],
                                 start=True, stop=True)
                with nc.Block() as b:
                    @b.vector
                    def _(v):
                        v = P.wrap(v, "v")
                        v.tensor_copy(lbc[:], psS[0][:])
                        v.tensor_tensor(oT[:, 2048 * h + 512 * qg:2048 * h + 512 * qg + 512],
                                        psO[qg][:], lbc[:], ALU.mult)
        ph2.close()
        P._scopes.pop()

        yTl = P.sbuf("yTl3", [128, 512], BF16)
        ytokl = [P.sbuf(f"ytokl3{j}", [128, 2048], BF16) for j in range(4)]
        ph4 = contextlib.ExitStack()
        wtile = ph4.enter_context(nc.sbuf_tensor(f"wtile3c_{P.uid()}", [128, 2 * 2048], BF16))
        psY4 = ph4.enter_context(nc.psum_tensor(f"psY43_{P.uid()}", [128, 512], F32))
        psT4 = ph4.enter_context(nc.psum_tensor(f"psT43_{P.uid()}", [128, 1024], BF16))
        for tg in range(4):
            for dc in range(16):
                with nc.Block() as b:
                    @b.sync
                    def _(sync):
                        P.ld(sync, ap3(wtile, 0, [[4096, 128], [128, 16], [1, 128]]),
                             ap3(wo_bf, 128 * dc, [[2048, 128], [128 * 2048, 16], [1, 128]]))
                        P.ldwait(sync)
                    @b.tensor
                    def _(t):
                        t.wait_ge(P.dsem, 16 * P.dn)
                        for e in range(16):
                            t.matmul(psY4[:], wtile[:, 128 * e:128 * e + 128],
                                     oT[:, 2048 * e + 512 * tg:2048 * e + 512 * tg + 512],
                                     start=(e == 0), stop=(e == 15))
                with nc.Block() as b:
                    @b.vector
                    def _(v):
                        v = P.wrap(v, "v")
                        v.tensor_copy(yTl[:], psY4[:])
                with nc.Block() as b:
                    @b.tensor
                    def _(t):
                        for j in range(4):
                            t.transpose(psT4[:, 128 * j:128 * j + 128],
                                        yTl[:, 128 * j:128 * j + 128], sb_idb[:])
                with nc.Block() as b:
                    @b.vector
                    def _(v):
                        v = P.wrap(v, "v")
                        for j in range(4):
                            v.tensor_copy(ytokl[j][:, 128 * dc:128 * dc + 128],
                                          psT4[:, 128 * j:128 * j + 128])
            for j in range(4):
                ti = 4 * tg + j
                with nc.Block() as b:
                    @b.sync
                    def _(sync):
                        P.ld(sync, xt[:], x_t[ti])
                        P.ldwait(sync)
                with nc.Block() as b:
                    @b.vector
                    def _(v):
                        v = P.wrap(v, "v")
                        v.tensor_tensor(scr[:], xt[:], ytokl[j][:], ALU.add)
                with nc.Block() as b:
                    @b.sync
                    def _(sync):
                        P.st(sync, y_t[ti], scr[:])
                        P.stwait(sync)
        ph4.close()



def _emit_lut_tile(P, v, idx_i32, idxf, lutbc, wout, wout_fs, W, w0=0):
    v.tensor_copy(idxf[:, 0:W], idx_i32)
    msk = [P._lut_msk[j] for j in range(3)]
    acc = [P._lut_acc[j] for j in range(3)]
    for w in range(6):
        gs = [g for g in (3 * w, 3 * w + 1, 3 * w + 2) if g < 16]
        for g in gs:
            v.tensor_single_scalar(msk[g % 3][:, 0:W], idxf[:, 0:W], float(g), ALU.is_equal)
        for g in gs:
            ln = g % 3
            if g < 3:
                v.tensor_scalar(acc[ln][:, 0:W], msk[ln][:, 0:W], lutbc[:, g:g + 1],
                                None, ALU.mult)
            else:
                v.scalar_tensor_tensor(acc[ln][:, 0:W], msk[ln][:, 0:W],
                                       lutbc[:, g:g + 1], acc[ln][:, 0:W],
                                       ALU.mult, ALU.add)
    v.tensor_tensor(msk[0][:, 0:W], acc[0][:, 0:W], acc[1][:, 0:W], ALU.add)
    v.tensor_tensor(ap3(wout, w0, [[wout_fs, 128], [1, W]]), msk[0][:, 0:W],
                    acc[2][:, 0:W], ALU.add)


def build_l3ffn(P, x_in, y_out, wnp, tag):
    nc = P.nc
    import ml_dtypes
    I3 = 5504
    NHC = 43
    id128b = P.hconst("c_id128b", host_identity(128, np.float32).astype(ml_dtypes.bfloat16))
    P.need_consts([-1.0 / D, D * 1e-5, 0.5 * math.log(D), 0.5] +
                  [float(g) for g in range(2, 16)])

    dq = "l3w" not in P.cache
    if dq:
        gi_h = P.hconst("w3_gi", [np.ascontiguousarray(wnp["gate_idx"][:, 256 * c:256 * c + 256])
                                  for c in range(8)])
        ui_h = P.hconst("w3_ui", [np.ascontiguousarray(wnp["up_idx"][:, 256 * c:256 * c + 256])
                                  for c in range(8)])
        di_h = P.hconst("w3_di", [np.ascontiguousarray(wnp["down_idx"][:, 688 * c:688 * c + 688])
                                  for c in range(8)])
        lut_h = P.hconst("w3_lut", np.ascontiguousarray(
            np.broadcast_to(wnp["lut"].astype(np.float32)[None, :], (128, 16))))
        w_gT = P.dram("l3_wgT", [D, I3], BF16)
        w_uT = P.dram("l3_wuT", [D, I3], BF16)
        w_dT = P.dram("l3_wdT", [I3, D], BF16)
        gT_sh = P.dram("l3_gTsh", [256, I3], BF16)
        uT_sh = P.dram("l3_uTsh", [256, I3], BF16)
        dT_sh = P.dram("l3_dTsh", [688, D], BF16)
        P.cache["l3w"] = (w_gT, w_uT, w_dT)
    else:
        w_gT, w_uT, w_dT = P.cache["l3w"]

    x_t = x_in.ap().rearrange("(n p) d -> n p d", p=128)
    y_t = y_out.ap().rearrange("(n p) d -> n p d", p=128)

    with P.scope():
        sb_idb = P.sbuf("idb", [128, 128], BF16)
        with nc.Block() as b:
            @b.sync
            def _(sync):
                P.ld(sync, sb_idb[:], id128b.ap())
                P.ldwait(sync)
            @b.gpsimd
            def _(g):
                P.emit_consts(g)

        if dq:
            phDQ3 = contextlib.ExitStack()
            P._scopes.append(phDQ3)
            lutbc = P.sbuf("lutbc", [128, 16], F32)
            pk3 = P.sbuf("pk3", [128, 2 * 688], mybir.dt.int32)
            idxf = P.sbuf("idxf3", [128, 688], F32)
            P._lut_msk = [P.sbuf(f"lmsk{j}", [128, 688], F32) for j in range(3)]
            P._lut_acc = [P.sbuf(f"lacc{j}", [128, 688], F32) for j in range(3)]
            wde3 = P.sbuf("wde3", [128, 2 * 688], BF16)
            wtr3 = P.sbuf("wtr3", [128, 2 * 768], BF16)
            phW3 = contextlib.ExitStack()
            psW3 = [phW3.enter_context(nc.psum_tensor(f"psW3{i}_{P.uid()}", [128, 1024], BF16))
                    for i in range(2)]
            with nc.Block() as b:
                @b.sync
                def _(sync):
                    P.ld(sync, lutbc[:], lut_h.ap())
                    P.ldwait(sync)
            l3cnt = {}

            def tile_loop3(ntiles, W, src_h, tr_tile, cp_tile, st_tile):
                def ld_t(sync, i):
                    pp = i % 2
                    P.ld(sync, pk3[:, 688 * pp:688 * pp + W],
                         src_h.ap()[128 * i:128 * i + 128, :])
                    l3cnt[i] = P.dn
                with nc.Block() as b:
                    @b.sync
                    def _(sync):
                        ld_t(sync, 0)
                for i in range(ntiles + 3):
                    with nc.Block() as b:
                        @b.sync
                        def _(sync):
                            if i + 1 <= ntiles - 1:
                                ld_t(sync, i + 1)
                            if 3 <= i:
                                st_tile(sync, i - 3)
                        @b.vector
                        def _(v):
                            v = P.wrap(v, "v")
                            if i <= ntiles - 1:
                                v.wait_ge(P.dsem, 16 * l3cnt[i])
                                pp = i % 2
                                _emit_lut_tile(P, v, pk3[:, 688 * pp:688 * pp + W],
                                               idxf, lutbc, wde3, 1376, W, w0=688 * pp)
                            if 2 <= i:
                                cp_tile(v, (i - 2) % 2)
                        @b.tensor
                        def _(t):
                            if 1 <= i <= ntiles:
                                tr_tile(t, (i - 1) % 2)

            def tr_gu3(t, pp):
                t.transpose(psW3[pp][:, 0:128], wde3[:, 688 * pp:688 * pp + 128], sb_idb[:])
                t.transpose(psW3[pp][:, 128:256], wde3[:, 688 * pp + 128:688 * pp + 256],
                            sb_idb[:])

            def cp_gu3(v, pp):
                v.tensor_copy(wtr3[:, 768 * pp:768 * pp + 256], psW3[pp][:, 0:256])

            for mat, src_h, sh in [("g", gi_h, gT_sh), ("u", ui_h, uT_sh)]:
                def st_gu3(sync, i, sh=sh):
                    pp = i % 2
                    P.ld(sync, ap3(sh, 128 * i, [[I3, 128], [128 * I3, 2], [1, 128]]),
                         ap3(wtr3, 768 * pp, [[1536, 128], [128, 2], [1, 128]]))
                tile_loop3(NHC, 256, src_h, tr_gu3, cp_gu3, st_gu3)

            def tr_dn3(t, pp):
                for blk in range(5):
                    t.transpose(psW3[pp][:, 128 * blk:128 * blk + 128],
                                wde3[:, 688 * pp + 128 * blk:688 * pp + 128 * blk + 128],
                                sb_idb[:])
                t.transpose(psW3[pp][0:48, 640:768], wde3[:, 688 * pp + 640:688 * pp + 688],
                            sb_idb[:])

            def cp_dn3(v, pp):
                v.tensor_copy(wtr3[:, 768 * pp:768 * pp + 640], psW3[pp][:, 0:640])
                v.tensor_copy(wtr3[0:48, 768 * pp + 640:768 * pp + 768],
                              psW3[pp][0:48, 640:768])

            def st_dn3(sync, i):
                pp = i % 2
                P.ld(sync, ap3(dT_sh, 128 * i, [[2048, 128], [128 * 2048, 5], [1, 128]]),
                     ap3(wtr3, 768 * pp, [[1536, 128], [128, 5], [1, 128]]))
                P.ld(sync, ap3(dT_sh, 5 * 128 * 2048 + 128 * i, [[2048, 48], [1, 128]]),
                     ap3(wtr3, 768 * pp + 640, [[1536, 48], [1, 128]]))

            tile_loop3(16, 688, di_h, tr_dn3, cp_dn3, st_dn3)
            for nm, a, o in [("ag3g", gT_sh, w_gT), ("ag3u", uT_sh, w_uT), ("ag3d", dT_sh, w_dT)]:
                cs = P.sem(nm)
                with nc.Block() as b:
                    @b.gpsimd
                    def _(g):
                        g.collective_compute("AllGather", ALU.bypass, replica_groups=RG8,
                                             ins=[a.ap().opt()], outs=[o.ap().opt()]).then_inc(cs)
                        g.wait_ge(cs, 1)
            phW3.close()
            P._scopes.pop()
            phDQ3.close()

        xt = P.sbuf("xt", [128, D], F32)
        scr = P.sbuf("scr", [128, D], F32)
        zb = P.sbuf("zb", [128, D], BF16)
        stats = dict(negmu=P.sbuf("negmu", [128, 1], F32), rstd=P.sbuf("rstd", [128, 1], F32),
                     s1=P.sbuf("s1", [128, 1], F32), ssq=P.sbuf("ssq", [128, 1], F32))
        zT_half = P.sbuf("zTh3", [128, 16 * 1024], BF16)
        hT_half = P.sbuf("hTh3", [128, NHC * 1024], BF16)
        y0T = zT_half
        wg_sb = P.sbuf("wg3", [128, 2 * 16 * 128], BF16)
        wu_sb = P.sbuf("wu3", [128, 2 * 16 * 128], BF16)
        wd_sb = P.sbuf("wd3", [128, 2 * NHC * 128], BF16)
        sg0 = P.sbuf("sg03", [128, 512], F32)
        sg1 = P.sbuf("sg13", [128, 512], F32)
        y0tok = P.sbuf("y0tok3", [128, D], BF16)
        sil = P.sem(f"sil_{tag}")
        nsil = [0]
        for half in range(2):
            phA = contextlib.ExitStack()
            psT2 = phA.enter_context(nc.psum_tensor(f"psT2a3_{P.uid()}", [128, 1024], BF16))
            for j in range(8):
                ti = 8 * half + j
                with nc.Block() as b:
                    @b.sync
                    def _(sync):
                        P.ld(sync, xt[:], x_t[ti])
                        P.ldwait(sync)
                ln_stats(P, None, None, xt[:], scr[:], stats, 1.0 / D, 1e-5)
                with nc.Block() as b:
                    @b.vector
                    def _(v):
                        v = P.wrap(v, "v")
                        v.tensor_scalar(zb[:], xt[:], stats["negmu"][:], stats["rstd"][:],
                                        ALU.add, ALU.mult)
                for hf in range(2):
                    with nc.Block() as b:
                        @b.tensor
                        def _(t):
                            for k in range(8):
                                kk = 8 * hf + k
                                t.transpose(psT2[:, 128 * k:128 * k + 128],
                                            zb[:, 128 * kk:128 * kk + 128], sb_idb[:])
                    with nc.Block() as b:
                        @b.vector
                        def _(v):
                            v = P.wrap(v, "v")
                            for k in range(8):
                                kk = 8 * hf + k
                                v.tensor_copy(zT_half[:, 1024 * kk + 128 * j:1024 * kk + 128 * j + 128],
                                              psT2[:, 128 * k:128 * k + 128])
            phA.close()
            phB = contextlib.ExitStack()
            psG = [phB.enter_context(nc.psum_tensor(f"psG3{i}_{P.uid()}", [128, 512], F32))
                   for i in range(4)]
            psU2 = [phB.enter_context(nc.psum_tensor(f"psU23{i}_{P.uid()}", [128, 512], F32))
                    for i in range(4)]
            ldcnt = {}

            def ld_gu(sync, hc):
                pp = hc % 2
                P.ld(sync, ap3(wg_sb, 2048 * pp, [[4096, 128], [128, 16], [1, 128]]),
                     ap3(w_gT, 128 * hc, [[I3, 128], [128 * I3, 16], [1, 128]]))
                P.ld(sync, ap3(wu_sb, 2048 * pp, [[4096, 128], [128, 16], [1, 128]]),
                     ap3(w_uT, 128 * hc, [[I3, 128], [128 * I3, 16], [1, 128]]))
                ldcnt[hc] = P.dn
            with nc.Block() as b:
                @b.sync
                def _(sync):
                    ld_gu(sync, 0)
            for hc in range(NHC + 1):
                with nc.Block() as b:
                    @b.sync
                    def _(sync):
                        if hc + 1 <= NHC - 1:
                            ld_gu(sync, hc + 1)
                    @b.tensor
                    def _(t):
                        if hc <= NHC - 1:
                            pp = hc % 2
                            t.wait_ge(P.dsem, 16 * ldcnt[hc])
                            for k in range(16):
                                for tg in range(2):
                                    t.matmul(psG[2 * pp + tg][:],
                                             wg_sb[:, 2048 * pp + 128 * k:2048 * pp + 128 * k + 128],
                                             zT_half[:, 1024 * k + 512 * tg:1024 * k + 512 * tg + 512],
                                             start=(k == 0), stop=(k == 15))
                                for tg in range(2):
                                    t.matmul(psU2[2 * pp + tg][:],
                                             wu_sb[:, 2048 * pp + 128 * k:2048 * pp + 128 * k + 128],
                                             zT_half[:, 1024 * k + 512 * tg:1024 * k + 512 * tg + 512],
                                             start=(k == 0), stop=(k == 15))
                    @b.scalar
                    def _(sc):
                        if hc >= 1:
                            pq = (hc - 1) % 2
                            sc.activation(sg0[:], psG[2 * pq][:], AF.Silu)
                            sc.activation(sg1[:], psG[2 * pq + 1][:], AF.Silu).then_inc(sil, 1)
                            nsil[0] += 1
                    @b.vector
                    def _(v):
                        if hc >= 1:
                            pq = (hc - 1) % 2
                            v.wait_ge(sil, nsil[0])
                            v.tensor_tensor(hT_half[:, 1024 * (hc - 1):1024 * (hc - 1) + 512],
                                            sg0[:], psU2[2 * pq][:], ALU.mult)
                            v.tensor_tensor(hT_half[:, 1024 * (hc - 1) + 512:1024 * (hc - 1) + 1024],
                                            sg1[:], psU2[2 * pq + 1][:], ALU.mult)
            phB.close()
            phC = contextlib.ExitStack()
            psY = [phC.enter_context(nc.psum_tensor(f"psY3f{i}_{P.uid()}", [128, 512], F32))
                   for i in range(4)]
            ldcnt2 = {}

            def ld_dn(sync, dcs):
                pp = dcs % 2
                P.ld(sync, ap3(wd_sb, NHC * 128 * pp, [[2 * NHC * 128, 128], [128, NHC], [1, 128]]),
                     ap3(w_dT, 128 * dcs, [[2048, 128], [128 * 2048, NHC], [1, 128]]))
                ldcnt2[dcs] = P.dn
            with nc.Block() as b:
                @b.sync
                def _(sync):
                    ld_dn(sync, 0)
            for dcs in range(17):
                with nc.Block() as b:
                    @b.sync
                    def _(sync):
                        if dcs + 1 <= 15:
                            ld_dn(sync, dcs + 1)
                    @b.tensor
                    def _(t):
                        if dcs <= 15:
                            pp = dcs % 2
                            t.wait_ge(P.dsem, 16 * ldcnt2[dcs])
                            for ic in range(NHC):
                                for tg in range(2):
                                    t.matmul(psY[2 * pp + tg][:],
                                             wd_sb[:, NHC * 128 * pp + 128 * ic:NHC * 128 * pp + 128 * ic + 128],
                                             hT_half[:, 1024 * ic + 512 * tg:1024 * ic + 512 * tg + 512],
                                             start=(ic == 0), stop=(ic == NHC - 1))
                    @b.vector
                    def _(v):
                        if dcs >= 1:
                            pq = (dcs - 1) % 2
                            dc0 = dcs - 1
                            v.tensor_copy(y0T[:, 1024 * dc0:1024 * dc0 + 512], psY[2 * pq][:])
                            v.tensor_copy(y0T[:, 1024 * dc0 + 512:1024 * dc0 + 1024],
                                          psY[2 * pq + 1][:])
            phC.close()
            phD = contextlib.ExitStack()
            psT2 = phD.enter_context(nc.psum_tensor(f"psT2d3_{P.uid()}", [128, 1024], BF16))
            for j in range(8):
                ti = 8 * half + j
                for hf in range(2):
                    with nc.Block() as b:
                        @b.tensor
                        def _(t):
                            for k in range(8):
                                dc = 8 * hf + k
                                t.transpose(psT2[:, 128 * k:128 * k + 128],
                                            y0T[:, 1024 * dc + 128 * j:1024 * dc + 128 * j + 128],
                                            sb_idb[:])
                    with nc.Block() as b:
                        @b.vector
                        def _(v):
                            v = P.wrap(v, "v")
                            for k in range(8):
                                dc = 8 * hf + k
                                v.tensor_copy(y0tok[:, 128 * dc:128 * dc + 128],
                                              psT2[:, 128 * k:128 * k + 128])
                with nc.Block() as b:
                    @b.sync
                    def _(sync):
                        P.ld(sync, xt[:], x_t[ti])
                        P.ldwait(sync)
                with nc.Block() as b:
                    @b.vector
                    def _(v):
                        v = P.wrap(v, "v")
                        v.scalar_tensor_tensor(scr[:], y0tok[:], 0.5, xt[:], ALU.mult, ALU.add)
                with nc.Block() as b:
                    @b.sync
                    def _(sync):
                        P.st(sync, y_t[ti], scr[:])
                        P.stwait(sync)
            phD.close()



def kernel(**inputs):
    from concourse.bass_utils import run_bass_kernel_spmd
    x = np.ascontiguousarray(np.asarray(inputs["L0_attn_x"], np.float32).reshape(16384, 2048))
    P = Prog()
    x_in = P.dram("x", [TOK, D], F32, kind="ExternalInput")
    y_out = P.dram("y", [TOK, D], F32, kind="ExternalOutput")
    xa = P.dram("xa", [TOK, D], F32)
    xb = P.dram("xb", [TOK, D], F32)
    UNION = ([-1.0 / D, D * 1e-5, 0.5 * math.log(D), 0.5, -1.0, 100.0, 1e-8,
              1.0 - 1e-6, 16.0, -BIG, BIG, 2.0, 8.0, -8.0, -2.0, D * 1e-6,
              1.0 / math.sqrt(128.0), -1.0 / math.sqrt(128.0)] +
             [float(g) for g in range(2, 16)])
    P.need_consts(UNION)
    with P.nc.Block() as b:
        @b.gpsimd
        def _(g):
            P.emit_consts(g)
    w_l0f = dict(ln_gamma=inputs["L0_ffn_ln_gamma"], ln_beta=inputs["L0_ffn_ln_beta"],
                 bases=inputs["L0_ffn_spline_bases"], slopes=inputs["L0_ffn_spline_slopes"],
                 oscale=inputs["L0_ffn_output_scale"])
    w_l0a = dict(W_in=inputs["L0_attn_W_in"], b_in=inputs["L0_attn_b_in"],
                 W_out=inputs["L0_attn_W_out"], b_out=inputs["L0_attn_b_out"],
                 gamma=inputs["L0_attn_gamma"], beta=inputs["L0_attn_beta"])
    w_l1f = dict(gate_up_packed=inputs["L1_ffn_gate_up_packed"],
                 gate_up_scales=inputs["L1_ffn_gate_up_scales"],
                 down_packed=inputs["L1_ffn_down_packed"],
                 down_scales=inputs["L1_ffn_down_scales"])
    w_l1a = dict(w_qkv=inputs["L1_attn_w_qkv"], b_qkv=inputs["L1_attn_b_qkv"],
                 w_out=inputs["L1_attn_w_out"], b_out=inputs["L1_attn_b_out"])
    w_l2f = dict(select_weight=inputs["L2_ffn_select_weight"],
                 select_mean_diff=inputs["L2_ffn_select_mean_diff"],
                 imp_center=inputs["L2_ffn_imp_center"], imp_slope=inputs["L2_ffn_imp_slope"],
                 lora_update=inputs["L2_ffn_lora_update"], lora_degen=inputs["L2_ffn_lora_degen"],
                 bias_w=inputs["L2_ffn_bias_w"], debias_w=inputs["L2_ffn_debias_w"])
    w_l2a = dict(wq=inputs["L2_attn_wq"], wk=inputs["L2_attn_wk"],
                 wv=inputs["L2_attn_wv"], wo=inputs["L2_attn_wo"])
    w_l3f = dict(lut=inputs["L3_ffn_lut"], gate_idx=inputs["L3_ffn_gate_idx"],
                 up_idx=inputs["L3_ffn_up_idx"], down_idx=inputs["L3_ffn_down_idx"])
    w_l3a = dict(Wq=inputs["L3_attn_Wq"], Wk=inputs["L3_attn_Wk"],
                 Wv=inputs["L3_attn_Wv"], Wo=inputs["L3_attn_Wo"])
    seq = [
        (build_l0ffn, w_l0f, "l0f", x_in, xa), (build_l0attn, w_l0a, "l0a", xa, xb),
        (build_l0ffn, w_l0f, "l0f2", xb, xa), (build_l1ffn, w_l1f, "l1f", xa, xb),
        (build_l1attn, w_l1a, "l1a", xb, xa), (build_l1ffn, w_l1f, "l1f2", xa, xb),
        (build_l2ffn, w_l2f, "l2f", xb, xa), (build_l2attn, w_l2a, "l2a", xa, xb),
        (build_l2ffn, w_l2f, "l2f2", xb, xa), (build_l3ffn, w_l3f, "l3f", xa, xb),
        (build_l3attn, w_l3a, "l3a", xb, xa), (build_l3ffn, w_l3f, "l3f2", xa, y_out),
    ]
    for fn, w, tg, xi, xo in seq:
        with P.scope():
            fn(P, xi, xo, w, tg)
    finish(P)
    maps = in_maps_for(P, [{"x": np.ascontiguousarray(x[2048 * c:2048 * c + 2048])}
                           for c in range(8)])
    res = run_bass_kernel_spmd(P.nc, maps, core_ids=list(range(8)))
    out = np.concatenate([np.asarray(res.results[c]["y"]) for c in range(8)], axis=0)
    return out.reshape(2, 8192, 2048).astype(np.float32)
